# version 06
# baseline (speedup 1.0000x reference)
import math
import numpy as np
from contextlib import ExitStack

import concourse.bass as bass
import concourse.mybir as mybir
from concourse.bass_utils import run_bass_kernel_spmd
from concourse.alu_op_type import AluOpType

F32 = mybir.dt.float32
BF16 = mybir.dt.bfloat16
I32 = mybir.dt.int32
AF = mybir.ActivationFunctionType
AX = mybir.AxisListType.X
OP = AluOpType

S_ = 4096
D_ = 2048
F_ = 8192
R_ = 1024
NU = 4
LAMBDA_INIT = 0.8 - 0.6 * math.exp(-0.3 * 12)
CSH = 12.0
EPS = 1e-5
NEG = -1.0e9

PHASES = (1, 2, 3, 4)
DBG = ()
NKCAP = None
SERIAL1 = False
SERIAL_ACT = True


class Prog:
    ENGINES = ("sync", "tensor", "vector", "scalar", "gpsimd")

    def __init__(self):
        self.ops = {e: [] for e in self.ENGINES}
        self.cnt = {}
        self.sems = {}
        self.last_wait = {e: {} for e in self.ENGINES}

    def tick(self, sem, n=1):
        self.cnt[sem] = self.cnt.get(sem, 0) + n
        return self.cnt[sem]

    def val(self, sem):
        return self.cnt.get(sem, 0)

    def emi(self, eng, fn, sem=None, n=1):
        if sem is None:
            self.ops[eng].append(lambda e, fn=fn: fn(e))
            return None
        v = self.tick(sem, n)
        def run(e, fn=fn, sem=sem, n=n):
            fn(e).then_inc(self.sems[sem], n)
        self.ops[eng].append(run)
        return v

    def wait(self, eng, sem, v):
        if not v or v <= 0:
            return
        if self.last_wait[eng].get(sem, 0) >= v:
            return
        self.last_wait[eng][sem] = v
        self.ops[eng].append(lambda e, sem=sem, v=v: e.wait_ge(self.sems[sem], v))

    def dma(self, qeng, dst, src, sem, **kw):
        return self.emi(qeng, lambda e, d=dst, s=src, kw=kw:
                        e.dma_start(out=d, in_=s, **kw), sem, 16)

    def run(self, eng_obj, name):
        for fn in self.ops[name]:
            fn(eng_obj)


def build_nc(dbg=(), phases=PHASES, collective=True):
    nc = bass.Bass("TRN2", target_bir_lowering=False, detect_race_conditions=False)
    ap = lambda t: t.ap()

    x_in = ap(nc.dram_tensor("x", [S_, D_], F32, kind="ExternalInput"))
    xr_in = ap(nc.dram_tensor("xr", [R_, D_], F32, kind="ExternalInput"))
    wstat_d = None
    wq_in = ap(nc.dram_tensor("wq", [D_, 512], F32, kind="ExternalInput"))
    wk_in = ap(nc.dram_tensor("wk", [D_, 512], F32, kind="ExternalInput"))
    wv_in = ap(nc.dram_tensor("wv", [D_, 512], F32, kind="ExternalInput"))
    wo_in = ap(nc.dram_tensor("wo", [D_, D_], F32, kind="ExternalInput"))
    w1t_in = ap(nc.dram_tensor("w1t", [D_, F_], F32, kind="ExternalInput"))
    w2t_in = ap(nc.dram_tensor("w2t", [F_, D_], F32, kind="ExternalInput"))
    b1_in = ap(nc.dram_tensor("b1", [1, F_], F32, kind="ExternalInput"))
    gam_in = ap(nc.dram_tensor("gam", [1, F_], F32, kind="ExternalInput"))
    bet_in = ap(nc.dram_tensor("bet", [1, F_], F32, kind="ExternalInput"))
    b2_in = ap(nc.dram_tensor("b2", [1, D_], F32, kind="ExternalInput"))
    lq1_in = ap(nc.dram_tensor("lq1", [1, 64], F32, kind="ExternalInput"))
    lk1_in = ap(nc.dram_tensor("lk1", [1, 64], F32, kind="ExternalInput"))
    lq2_in = ap(nc.dram_tensor("lq2", [1, 64], F32, kind="ExternalInput"))
    lk2_in = ap(nc.dram_tensor("lk2", [1, 64], F32, kind="ExternalInput"))
    ident_in = ap(nc.dram_tensor("ident", [128, 128], BF16, kind="ExternalInput"))
    identf_in = ap(nc.dram_tensor("identf", [128, 128], F32, kind="ExternalInput"))
    masks_in = ap(nc.dram_tensor("masks", [4 * 128, 512], F32, kind="ExternalInput"))

    out_t = ap(nc.dram_tensor("out", [R_, D_], F32, kind="ExternalOutput"))

    lam_d = ap(nc.dram_tensor("lam_d", [1, 1], F32))
    wstat_d = ap(nc.dram_tensor("wstat_d", [1, 8], F32))
    ffn_d = ap(nc.dram_tensor("ffn_d", [R_, D_], BF16))
    ag_in_t = ap(nc.dram_tensor("ag_in", [512, S_], BF16))
    ag_out_t = ap(nc.dram_tensor("ag_out", [4 * 512, S_], BF16))

    dbg_t = {}
    for name, shape in [
        ("d_qfm", [512, S_]), ("d_kfm", [512, S_]), ("d_v", [S_, 512]),
        ("d_lam", [128, 1]), ("d_attn", [512, S_]), ("d_ag", [2048, S_]),
        ("d_xq", [R_, D_]), ("d_alpha", [128, 8]), ("d_g", [R_, F_]),
        ("d_hq", [R_, F_]), ("d_ffn", [R_, D_]), ("d_thr", [128, 8]),
        ("d_p1", [128, 512]), ("d_p2", [128, 512]), ("d_l", [1, 1024]),
        ("d_xn", [S_, D_]), ("d_st", [128, 16]),
        ("d_s1", [128, 512]), ("d_p1k0", [128, 512]),
        ("d_a1", [128, 512]), ("d_a2", [128, 512]), ("d_a1t", [128, 512]),
        ("d_comb", [128, 512]), ("d_rc", [128, 12]), ("d_linv", [128, 8]),
        ("d_3c", [384, 8192]),
    ]:
        if name in dbg:
            dbg_t[name] = ap(nc.dram_tensor(name, shape, F32, kind="ExternalOutput"))

    p = Prog()

    ctx = ExitStack()
    with ctx:
        sbg = lambda n, s, d, **kw: ctx.enter_context(nc.sbuf_tensor(n, s, d, **kw))
        SEMS = ("sIdent", "sLam", "sX0", "sX1", "sWp", "sMsk", "sW10", "sW11",
                "sW12", "sW13", "sW20", "sW21", "sW22", "sW23",
                "sB0", "sB1", "sWo", "sAG", "sAT", "sSt",
                "sM0", "sM1", "sT", "sV", "sA", "sG", "sCC")
        for s in SEMS:
            p.sems[s] = ctx.enter_context(nc.semaphore(s))

        ident = sbg("ident_sb", [128, 128], BF16)
        identf = sbg("identf_sb", [128, 128], F32)
        ones_f = sbg("ones_f", [128, 1], F32)
        ones_b = sbg("ones_b", [128, 1], BF16)
        lam_col = sbg("lam_col", [128, 1], F32)
        lam_sc = sbg("lam_sc", [1, 8], F32)
        consts = sbg("consts", [128, 8], F32)
        wmacc = sbg("wmacc", [128, 4], F32)
        wstat = sbg("wstat", [128, 8], F32)
        alph = sbg("alph", [128, 8], F32)
        lamv = sbg("lamv", [1, 64], F32)
        lamv2 = sbg("lamv2", [1, 64], F32)
        lamk = sbg("lamk", [1, 64], F32)
        lamk2 = sbg("lamk2", [1, 64], F32)
        ph12 = ExitStack()
        sb12 = lambda n, sh, d: ph12.enter_context(nc.sbuf_tensor(n, sh, d))
        kfm = [sb12(f"kfm{u}", [128, S_], BF16) for u in range(NU)]
        qfm = [sb12(f"qfm{u}", [128, S_], BF16) for u in range(NU)]
        v_sb = sb12("v_sb", [128, 32 * 512], BF16)

        def setup_consts(e):
            e.memset(consts[:, 0:1], 0.125)
            e.memset(consts[:, 1:2], -CSH)
            e.memset(consts[:, 2:3], 256.5)
            e.memset(consts[:, 3:4], -0.5)
            e.memset(ones_f[:], 1.0)
            e.memset(ones_b[:], 1.0)
            e.memset(wmacc[:], 0.0)
            return e.engine_nop()
        vconst = p.emi("vector", setup_consts, "sV")

        p.dma("sync", ident[:], ident_in[:, :], "sIdent")
        ld_ident = p.dma("sync", identf[:], identf_in[:, :], "sIdent")
        p.dma("sync", lamv[0:1, :], lq1_in[:, :], "sLam")
        p.dma("sync", lamk[0:1, :], lk1_in[:, :], "sLam")
        p.dma("sync", lamv2[0:1, :], lq2_in[:, :], "sLam")
        lam_ld = p.dma("sync", lamk2[0:1, :], lk2_in[:, :], "sLam")

        p.wait("vector", "sLam", lam_ld)
        def lam_dots(e):
            e.tensor_tensor(lamv[0:1, :], lamv[0:1, :], lamk[0:1, :], OP.mult)
            e.engine_nop()
            e.tensor_reduce(lam_sc[0:1, 0:1], lamv[0:1, :], AX, OP.add)
            e.tensor_tensor(lamv2[0:1, :], lamv2[0:1, :], lamk2[0:1, :], OP.mult)
            e.engine_nop()
            return e.tensor_reduce(lam_sc[0:1, 1:2], lamv2[0:1, :], AX, OP.add)
        v_dots = p.emi("vector", lam_dots, "sV")
        p.wait("scalar", "sV", v_dots)
        a_exp = p.emi("scalar", lambda e:
                      e.activation(lam_sc[0:1, 2:4], lam_sc[0:1, 0:2], AF.Exp), "sA")
        p.wait("vector", "sA", a_exp)
        def lam_fin(e):
            e.tensor_tensor(lam_sc[0:1, 4:5], lam_sc[0:1, 2:3], lam_sc[0:1, 3:4], OP.subtract)
            e.engine_nop()
            return e.tensor_single_scalar(lam_sc[0:1, 5:6], lam_sc[0:1, 4:5],
                                          LAMBDA_INIT, OP.add)
        v_lam = p.emi("vector", lam_fin, "sV")
        p.wait("sync", "sV", v_lam)
        lam_st = p.dma("sync", lam_d[:, :], lam_sc[0:1, 5:6], "sLam")
        p.wait("sync", "sLam", lam_st)
        lam_bc = p.dma("sync", lam_col[:], lam_d[:, :].partition_broadcast(128), "sLam")
        if "d_lam" in dbg_t:
            p.wait("gpsimd", "sLam", lam_bc)
            p.dma("gpsimd", dbg_t["d_lam"][:, :], lam_col[:], "sG")

        ph1 = ExitStack()
        sb1 = lambda n, s, d: ph1.enter_context(nc.sbuf_tensor(n, s, d))
        ps1 = lambda n, s, d: ph1.enter_context(nc.psum_tensor(n, s, d))
        wq_sb = sb1("wq_sb", [128, 16 * 512], BF16)
        wk_sb = sb1("wk_sb", [128, 16 * 512], BF16)
        wv_sb = sb1("wv_sb", [128, 16 * 512], BF16)
        x_db = [sb1(f"x_db{i}", [128, D_], F32) for i in range(2)]
        sq_scr = sb1("sq_scr", [128, D_], F32)
        xnb_db = [sb1(f"xnb{i}", [128, D_], BF16) for i in range(2)]
        xnfm_blk = sb1("xnfm_blk", [128, 16 * 512], BF16)
        st_sb = sb1("st_sb", [128, 16], F32)
        pQ = [ps1(f"pQ{i}", [128, 512], F32) for i in range(2)]
        pTr = [ps1(f"pTr{i}", [128, 512], BF16) for i in range(2)]

        wld = 0
        for (wsb, win) in ((wq_sb, wq_in), (wk_sb, wk_in), (wv_sb, wv_in)):
            for dc in range(16):
                wld = p.dma("gpsimd", wsb[:, 512 * dc:512 * dc + 512],
                            win[128 * dc:128 * dc + 128, :], "sWp")

        NT = S_ // 128
        x_user = [0, 0]
        x_userV = [0, 0]
        xnb_user = [0, 0]
        tr_copy = [0, 0]
        st_user = [0, 0]
        xsem = ["sX0", "sX1"]

        for t in range(NT):
            db = t % 2
            p.wait("sync", "sA", x_user[db])
            p.wait("sync", "sV", x_userV[db])
            xl = p.dma("sync", x_db[db][:], x_in[128 * t:128 * t + 128, :], xsem[db])
            st = st_sb[:, 8 * db:8 * db + 8]
            p.wait("scalar", xsem[db], xl)
            p.wait("scalar", "sV", st_user[db] if not SERIAL_ACT else p.val("sV"))
            a_sq = p.emi("scalar", lambda e, xt=x_db[db], st=st:
                         e.activation(sq_scr[:], xt[:], AF.Square, accum_out=st[:, 1:2]), "sA")
            p.wait("vector", xsem[db], xl)
            def red1(e, xt=x_db[db], st=st):
                e.tensor_reduce(st[:, 0:1], xt[:], AX, OP.add)
                for _ in range(4):
                    e.engine_nop()
                return e.engine_nop()
            p.emi("vector", red1)
            p.wait("vector", "sA", a_sq)
            def ln_stats(e, st=st):
                e.engine_nop()
                e.tensor_single_scalar(st[:, 2:3], st[:, 0:1], 1.0 / D_, OP.mult)
                e.tensor_single_scalar(st[:, 3:4], st[:, 1:2], 1.0 / D_, OP.mult)
                e.tensor_tensor(st[:, 4:5], st[:, 2:3], st[:, 2:3], OP.mult)
                e.engine_nop()
                e.tensor_tensor(st[:, 5:6], st[:, 3:4], st[:, 4:5], OP.subtract)
                e.engine_nop()
                return e.tensor_single_scalar(st[:, 5:6], st[:, 5:6], EPS, OP.add)
            v_st = p.emi("vector", ln_stats, "sV")
            st_user[db] = v_st
            x_userV[db] = v_st
            p.wait("scalar", "sV", max(v_st, vconst))
            p.emi("scalar", lambda e, st=st: e.activation(st[:, 6:7], st[:, 5:6], AF.Ln))
            a_r = p.emi("scalar", lambda e, st=st:
                        e.activation(st[:, 7:8], st[:, 6:7], AF.Exp, scale=consts[:, 3:4]), "sA")
            p.wait("vector", "sA", a_r)
            def mk_mr(e, st=st):
                e.scalar_tensor_tensor(st[:, 4:5], st[:, 2:3], -1.0, st[:, 7:8],
                                       OP.mult, OP.mult)
                return e.tensor_copy(st[:, 6:7], st[:, 7:8])
            v_mr = p.emi("vector", mk_mr, "sV")
            p.wait("scalar", "sV", v_mr)
            p.wait("scalar", "sT", xnb_user[db])
            a_xn = p.emi("scalar", lambda e, xt=x_db[db], st=st, ot=xnb_db[db]:
                         e.activation(ot[:], xt[:], AF.Identity,
                                      bias=st[:, 4:5], scale=st[:, 6:7]), "sA")
            x_user[db] = a_xn

            if "d_xn" in dbg_t:
                p.wait("gpsimd", "sA", a_xn)
                dxn = p.dma("gpsimd", dbg_t["d_xn"][128 * t:128 * t + 128, :],
                            xnb_db[db][:], "sG")
                p.wait("scalar", "sG", dxn)
                if "d_st" in dbg_t and t == 4:
                    p.dma("gpsimd", dbg_t["d_st"][:, :], st_sb[:, :], "sG")
            p.wait("tensor", "sA", a_xn)
            p.wait("tensor", "sIdent", ld_ident)
            tsub = t % 4
            for dc in range(16):
                tb = dc % 2
                p.wait("tensor", "sV", tr_copy[tb] if not SERIAL1 else p.val("sV"))
                t_tr = p.emi("tensor", lambda e, xt=xnb_db[db], dc=dc, pt=pTr[tb]:
                             e.transpose(pt[:, :128], xt[:, 128 * dc:128 * dc + 128],
                                         ident[:]), "sT")
                p.wait("vector", "sT", t_tr)
                tr_copy[tb] = p.emi(
                    "vector", lambda e, dc=dc, tsub=tsub, pt=pTr[tb]:
                    e.tensor_copy(xnfm_blk[:, 512 * dc + 128 * tsub:512 * dc + 128 * tsub + 128],
                                  pt[:, :128]), "sV")
            xnb_user[db] = p.val("sT")

            if t % 4 == 3:
                blk = t // 4
                if SERIAL1:
                    p.wait("tensor", "sV", p.val("sV"))
                    p.wait("tensor", "sA", p.val("sA"))
                if blk == 0 and "d_g" in dbg_t and 1 in phases and 3 not in phases:
                    p.wait("gpsimd", "sV", p.val("sV"))
                    dxf = p.dma("gpsimd", dbg_t["d_g"][0:128, :], xnfm_blk[:], "sG")
                    p.wait("tensor", "sG", dxf)
                p.wait("tensor", "sV", p.val("sV"))
                p.wait("tensor", "sWp", wld)
                pq_guard = [0, 0]
                pi = 0
                for (wsb, dst) in ((wq_sb, qfm), (wk_sb, kfm)):
                    for u in range(NU):
                        cc = 128 * u
                        pqb = pi % 2
                        pi += 1
                        p.wait("tensor", "sV", pq_guard[pqb])
                        if SERIAL1:
                            p.wait("tensor", "sV", p.val("sV"))
                        t_q = None
                        for dc in range(16):
                            t_q = p.emi(
                                "tensor", lambda e, w=wsb, dc=dc, cc=cc, pq=pQ[pqb]:
                                e.matmul(pq[:],
                                         w[:, 512 * dc + cc:512 * dc + cc + 128],
                                         xnfm_blk[:, 512 * dc:512 * dc + 512],
                                         start=(dc == 0), stop=(dc == 15)),
                                "sT" if dc == 15 else None)
                        p.wait("vector", "sT", t_q)
                        pq_guard[pqb] = p.emi(
                            "vector", lambda e, d=dst[u], blk=blk, pq=pQ[pqb]:
                            e.tensor_copy(d[:, 512 * blk:512 * blk + 512],
                                          pq[:]), "sV")
                for tsub2 in range(4):
                    kt = 4 * blk + tsub2
                    pqb = pi % 2
                    pi += 1
                    p.wait("tensor", "sV", pq_guard[pqb])
                    if SERIAL1:
                        p.wait("tensor", "sV", p.val("sV"))
                    t_v = None
                    for dc in range(16):
                        t_v = p.emi(
                            "tensor", lambda e, dc=dc, t2=tsub2, pv=pQ[pqb]:
                            e.matmul(pv[:],
                                     xnfm_blk[:, 512 * dc + 128 * t2:512 * dc + 128 * t2 + 128],
                                     wv_sb[:, 512 * dc:512 * dc + 512],
                                     start=(dc == 0), stop=(dc == 15)),
                            "sT" if dc == 15 else None)
                    p.wait("vector", "sT", t_v)
                    pq_guard[pqb] = p.emi(
                        "vector", lambda e, kt=kt, pv=pQ[pqb]:
                        e.tensor_copy(v_sb[:, 512 * kt:512 * kt + 512], pv[:]), "sV")

        ph1_V = p.val("sV")
        ph1_T = p.val("sT")

        if dbg_t:
            p.wait("gpsimd", "sV", ph1_V)
            p.wait("gpsimd", "sT", ph1_T)
            if "d_qfm" in dbg_t:
                for u in range(NU):
                    p.dma("gpsimd", dbg_t["d_qfm"][128 * u:128 * u + 128, :], qfm[u][:, :], "sG")
            if "d_kfm" in dbg_t:
                for u in range(NU):
                    p.dma("gpsimd", dbg_t["d_kfm"][128 * u:128 * u + 128, :], kfm[u][:, :], "sG")
            if "d_v" in dbg_t:
                for kt in range(32):
                    p.dma("gpsimd", dbg_t["d_v"][128 * kt:128 * kt + 128, :],
                          v_sb[:, 512 * kt:512 * kt + 512], "sG")

        ph1.close()

        cc_v = 0
        at_store = 0
        if 2 in phases:
            ph2 = ExitStack()
            sb2 = lambda n, s, d: ph2.enter_context(nc.sbuf_tensor(n, s, d))
            ps2 = lambda n, s, d: ph2.enter_context(nc.psum_tensor(n, s, d))
            masks_sb = sb2("masks_sb", [128, 4 * 512], F32)
            P1 = [sb2(f"P1_{i}", [128, 512], BF16) for i in range(2)]
            P2 = [sb2(f"P2_{i}", [128, 512], BF16) for i in range(2)]
            a1_sb = sb2("a1_sb", [128, 512], F32)
            a2_sb = sb2("a2_sb", [128, 512], F32)
            lrow = sb2("lrow", [1, 1024], F32)
            linv = sb2("linv", [128, 8], F32)
            comb = sb2("comb", [128, 512], F32)
            rms_scr = sb2("rms_scr", [128, 512], F32)
            rcol = sb2("rcol", [128, 12], F32)
            atn_bf = sb2("atn_bf", [128, 512], BF16)
            atT_bf = sb2("atT_bf", [128, 512], BF16)
            pS1 = [ps2(f"pS1{i}", [128, 512], F32) for i in range(2)]
            pS2 = [ps2(f"pS2{i}", [128, 512], F32) for i in range(2)]
            pA1 = ps2("pA1", [128, 512], F32)
            pA2 = ps2("pA2", [128, 512], F32)
            pL1 = ps2("pL1", [128, 512], F32)
            pL2 = ps2("pL2", [128, 512], F32)
            pTb = pS1[0][:].bitcast(BF16)

            p.wait("sync", "sT", ph1_T)
            p.wait("sync", "sV", ph1_V)
            mld = p.dma("sync", masks_sb[:].rearrange("p (m q) -> p m q", m=4),
                        masks_in.rearrange("(m p) q -> p m q", p=128), "sMsk")

            expA = [0, 0]
            avT = [0, 0]
            prev_store = 0
            for u in range(NU):
                for B in range(8):
                    nk = 4 * B + 4 if NKCAP is None else min(NKCAP, 4 * B + 4)
                    for kt in range(nk):
                        bi = kt % 2
                        p.wait("tensor", "sA", expA[bi])
                        p.wait("tensor", "sV", ph1_V)
                        p.emi("tensor", lambda e, u=u, kt=kt, B=B, pb=pS1[bi]:
                              e.matmul(pb[:], kfm[u][0:64, 128 * kt:128 * kt + 128],
                                       qfm[u][0:64, 512 * B:512 * B + 512],
                                       start=True, stop=True))
                        t_s = p.emi("tensor", lambda e, u=u, kt=kt, B=B, pb=pS2[bi]:
                                    e.matmul(pb[:], kfm[u][64:128, 128 * kt:128 * kt + 128],
                                             qfm[u][64:128, 512 * B:512 * B + 512],
                                             start=True, stop=True), "sT")
                        need_mask = kt >= 4 * B
                        s_mask = 0
                        if need_mask:
                            pv = kt - 4 * B
                            p.wait("vector", "sT", t_s)
                            p.wait("vector", "sMsk", mld)
                            p.emi("vector", lambda e, pb=pS1[bi], pv=pv:
                                  e.tensor_tensor(pb[:], pb[:],
                                                  masks_sb[:, 512 * pv:512 * pv + 512], OP.add))
                            s_mask = p.emi(
                                "vector", lambda e, pb=pS2[bi], pv=pv:
                                e.tensor_tensor(pb[:], pb[:],
                                                masks_sb[:, 512 * pv:512 * pv + 512], OP.add),
                                "sV")
                        p.wait("scalar", "sT", t_s)
                        if need_mask:
                            p.wait("scalar", "sV", s_mask)
                        p.wait("scalar", "sT", avT[bi])
                        p.emi("scalar", lambda e, pb=pS1[bi], o=P1[bi]:
                              e.activation(o[:], pb[:], AF.Exp,
                                           bias=consts[:, 1:2], scale=consts[:, 0:1]))
                        expA[bi] = p.emi("scalar", lambda e, pb=pS2[bi], o=P2[bi]:
                                         e.activation(o[:], pb[:], AF.Exp,
                                                      bias=consts[:, 1:2],
                                                      scale=consts[:, 0:1]), "sA")
                        if u == 0 and B == 0 and kt == 3 and "d_p1" in dbg_t and "d_p2" in dbg_t:
                            p.wait("gpsimd", "sA", expA[bi])
                            p.dma("gpsimd", dbg_t["d_p1"][:, :], P1[bi][:], "sG")
                            dmp = p.dma("gpsimd", dbg_t["d_p2"][:, :], P2[bi][:], "sG")
                            p.wait("tensor", "sG", dmp)
                        if u == 0 and B == 0 and kt == 0 and "d_p1k0" in dbg_t:
                            p.wait("gpsimd", "sA", expA[bi])
                            dmp0b = p.dma("gpsimd", dbg_t["d_p1k0"][:, :], P1[bi][:], "sG")
                            p.wait("tensor", "sG", dmp0b)
                        p.wait("tensor", "sA", expA[bi])
                        fl = dict(start=(kt == 0), stop=(kt == nk - 1))
                        p.emi("tensor", lambda e, u=u, kt=kt, P=P1[bi], f=fl:
                              e.matmul(pA1[:],
                                       v_sb[:, 512 * kt + 128 * u:512 * kt + 128 * u + 128],
                                       P[:], **f))
                        p.emi("tensor", lambda e, u=u, kt=kt, P=P2[bi], f=fl:
                              e.matmul(pA2[:],
                                       v_sb[:, 512 * kt + 128 * u:512 * kt + 128 * u + 128],
                                       P[:], **f))
                        p.emi("tensor", lambda e, P=P1[bi], f=fl:
                              e.matmul(pL1[0:1, 0:512], ones_b[:], P[:], **f))
                        avT[bi] = p.emi("tensor", lambda e, P=P2[bi], f=fl:
                                        e.matmul(pL2[0:1, 0:512], ones_b[:], P[:], **f), "sT")
                    t_all = avT[(nk - 1) % 2]
                    p.wait("vector", "sT", t_all)
                    p.emi("vector", lambda e: e.tensor_copy(lrow[0:1, 0:512], pL1[0:1, 0:512]))
                    p.emi("vector", lambda e: e.tensor_copy(lrow[0:1, 512:1024], pL2[0:1, 0:512]))
                    p.emi("vector", lambda e: e.tensor_copy(a1_sb[:], pA1[:]))
                    evac_V = p.emi("vector", lambda e: e.tensor_copy(a2_sb[:], pA2[:]), "sV")
                    p.wait("tensor", "sV", evac_V)
                    for j in range(4):
                        p.emi("tensor", lambda e, j=j:
                              e.matmul(pL1[:, 8 + j:9 + j], lrow[0:1, 128 * j:128 * j + 128],
                                       ones_f[0:1, 0:1], start=True, stop=True))
                    t_lt = None
                    for j in range(4):
                        t_lt = p.emi("tensor", lambda e, j=j:
                                     e.matmul(pL2[:, 8 + j:9 + j],
                                              lrow[0:1, 512 + 128 * j:640 + 128 * j],
                                              ones_f[0:1, 0:1], start=True, stop=True),
                                     "sT" if j == 3 else None)
                    for j in range(4):
                        p.emi("tensor", lambda e, j=j:
                              e.transpose(pA1[:, 128 * j:128 * j + 128],
                                          a1_sb[:, 128 * j:128 * j + 128], identf[:]))
                    t_tr = None
                    for j in range(4):
                        t_tr = p.emi("tensor", lambda e, j=j:
                                     e.transpose(pA2[:, 128 * j:128 * j + 128],
                                                 a2_sb[:, 128 * j:128 * j + 128], identf[:]),
                                     "sT" if j == 3 else None)
                    p.wait("vector", "sT", t_lt)
                    p.wait("vector", "sLam", lam_bc)
                    p.emi("vector", lambda e: e.reciprocal(linv[:, 0:4], pL1[:, 8:12]))
                    p.emi("vector", lambda e: e.reciprocal(linv[:, 4:8], pL2[:, 8:12]))
                    p.emi("vector", lambda e: e.engine_nop())
                    v_li = p.emi("vector", lambda e:
                                 e.tensor_scalar(linv[:, 4:8], linv[:, 4:8],
                                                 lam_col[:, 0:1], None, OP.mult), "sV")
                    p.emi("vector", lambda e: e.engine_nop())
                    if u == 0 and B == 0 and "d_l" in dbg_t:
                        p.wait("gpsimd", "sV", evac_V)
                        dmp2 = p.dma("gpsimd", dbg_t["d_l"][:, :], lrow[0:1, :], "sG")
                        if "d_a1" in dbg_t and "d_a2" in dbg_t:
                            p.dma("gpsimd", dbg_t["d_a1"][:, :], a1_sb[:], "sG")
                            dmp2 = p.dma("gpsimd", dbg_t["d_a2"][:, :], a2_sb[:], "sG")
                        p.wait("vector", "sG", dmp2)
                    p.wait("vector", "sT", t_tr)
                    a_sq2 = 0
                    for j in range(4):
                        p.wait("vector", "sA", a_sq2)
                        p.emi("vector", lambda e, j=j:
                              e.tensor_scalar(comb[:, 128 * j:128 * j + 128],
                                              pA1[:, 128 * j:128 * j + 128],
                                              linv[:, j:j + 1], None, OP.mult))
                        v_cb = p.emi("vector", lambda e, j=j:
                                     e.scalar_tensor_tensor(comb[:, 128 * j:128 * j + 128],
                                                            pA2[:, 128 * j:128 * j + 128],
                                                            linv[:, 4 + j:5 + j],
                                                            comb[:, 128 * j:128 * j + 128],
                                                            OP.mult, OP.subtract), "sV")
                        p.wait("scalar", "sV", v_cb)
                        a_sq2 = p.emi("scalar", lambda e, j=j:
                                      e.activation(rms_scr[:, 128 * j:128 * j + 128],
                                                   comb[:, 128 * j:128 * j + 128],
                                                   AF.Square, accum_out=rcol[:, j:j + 1]), "sA")
                    if u == 0 and B == 0 and "d_comb" in dbg_t and "d_linv" in dbg_t:
                        p.wait("gpsimd", "sA", a_sq2)
                        p.dma("gpsimd", dbg_t["d_comb"][:, :], comb[:], "sG")
                        dmp3 = p.dma("gpsimd", dbg_t["d_linv"][:, :], linv[:], "sG")
                        p.wait("vector", "sG", dmp3)
                    p.wait("vector", "sA", a_sq2)
                    v_rs = p.emi("vector", lambda e:
                                 e.tensor_scalar(rcol[:, 0:4], rcol[:, 0:4], 1.0 / 128.0,
                                                 EPS, OP.mult, OP.add), "sV")
                    p.wait("scalar", "sV", v_rs)
                    p.emi("scalar", lambda e: e.activation(rcol[:, 4:8], rcol[:, 0:4], AF.Ln))
                    a_rr = p.emi("scalar", lambda e:
                                 e.activation(rcol[:, 8:12], rcol[:, 4:8], AF.Exp,
                                              scale=consts[:, 3:4]), "sA")
                    p.wait("vector", "sA", a_rr)
                    p.emi("vector", lambda e:
                          e.tensor_single_scalar(rcol[:, 8:12], rcol[:, 8:12],
                                                 LAMBDA_INIT - 1.0, OP.mult))
                    p.emi("vector", lambda e: e.engine_nop())
                    if u == 0 and B == 0 and "d_rc" in dbg_t:
                        p.wait("gpsimd", "sV", p.val("sV"))
                        dmp4 = p.dma("gpsimd", dbg_t["d_rc"][:, :], rcol[:], "sG")
                        p.wait("vector", "sG", dmp4)
                    p.wait("vector", "sAT", prev_store)
                    v_at = 0
                    for j in range(4):
                        v_at = p.emi("vector", lambda e, j=j:
                                     e.tensor_scalar(atn_bf[:, 128 * j:128 * j + 128],
                                                     comb[:, 128 * j:128 * j + 128],
                                                     rcol[:, 8 + j:9 + j], None, OP.mult), "sV")
                    p.wait("tensor", "sV", v_at)
                    t_bt = None
                    for j in range(4):
                        t_bt = p.emi("tensor", lambda e, j=j:
                                     e.transpose(pTb[:, 128 * j:128 * j + 128],
                                                 atn_bf[:, 128 * j:128 * j + 128], ident[:]),
                                     "sT" if j == 3 else None)
                    p.wait("vector", "sT", t_bt)
                    v_att = p.emi("vector", lambda e: e.tensor_copy(atT_bf[:], pTb[:, 0:512]), "sV")
                    p.wait("sync", "sV", v_att)
                    prev_store = p.dma("sync",
                                       ag_in_t[128 * u:128 * u + 128, 512 * B:512 * B + 512],
                                       atT_bf[:], "sAT")
                    p.wait("tensor", "sV", v_li)
                    p.wait("tensor", "sV", v_at)
                    p.wait("tensor", "sV", v_att)
            at_store = prev_store

            if "d_attn" in dbg_t:
                p.wait("gpsimd", "sAT", at_store)
                abscr = sb2("abscr", [128, S_], BF16)
                for u in range(NU):
                    lb = p.dma("gpsimd", abscr[:], ag_in_t[128 * u:128 * u + 128, :], "sG")
                    p.wait("gpsimd", "sG", lb)
                    sv = p.dma("gpsimd", dbg_t["d_attn"][128 * u:128 * u + 128, :],
                               abscr[:], "sG")
                    p.wait("gpsimd", "sG", sv)

            if collective:
                p.wait("gpsimd", "sAT", at_store)
                cc_v = p.emi("gpsimd", lambda e:
                             e.collective_compute("AllGather", OP.bypass,
                                                  replica_groups=[[0, 1, 2, 3], [4, 5, 6, 7]],
                                                  ins=[ag_in_t.opt()],
                                                  outs=[ag_out_t.opt()]), "sCC")
            if "d_ag" in dbg_t:
                p.wait("gpsimd", "sCC", cc_v)
                agscr = sb2("agscr", [128, S_], BF16)
                for r in range(16):
                    lb = p.dma("gpsimd", agscr[:], ag_out_t[128 * r:128 * r + 128, :], "sG")
                    p.wait("gpsimd", "sG", lb)
                    sv = p.dma("gpsimd", dbg_t["d_ag"][128 * r:128 * r + 128, :],
                               agscr[:], "sG")
                    p.wait("gpsimd", "sG", sv)
            ph2.close()

        ph2_T = p.val("sT")
        ph2_V = p.val("sV")
        ph2_A = p.val("sA")

        ph12.close()

        if 3 in phases and 4 in phases:
            ph3p = ExitStack()
            sb3p = lambda n, sh, d: ph3p.enter_context(nc.sbuf_tensor(n, sh, d))
            alsc = sb3p("alsc", [128, 48], F32)
            gsums = sb3p("gsums", [128, 8 * 16], F32)
            gssqs = sb3p("gssqs", [128, 8 * 16], F32)
            wmp = sb3p("wmp", [128, 128], F32)
            wrow3 = sb3p("wrow3", [1, 8], F32)

            ph3g = ExitStack()
            sb3g = lambda n, sh, d: ph3g.enter_context(nc.sbuf_tensor(n, sh, d))
            g_sb = sb3g("g_sb", [128, 8 * F_], BF16)
            ph3m = ExitStack()
            mscr = [ph3m.enter_context(nc.sbuf_tensor(f"mscr{i}", [128, D_], BF16))
                    for i in range(2)]

            for qe in ("sync", "gpsimd"):
                p.wait(qe, "sT", ph2_T)
                p.wait(qe, "sV", ph2_V)
                p.wait(qe, "sA", ph2_A)

            msem = ["sM0", "sM1"]
            m_user = [0, 0]
            col = 0
            for wi, win, nr in ((0, w1t_in, 16), (1, w2t_in, 64)):
                for i in range(nr):
                    nchunk = 4 if wi == 0 else 1
                    for j in range(nchunk):
                        db = col % 2
                        p.wait("gpsimd", "sV", m_user[db])
                        ml = p.dma("gpsimd", mscr[db][:],
                                   win[128 * i:128 * i + 128, D_ * j:D_ * j + D_],
                                   msem[db])
                        p.wait("vector", msem[db], ml)
                        m_user[db] = p.emi(
                            "vector", lambda e, db=db, col=col:
                            e.tensor_reduce(wmp[:, col:col + 1], mscr[db][:], AX,
                                            OP.add, apply_absolute_value=True), "sV")
                        col += 1
            assert col == 128
            ph3m.close()
            mean_V = max(m_user)
            ph3ab = ExitStack()
            sb3a = lambda n, sh, d: ph3ab.enter_context(nc.sbuf_tensor(n, sh, d))
            ps3a = lambda n, sh, d: ph3ab.enter_context(nc.psum_tensor(n, sh, d))
            pH = [ps3a(f"pH{i}", [128, 512], F32) for i in range(8)]
            xqfm = sb3a("xqfm", [128, 16 * R_], BF16)
            ph3a = ExitStack()
            sb3aa = lambda n, sh, d: ph3a.enter_context(nc.sbuf_tensor(n, sh, d))
            pXT = pH[0].bitcast(BF16)
            xr_db = sb3aa("xr_db", [128, D_], F32)
            st3 = sb3aa("st3", [128, 16], F32)
            xnf = sb3aa("xnf", [128, D_], F32)
            isc = sb3aa("isc", [128, D_], I32)
            xqt = sb3aa("xqt", [128, D_], BF16)
            vsc = sb3aa("vsc", [128, D_], F32)
            xqt_v = xqt[:]


            def wm_fin(e):
                e.tensor_reduce(alsc[:, 40:41], wmp[:, 0:64], AX, OP.add)
                return e.tensor_reduce(alsc[:, 41:42], wmp[:, 64:128], AX, OP.add)
            v_wm = p.emi("vector", wm_fin, "sV")
            p.wait("tensor", "sV", v_wm)
            p.wait("tensor", "sIdent", ld_ident)
            t_wm = p.emi("tensor", lambda e:
                         e.matmul(pH[0][0:1, 0:2], ones_f[:], alsc[:, 40:42],
                                  start=True, stop=True), "sT")
            p.wait("vector", "sT", t_wm)
            def wrow_math(e):
                e.tensor_copy(wrow3[0:1, 0:2], pH[0][0:1, 0:2])
                e.engine_nop()
                e.tensor_single_scalar(wrow3[0:1, 0:2], wrow3[0:1, 0:2],
                                       1.0 / (D_ * F_), OP.mult)
                e.engine_nop()
                e.tensor_single_scalar(wrow3[0:1, 0:2], wrow3[0:1, 0:2], EPS, OP.max)
                e.engine_nop()
                e.tensor_single_scalar(wrow3[0:1, 2:4], wrow3[0:1, 0:2], 0.5, OP.mult)
                e.engine_nop()
                e.tensor_single_scalar(wrow3[0:1, 4:6], wrow3[0:1, 2:4], -1.0, OP.mult)
                e.engine_nop()
                return e.tensor_single_scalar(wrow3[0:1, 6:8], wrow3[0:1, 0:2],
                                              1.0 / 127.0, OP.mult)
            v_wr = p.emi("vector", wrow_math, "sV")
            p.wait("sync", "sV", v_wr)
            ws_st = p.dma("sync", wstat_d[:, :], wrow3[0:1, :], "sLam")
            p.wait("sync", "sLam", ws_st)
            ws_bc = p.dma("sync", wstat[:, :], wstat_d[:, :].partition_broadcast(128), "sLam")

            p.wait("sync", "sV", mean_V)
            x3_user = 0
            st3_user = 0
            xq_tr_ev = 0
            for tt in range(8):
                p.wait("sync", "sV", x3_user)
                xl = p.dma("sync", xr_db[:], xr_in[128 * tt:128 * tt + 128, :], "sX0")
                st = st3[:, 0:8]
                p.wait("scalar", "sX0", xl)
                p.wait("scalar", "sV", st3_user if not SERIAL_ACT else p.val("sV"))
                a_sq = p.emi("scalar", lambda e:
                             e.activation(xnf[:], xr_db[:], AF.Square,
                                          accum_out=st3[:, 1:2]), "sA")
                p.wait("vector", "sX0", xl)
                p.emi("vector", lambda e: e.tensor_reduce(st3[:, 0:1], xr_db[:], AX, OP.add))
                p.wait("vector", "sA", a_sq)
                def ln3(e):
                    e.engine_nop()
                    e.tensor_single_scalar(st3[:, 2:3], st3[:, 0:1], 1.0 / D_, OP.mult)
                    e.tensor_single_scalar(st3[:, 3:4], st3[:, 1:2], 1.0 / D_, OP.mult)
                    e.tensor_tensor(st3[:, 4:5], st3[:, 2:3], st3[:, 2:3], OP.mult)
                    e.engine_nop()
                    e.tensor_tensor(st3[:, 5:6], st3[:, 3:4], st3[:, 4:5], OP.subtract)
                    e.engine_nop()
                    return e.tensor_single_scalar(st3[:, 5:6], st3[:, 5:6], EPS, OP.add)
                v_st = p.emi("vector", ln3, "sV")
                st3_user = v_st
                p.wait("scalar", "sV", v_st)
                p.emi("scalar", lambda e: e.activation(st3[:, 6:7], st3[:, 5:6], AF.Ln))
                a_r = p.emi("scalar", lambda e:
                            e.activation(st3[:, 7:8], st3[:, 6:7], AF.Exp,
                                         scale=consts[:, 3:4]), "sA")
                p.wait("vector", "sA", a_r)
                def mr3(e):
                    e.scalar_tensor_tensor(st3[:, 4:5], st3[:, 2:3], -1.0, st3[:, 7:8],
                                           OP.mult, OP.mult)
                    return e.tensor_copy(st3[:, 6:7], st3[:, 7:8])
                v_mr = p.emi("vector", mr3, "sV")
                p.wait("scalar", "sV", v_mr)
                a_xn = p.emi("scalar", lambda e:
                             e.activation(xnf[:], xr_db[:], AF.Identity,
                                          bias=st3[:, 4:5], scale=st3[:, 6:7]), "sA")
                p.wait("vector", "sA", a_xn)
                def qsc(e, tt=tt):
                    e.tensor_reduce(st3[:, 8:9], xnf[:], AX, OP.max,
                                    apply_absolute_value=True)
                    for _ in range(4):
                        e.engine_nop()
                    e.tensor_single_scalar(alph[:, tt:tt + 1], st3[:, 8:9], EPS, OP.max)
                    for _ in range(4):
                        e.engine_nop()
                    e.reciprocal(st3[:, 9:10], alph[:, tt:tt + 1])
                    for _ in range(4):
                        e.engine_nop()
                    return e.tensor_single_scalar(alsc[:, tt:tt + 1], st3[:, 9:10],
                                                  127.0, OP.mult)
                v_qs = p.emi("vector", qsc, "sV")
                def qint(e, tt=tt):
                    for _ in range(4):
                        e.engine_nop()
                    e.tensor_scalar(vsc[:], xnf[:], alsc[:, tt:tt + 1], None,
                                    OP.mult)
                    e.tensor_copy(isc[:], vsc[:])
                    return e.tensor_copy(xqt_v[:, 0:D_], isc[:])
                v_qi = p.emi("vector", qint, "sV")
                x3_user = v_qi
                if tt == 0 and "d_ffn" in dbg_t and "d_xq" in dbg_t:
                    p.wait("gpsimd", "sV", v_qi)
                    p.dma("gpsimd", dbg_t["d_ffn"][0:128, :], vsc[:], "sG")
                    dqq = p.dma("gpsimd", dbg_t["d_ffn"][128:256, :], xqt[:], "sG")
                    p.wait("tensor", "sG", dqq)
                    p.wait("vector", "sG", dqq)
                p.wait("tensor", "sV", v_qi)
                for dc in range(16):
                    t_tr = p.emi("tensor", lambda e, dc=dc:
                                 e.transpose(pXT[:, 0:128],
                                             xqt_v[:, 128 * dc:128 * dc + 128],
                                             ident[:]), "sT")
                    p.wait("vector", "sT", t_tr)
                    xq_tr_ev = p.emi(
                        "vector", lambda e, dc=dc, tt=tt:
                        e.tensor_copy(xqfm[:, R_ * dc + 128 * tt:R_ * dc + 128 * tt + 128],
                                      pXT[:, 0:128]), "sV")
                    p.wait("tensor", "sV", xq_tr_ev)
            if "d_xq" in dbg_t:
                p.wait("gpsimd", "sV", p.val("sV"))
                p.dma("gpsimd", dbg_t["d_xq"][0:128, :], xqfm[:, 0:D_], "sG")
                dxq_v = p.dma("gpsimd", dbg_t["d_xq"][128:256, :], xqfm[:, D_:2 * D_], "sG")
                p.wait("vector", "sG", dxq_v)
            p.wait("vector", "sLam", ws_bc)
            v_a1c = p.emi("vector", lambda e:
                          e.tensor_scalar(alsc[:, 8:16], alph[:, 0:8],
                                          wstat[:, 6:7], None, OP.mult), "sV")
            ph3a_A = p.val("sA")
            ph3a_V = p.val("sV")
            ph3a.close()
            ph3b = ExitStack()
            sb3b = lambda n, sh, d: ph3b.enter_context(nc.sbuf_tensor(n, sh, d))
            w1tile = [sb3b(f"w1tile{i}", [128, 512], F32) for i in range(4)]
            tqs = sb3b("tqs", [128, 512], BF16)
            sqb = sb3b("sqb", [128, 512], BF16)
            w1q = [sb3b(f"w1q{i}", [128, 512], BF16) for i in range(4)]
            b1b = [sb3b(f"b1b{i}", [128, 512], F32) for i in range(2)]
            p.wait("sync", "sA", ph3a_A)
            p.wait("sync", "sV", ph3a_V)
            p.wait("gpsimd", "sA", ph3a_A)
            p.wait("gpsimd", "sV", ph3a_V)

            wsem = ["sW10", "sW11", "sW12", "sW13"]
            wqeng = ["sync", "sync", "gpsimd", "gpsimd"]
            w1t_user = [0, 0, 0, 0]
            w1q_user = [0, 0, 0, 0]
            b1_user = [0, 0]
            pH_gelu = [0] * 8
            for fs in range(16):
                sp = fs % 2
                p.wait("sync", "sV", b1_user[sp])
                bl = p.dma("sync", b1b[sp][:],
                           b1_in[0:1, 512 * fs:512 * fs + 512].partition_broadcast(128),
                           "sB0")
                t_mm = None
                for dc in range(16):
                    db = dc % 4
                    p.wait(wqeng[db], "sV", w1t_user[db])
                    wl = p.dma(wqeng[db], w1tile[db][:],
                               w1t_in[128 * dc:128 * dc + 128, 512 * fs:512 * fs + 512],
                               wsem[db])
                    p.wait("vector", wsem[db], wl)
                    p.wait("vector", "sT", w1q_user[db])
                    p.wait("vector", "sLam", ws_bc)
                    def wq1(e, db=db):
                        e.tensor_scalar(tqs[:], w1tile[db][:], wstat[:, 4:5], None,
                                        OP.is_lt)
                        return e.scalar_tensor_tensor(w1q[db][:], w1tile[db][:],
                                                      wstat[:, 2:3], tqs[:],
                                                      OP.is_gt, OP.subtract)
                    q_v = p.emi("vector", wq1, "sV")
                    w1t_user[db] = q_v
                    p.wait("tensor", "sV", q_v)
                    if fs == 0 and dc == 0:
                        p.wait("tensor", "sV", xq_tr_ev)
                    for tt in range(8):
                        if dc == 0:
                            p.wait("tensor", "sA", pH_gelu[tt])
                        t_mm = p.emi(
                            "tensor", lambda e, dc=dc, tt=tt, db=db:
                            e.matmul(pH[tt][:],
                                     xqfm[:, R_ * dc + 128 * tt:R_ * dc + 128 * tt + 128],
                                     w1q[db][:], start=(dc == 0), stop=(dc == 15)),
                            "sT" if tt == 7 else None)
                    w1q_user[db] = t_mm
                if fs == 0 and "d_alpha" in dbg_t:
                    p.wait("gpsimd", "sV", w1t_user[0])
                    p.dma("gpsimd", dbg_t["d_alpha"][:, :], alsc[:, 0:8], "sG")
                p.wait("vector", "sT", t_mm)
                p.wait("vector", "sV", v_a1c)
                p.wait("vector", "sB0", bl)
                for tt in range(8):
                    v_ev = p.emi("vector", lambda e, tt=tt, sp=sp:
                                 e.scalar_tensor_tensor(pH[tt][:], pH[tt][:],
                                                        alsc[:, 8 + tt:9 + tt],
                                                        b1b[sp][:],
                                                        OP.mult, OP.add), "sV")
                    b1_user[sp] = v_ev
                    p.wait("scalar", "sV", v_ev)
                    pH_gelu[tt] = p.emi(
                        "scalar", lambda e, tt=tt, fs=fs:
                        e.activation(g_sb[:, F_ * tt + 512 * fs:F_ * tt + 512 * fs + 512],
                                     pH[tt][:], AF.Gelu,
                                     accum_out=gsums[:, 16 * tt + fs:16 * tt + fs + 1]),
                        "sA")
                    p.emi("scalar", lambda e, tt=tt, fs=fs:
                          e.activation(sqb[:],
                                       g_sb[:, F_ * tt + 512 * fs:F_ * tt + 512 * fs + 512],
                                       AF.Square,
                                       accum_out=gssqs[:, 16 * tt + fs:16 * tt + fs + 1]),
                          "sA")

            ph3b_A = p.val("sA")
            ph3b_V = p.val("sV")
            ph3b_T = p.val("sT")
            if "d_hq" in dbg_t:
                gdump = sb3b("gdump", [128, 512], F32)
                dg_v = 0
                for tt in range(8):
                    for hh in range(F_ // 512):
                        p.wait("vector", "sA", ph3b_A)
                        cv = p.emi("vector", lambda e, tt=tt, hh=hh: e.tensor_copy(
                            gdump[:], g_sb[:, F_ * tt + 512 * hh:F_ * tt + 512 * hh + 512]),
                            "sV")
                        p.wait("gpsimd", "sV", cv)
                        dg_v = p.dma("gpsimd",
                                     dbg_t["d_hq"][128 * tt:128 * tt + 128,
                                                   512 * hh:512 * hh + 512],
                                     gdump[:], "sG")
                        p.wait("vector", "sG", dg_v)
            ph3b.close()
            ph3ab.close()

            ph3c = ExitStack()
            sb3c = lambda n, sh, d: ph3c.enter_context(nc.sbuf_tensor(n, sh, d))
            ps3c = lambda n, sh, d: ph3c.enter_context(nc.psum_tensor(n, sh, d))
            gamb = sb3c("gamb", [128, 512], F32)
            betb = sb3c("betb", [128, 512], F32)
            vsc3 = sb3c("vsc3", [128, F_ // 2], F32)
            isc3 = sb3c("isc3", [128, F_ // 2], I32)
            hqt_scr = sb3c("hqt_scr", [128, F_], BF16)
            st3c = sb3c("st3c", [128, 16], F32)
            pXT2 = ps3c("pXT2", [128, 512], BF16)

            p.wait("sync", "sT", ph3b_T)
            p.wait("sync", "sV", ph3b_V)
            p.wait("sync", "sA", ph3b_A)
            p.wait("vector", "sA", ph3b_A)
            t1_v = 0
            for tt in range(8):
                def lnstats(e, tt=tt):
                    e.tensor_reduce(st3c[:, 0:1], gsums[:, 16 * tt:16 * tt + 16], AX, OP.add)
                    e.tensor_reduce(st3c[:, 1:2], gssqs[:, 16 * tt:16 * tt + 16], AX, OP.add)
                    for _ in range(4):
                        e.engine_nop()
                    e.tensor_single_scalar(st3c[:, 2:3], st3c[:, 0:1], 1.0 / F_, OP.mult)
                    e.tensor_single_scalar(st3c[:, 3:4], st3c[:, 1:2], 1.0 / F_, OP.mult)
                    e.tensor_tensor(st3c[:, 4:5], st3c[:, 2:3], st3c[:, 2:3], OP.mult)
                    e.engine_nop()
                    e.tensor_tensor(st3c[:, 5:6], st3c[:, 3:4], st3c[:, 4:5], OP.subtract)
                    e.engine_nop()
                    return e.tensor_single_scalar(st3c[:, 5:6], st3c[:, 5:6], EPS, OP.add)
                v_s = p.emi("vector", lnstats, "sV")
                p.wait("scalar", "sV", v_s)
                p.emi("scalar", lambda e: e.activation(st3c[:, 6:7], st3c[:, 5:6], AF.Ln))
                a_r = p.emi("scalar", lambda e:
                            e.activation(st3c[:, 7:8], st3c[:, 6:7], AF.Exp,
                                         scale=consts[:, 3:4]), "sA")
                p.wait("vector", "sA", a_r)
                t1_v = p.emi("vector", lambda e, tt=tt:
                             e.tensor_scalar(g_sb[:, F_ * tt:F_ * tt + F_],
                                             g_sb[:, F_ * tt:F_ * tt + F_],
                                             st3c[:, 2:3], st3c[:, 7:8],
                                             OP.subtract, OP.mult), "sV")
            gb_user = 0
            gb_v = 0
            for fs in range(16):
                p.wait("sync", "sV", gb_user)
                gl = p.dma("sync", gamb[:],
                           gam_in[0:1, 512 * fs:512 * fs + 512].partition_broadcast(128),
                           "sB0")
                gl2 = p.dma("sync", betb[:],
                            bet_in[0:1, 512 * fs:512 * fs + 512].partition_broadcast(128),
                            "sB1")
                p.wait("vector", "sB0", gl)
                p.wait("vector", "sB1", gl2)
                p.wait("vector", "sV", t1_v)
                for tt in range(8):
                    sl = slice(F_ * tt + 512 * fs, F_ * tt + 512 * fs + 512)
                    p.emi("vector", lambda e, sl=sl:
                          e.tensor_tensor(g_sb[:, sl], g_sb[:, sl], gamb[:], OP.mult))
                    gb_v = p.emi("vector", lambda e, sl=sl:
                                 e.tensor_tensor(g_sb[:, sl], g_sb[:, sl], betb[:],
                                                 OP.add), "sV")
                gb_user = gb_v
            if "d_3c" in dbg_t:
                gd3 = sb3c("gd3", [128, 512], F32)
            hq_tr_ev = 0
            for tt in range(8):
                p.wait("vector", "sV", gb_v)
                if tt == 0 and "d_3c" in dbg_t:
                    dga = 0
                    for cc in range(16):
                        p.wait("vector", "sG", dga)
                        cva = p.emi("vector", lambda e, cc=cc: e.tensor_copy(
                            gd3[:], g_sb[:, 512 * cc:512 * cc + 512]), "sV")
                        p.wait("gpsimd", "sV", cva)
                        dga = p.dma("gpsimd",
                                    dbg_t["d_3c"][0:128, 512 * cc:512 * cc + 512],
                                    gd3[:], "sG")
                    p.wait("vector", "sG", dga)
                def qsc2(e, tt=tt):
                    e.tensor_reduce(st3c[:, 8:9], g_sb[:, F_ * tt:F_ * tt + F_], AX,
                                    OP.max, apply_absolute_value=True)
                    for _ in range(4):
                        e.engine_nop()
                    e.tensor_single_scalar(alsc[:, 16 + tt:17 + tt], st3c[:, 8:9],
                                           EPS, OP.max)
                    for _ in range(4):
                        e.engine_nop()
                    e.reciprocal(st3c[:, 9:10], alsc[:, 16 + tt:17 + tt])
                    for _ in range(4):
                        e.engine_nop()
                    e.tensor_single_scalar(alsc[:, 24 + tt:25 + tt], st3c[:, 9:10],
                                           127.0, OP.mult)
                    for _ in range(4):
                        e.engine_nop()
                    return e.tensor_scalar(alsc[:, 32 + tt:33 + tt],
                                           alsc[:, 16 + tt:17 + tt],
                                           wstat[:, 7:8], None, OP.mult)
                v_q2 = p.emi("vector", qsc2, "sV")
                v_q2b = 0
                for hh in range(2):
                    hof = (F_ // 2) * hh
                    p.wait("scalar", "sV", max(v_q2, v_q2b))
                    a_v2 = p.emi("scalar", lambda e, tt=tt, hof=hof:
                                 e.activation(vsc3[:],
                                              g_sb[:, F_ * tt + hof:F_ * tt + hof + F_ // 2],
                                              AF.Identity,
                                              scale=alsc[:, 24 + tt:25 + tt]), "sA")
                    p.wait("vector", "sA", a_v2)
                    def qint2(e, hof=hof):
                        e.tensor_copy(isc3[:], vsc3[:])
                        return e.tensor_copy(hqt_scr[:, hof:hof + F_ // 2], isc3[:])
                    v_q2b = p.emi("vector", qint2, "sV")
                if tt == 0 and "d_3c" in dbg_t:
                    p.wait("vector", "sV", v_q2b)
                    dgb = 0
                    for cc in range(16):
                        p.wait("vector", "sG", dgb)
                        cvb = p.emi("vector", lambda e, cc=cc: e.tensor_copy(
                            gd3[:], hqt_scr[:, 512 * cc:512 * cc + 512]), "sV")
                        p.wait("gpsimd", "sV", cvb)
                        dgb = p.dma("gpsimd",
                                    dbg_t["d_3c"][128:256, 512 * cc:512 * cc + 512],
                                    gd3[:], "sG")
                    dgc = p.dma("gpsimd", dbg_t["d_3c"][256:384, 0:24],
                                alsc[:, 16:40], "sG")
                    p.wait("vector", "sG", dgc)
                p.wait("tensor", "sV", v_q2b)
                for fc in range(64):
                    t_tr = p.emi("tensor", lambda e, fc=fc:
                                 e.transpose(pXT2[:, 0:128],
                                             hqt_scr[:, 128 * fc:128 * fc + 128],
                                             ident[:]), "sT")
                    p.wait("vector", "sT", t_tr)
                    hq_tr_ev = p.emi(
                        "vector", lambda e, fc=fc, tt=tt:
                        e.tensor_copy(g_sb[:, F_ * tt + 128 * fc:F_ * tt + 128 * fc + 128],
                                      pXT2[:, 0:128]), "sV")
                    p.wait("tensor", "sV", hq_tr_ev)

            ph3c_V = p.val("sV")
            ph3c_T = p.val("sT")
            ph3c_A = p.val("sA")
            ph3c.close()

            ph3d = ExitStack()
            sb3d = lambda n, sh, d: ph3d.enter_context(nc.sbuf_tensor(n, sh, d))
            ps3d = lambda n, sh, d: ph3d.enter_context(nc.psum_tensor(n, sh, d))
            w2tile = [sb3d(f"w2tile{i}", [128, 512], F32) for i in range(4)]
            fbuf = [sb3d(f"fbuf{i}", [128, 512], BF16) for i in range(2)]
            tqs2 = sb3d("tqs2", [128, 512], BF16)
            w2q = [sb3d(f"w2q{i}", [128, 512], BF16) for i in range(4)]
            b2b = [sb3d(f"b2b{i}", [128, 512], F32) for i in range(2)]
            pM2 = [ps3d(f"pM2_{i}", [128, 512], F32) for i in range(8)]

            p.wait("sync", "sT", ph3c_T)
            p.wait("sync", "sV", ph3c_V)
            p.wait("sync", "sA", ph3c_A)
            p.wait("gpsimd", "sT", ph3c_T)
            p.wait("gpsimd", "sV", ph3c_V)
            p.wait("gpsimd", "sA", ph3c_A)
            w2sem = ["sW20", "sW21", "sW22", "sW23"]
            w2eng = ["sync", "sync", "gpsimd", "gpsimd"]
            w2t_user = [0, 0, 0, 0]
            w2q_user = [0, 0, 0, 0]
            b2_user = [0, 0]
            pM_user = 0
            for ds in range(4):
                sp = ds % 2
                p.wait("sync", "sV", b2_user[sp])
                b2l = p.dma("sync", b2b[sp][:],
                            b2_in[0:1, 512 * ds:512 * ds + 512].partition_broadcast(128),
                            "sB0")
                t_mm2 = None
                for fc in range(64):
                    db = fc % 4
                    p.wait(w2eng[db], "sV", w2t_user[db])
                    wl = p.dma(w2eng[db], w2tile[db][:],
                               w2t_in[128 * fc:128 * fc + 128, 512 * ds:512 * ds + 512],
                               w2sem[db])
                    p.wait("vector", w2sem[db], wl)
                    p.wait("vector", "sT", w2q_user[db])
                    def wq2(e, db=db):
                        e.tensor_scalar(tqs2[:], w2tile[db][:], wstat[:, 5:6], None,
                                        OP.is_lt)
                        return e.scalar_tensor_tensor(w2q[db][:], w2tile[db][:],
                                                      wstat[:, 3:4], tqs2[:],
                                                      OP.is_gt, OP.subtract)
                    q_v = p.emi("vector", wq2, "sV")
                    w2t_user[db] = q_v
                    p.wait("tensor", "sV", q_v)
                    if fc == 0:
                        p.wait("tensor", "sV", pM_user)
                        p.wait("tensor", "sV", hq_tr_ev)
                    for tt in range(8):
                        t_mm2 = p.emi(
                            "tensor", lambda e, tt=tt, fc=fc, db=db:
                            e.matmul(pM2[tt][:],
                                     g_sb[:, F_ * tt + 128 * fc:F_ * tt + 128 * fc + 128],
                                     w2q[db][:], start=(fc == 0), stop=(fc == 63)),
                            "sT" if tt == 7 else None)
                    w2q_user[db] = t_mm2
                p.wait("vector", "sT", t_mm2)
                p.wait("vector", "sB0", b2l)
                fb_user = [0, 0]
                for tt in range(8):
                    fb = tt % 2
                    p.wait("vector", "sSt", fb_user[fb])
                    pM_user = p.emi(
                        "vector", lambda e, tt=tt, fb=fb, sp=sp:
                        e.scalar_tensor_tensor(
                            fbuf[fb][:],
                            pM2[tt][:], alsc[:, 32 + tt:33 + tt], b2b[sp][:],
                            OP.mult, OP.add), "sV")
                    p.wait("sync", "sV", pM_user)
                    fb_user[fb] = p.dma(
                        "sync", ffn_d[128 * tt:128 * tt + 128, 512 * ds:512 * ds + 512],
                        fbuf[fb][:], "sSt")
                b2_user[sp] = pM_user
                ffn_st = fb_user[1]

            ph3d_V = p.val("sV")
            ph3d.close()
            ph3g.close()

            if "d_g" in dbg_t:
                p.wait("gpsimd", "sV", ph3d_V)
                for tt in range(8):
                    p.dma("gpsimd", dbg_t["d_g"][128 * tt:128 * tt + 128, :],
                          g_sb[:, F_ * tt:F_ * tt + F_], "sG")
            if "d_ffn" in dbg_t:
                p.wait("gpsimd", "sSt", ffn_st)
                fscr = ph3p.enter_context(nc.sbuf_tensor("fscr", [128, D_], BF16))
                fscr2 = ph3p.enter_context(nc.sbuf_tensor("fscr2", [128, D_], F32))
                for tt in range(8):
                    fl2 = p.dma("gpsimd", fscr[:], ffn_d[128 * tt:128 * tt + 128, :], "sG")
                    p.wait("vector", "sG", fl2)
                    cv2 = p.emi("vector", lambda e: e.tensor_copy(fscr2[:], fscr[:]), "sV")
                    p.wait("gpsimd", "sV", cv2)
                    sv2 = p.dma("gpsimd", dbg_t["d_ffn"][128 * tt:128 * tt + 128, :],
                                fscr2[:], "sG")
                    p.wait("gpsimd", "sG", sv2)
                    p.wait("vector", "sG", sv2)
            if "d_thr" in dbg_t:
                p.wait("gpsimd", "sLam", ws_bc)
                p.dma("gpsimd", dbg_t["d_thr"][:, :], wstat[:, :], "sG")

            ph4 = ExitStack()
            sb4 = lambda n, sh, d: ph4.enter_context(nc.sbuf_tensor(n, sh, d))
            ps4 = lambda n, sh, d: ph4.enter_context(nc.psum_tensor(n, sh, d))
            wo_sb = sb4("wo_sb", [128, 16 * D_], BF16)
            ffn_t = [sb4(f"ffn_t{i}", [128, 512], BF16) for i in range(2)]
            afm = sb4("afm", [128, 16 * R_], BF16)
            xr4 = [sb4(f"xr4_{i}", [128, D_], F32) for i in range(2)]
            obuf = [sb4(f"obuf{i}", [128, 512], F32) for i in range(2)]
            osum = sb4("osum", [128, 512], F32)
            pWo = [ps4(f"pWo{i}", [128, 512], F32) for i in range(2)]

            p.wait("gpsimd", "sV", ph3d_V)
            wo_l = 0
            for cc in range(16):
                wo_l = p.dma("gpsimd", wo_sb[:, D_ * cc:D_ * cc + D_],
                             wo_in[128 * cc:128 * cc + 128, :], "sWo")
            p.wait("sync", "sV", ph3d_V)
            p.wait("sync", "sT", p.val("sT"))
            if collective:
                p.wait("sync", "sCC", cc_v)
            agv = p.tick("sAG", 16 * 16)
            def ag_read(e):
                rk = e.cc_rank([[0, 1, 2, 3], [4, 5, 6, 7]])
                last = None
                for cc in range(16):
                    last = e.dma_start(
                        out=afm[:, R_ * cc:R_ * cc + R_],
                        in_=ag_out_t[128 * cc:128 * cc + 128, bass.ts(rk, R_)])
                    last.then_inc(p.sems["sAG"], 16)
                return last
            p.ops["sync"].append(lambda e: ag_read(e))

            xr4_user = [0, 0]
            ob_user = [0, 0]
            ffn_u = [0, 0]
            st_v = 0
            for tt in range(8):
                db = tt % 2
                p.wait("sync", "sSt", xr4_user[db])
                xl = p.dma("sync", xr4[db][:], xr_in[128 * tt:128 * tt + 128, :],
                           ["sX0", "sX1"][db])
                for os in range(4):
                    ob = os % 2
                    p.wait("sync", "sV", ffn_u[ob] if True else 0)
                    fl_ = p.dma("sync", ffn_t[ob][:],
                                ffn_d[128 * tt:128 * tt + 128, 512 * os:512 * os + 512],
                                ["sW10", "sW11"][ob])
                    p.wait("tensor", "sWo", wo_l)
                    p.wait("tensor", "sAG", agv)
                    p.wait("tensor", "sV", ob_user[ob])
                    t_wo = None
                    for cc in range(16):
                        t_wo = p.emi(
                            "tensor", lambda e, cc=cc, tt=tt, os=os, ob=ob:
                            e.matmul(pWo[ob][:],
                                     afm[:, R_ * cc + 128 * tt:R_ * cc + 128 * tt + 128],
                                     wo_sb[:, D_ * cc + 512 * os:D_ * cc + 512 * os + 512],
                                     start=(cc == 0), stop=(cc == 15)),
                            "sT" if cc == 15 else None)
                    p.wait("vector", "sT", t_wo)
                    p.wait("vector", ["sX0", "sX1"][db], xl)
                    p.wait("vector", ["sW10", "sW11"][ob], fl_)
                    def fin(e, tt=tt, os=os, ob=ob, db=db):
                        e.tensor_tensor(osum[:], pWo[ob][:], ffn_t[ob][:], OP.add)
                        return e.tensor_tensor(obuf[ob][:], osum[:],
                                               xr4[db][:, 512 * os:512 * os + 512], OP.add)
                    v_f = p.emi("vector", fin, "sV")
                    ob_user[ob] = v_f
                    ffn_u[ob] = v_f
                    p.wait("sync", "sV", v_f)
                    st_v = p.dma("sync", out_t[128 * tt:128 * tt + 128,
                                               512 * os:512 * os + 512],
                                 obuf[ob][:], "sSt")
                    p.wait("vector", "sSt", st_v)
                xr4_user[db] = st_v
            ph4.close()
            ph3p.close()
        st_total = p.val("sSt")
        if st_total:
            p.wait("gpsimd", "sSt", st_total)
        if cc_v:
            p.wait("gpsimd", "sCC", cc_v)

        with nc.Block() as block:
            @block.sync
            def _(e):
                p.run(e, "sync")

            @block.tensor
            def _(e):
                p.run(e, "tensor")

            @block.vector
            def _(e):
                p.run(e, "vector")

            @block.scalar
            def _(e):
                p.run(e, "scalar")

            @block.gpsimd
            def _(e):
                p.run(e, "gpsimd")

    return nc


def _shard_inputs(inputs):
    x = np.asarray(inputs["L0_attn_x"], dtype=np.float32)
    Wq = np.asarray(inputs["L0_attn_Wq"], dtype=np.float32)
    Wk = np.asarray(inputs["L0_attn_Wk"], dtype=np.float32)
    Wv = np.asarray(inputs["L0_attn_Wv"], dtype=np.float32)
    Wo = np.asarray(inputs["L0_attn_Wo"], dtype=np.float32)
    w1 = np.asarray(inputs["L0_ffn_w1"], dtype=np.float32)
    w2 = np.asarray(inputs["L0_ffn_w2"], dtype=np.float32)
    import ml_dtypes
    ident = np.eye(128, dtype=ml_dtypes.bfloat16)
    masks = np.zeros((4, 128, 512), dtype=np.float32)
    tri = np.tril(np.full((128, 128), NEG, dtype=np.float32), -1)
    for pvar in range(4):
        for j in range(4):
            if j < pvar:
                masks[pvar][:, 128 * j:128 * j + 128] = NEG
            elif j == pvar:
                masks[pvar][:, 128 * j:128 * j + 128] = tri
    masks = masks.reshape(4 * 128, 512)
    w1t = np.ascontiguousarray(w1.T)
    w2t = np.ascontiguousarray(w2.T)
    in_maps = []
    for c in range(8):
        s, g = c // 4, c % 4
        m = {
            "x": np.ascontiguousarray(x[s]),
            "xr": np.ascontiguousarray(x[s][1024 * g:1024 * g + 1024]),
            "wq": np.ascontiguousarray(Wq[:, 512 * g:512 * g + 512]),
            "wk": np.ascontiguousarray(Wk[:, 512 * g:512 * g + 512]),
            "wv": np.ascontiguousarray(Wv[:, 512 * g:512 * g + 512]),
            "wo": Wo,
            "w1t": w1t,
            "w2t": w2t,
            "b1": np.asarray(inputs["L0_ffn_b1"], dtype=np.float32).reshape(1, F_),
            "gam": np.asarray(inputs["L0_ffn_gamma"], dtype=np.float32).reshape(1, F_),
            "bet": np.asarray(inputs["L0_ffn_beta"], dtype=np.float32).reshape(1, F_),
            "b2": np.asarray(inputs["L0_ffn_b2"], dtype=np.float32).reshape(1, D_),
            "lq1": np.asarray(inputs["L0_attn_lq1"], dtype=np.float32).reshape(1, 64),
            "lk1": np.asarray(inputs["L0_attn_lk1"], dtype=np.float32).reshape(1, 64),
            "lq2": np.asarray(inputs["L0_attn_lq2"], dtype=np.float32).reshape(1, 64),
            "lk2": np.asarray(inputs["L0_attn_lk2"], dtype=np.float32).reshape(1, 64),
            "ident": ident,
            "identf": np.eye(128, dtype=np.float32),
            "masks": masks,
        }
        in_maps.append(m)
    return in_maps


def kernel(**inputs):
    in_maps = _shard_inputs(inputs)
    nc = build_nc(dbg=DBG, phases=PHASES)
    res = run_bass_kernel_spmd(nc, in_maps, core_ids=list(range(8)))
    out = np.empty((2, S_, D_), dtype=np.float32)
    for c in range(8):
        s, g = c // 4, c % 4
        out[s, R_ * g:R_ * g + R_] = res.results[c]["out"]
    return out


# version 07
# speedup vs baseline: 1.0940x; 1.0940x over previous
import math
import numpy as np
from contextlib import ExitStack

import concourse.bass as bass
import concourse.mybir as mybir
from concourse.bass_utils import run_bass_kernel_spmd
from concourse.alu_op_type import AluOpType

F32 = mybir.dt.float32
BF16 = mybir.dt.bfloat16
I32 = mybir.dt.int32
AF = mybir.ActivationFunctionType
AX = mybir.AxisListType.X
OP = AluOpType

S_ = 4096
D_ = 2048
F_ = 8192
R_ = 1024
NU = 4
LAMBDA_INIT = 0.8 - 0.6 * math.exp(-0.3 * 12)
CSH = 12.0
EPS = 1e-5
NEG = -1.0e9

PHASES = (1, 2, 3, 4)
DBG = ()
NKCAP = None
SERIAL1 = False
SERIAL_ACT = True


class Prog:
    ENGINES = ("sync", "tensor", "vector", "scalar", "gpsimd")

    def __init__(self):
        self.ops = {e: [] for e in self.ENGINES}
        self.cnt = {}
        self.sems = {}
        self.last_wait = {e: {} for e in self.ENGINES}

    def tick(self, sem, n=1):
        self.cnt[sem] = self.cnt.get(sem, 0) + n
        return self.cnt[sem]

    def val(self, sem):
        return self.cnt.get(sem, 0)

    def emi(self, eng, fn, sem=None, n=1):
        if sem is None:
            self.ops[eng].append(lambda e, fn=fn: fn(e))
            return None
        v = self.tick(sem, n)
        def run(e, fn=fn, sem=sem, n=n):
            fn(e).then_inc(self.sems[sem], n)
        self.ops[eng].append(run)
        return v

    def wait(self, eng, sem, v):
        if not v or v <= 0:
            return
        if self.last_wait[eng].get(sem, 0) >= v:
            return
        self.last_wait[eng][sem] = v
        self.ops[eng].append(lambda e, sem=sem, v=v: e.wait_ge(self.sems[sem], v))

    def dma(self, qeng, dst, src, sem, **kw):
        return self.emi(qeng, lambda e, d=dst, s=src, kw=kw:
                        e.dma_start(out=d, in_=s, **kw), sem, 16)

    def run(self, eng_obj, name):
        for fn in self.ops[name]:
            fn(eng_obj)


def build_nc(dbg=(), phases=PHASES, collective=True):
    nc = bass.Bass("TRN2", target_bir_lowering=False, detect_race_conditions=False)
    ap = lambda t: t.ap()

    x_in = ap(nc.dram_tensor("x", [S_, D_], F32, kind="ExternalInput"))
    xr_in = ap(nc.dram_tensor("xr", [R_, D_], F32, kind="ExternalInput"))
    wstat_d = None
    wq_in = ap(nc.dram_tensor("wq", [D_, 512], F32, kind="ExternalInput"))
    wk_in = ap(nc.dram_tensor("wk", [D_, 512], F32, kind="ExternalInput"))
    wv_in = ap(nc.dram_tensor("wv", [D_, 512], F32, kind="ExternalInput"))
    wo_in = ap(nc.dram_tensor("wo", [D_, D_], F32, kind="ExternalInput"))
    w1t_in = ap(nc.dram_tensor("w1t", [D_, F_], F32, kind="ExternalInput"))
    w2t_in = ap(nc.dram_tensor("w2t", [F_, D_], F32, kind="ExternalInput"))
    b1_in = ap(nc.dram_tensor("b1", [1, F_], F32, kind="ExternalInput"))
    gam_in = ap(nc.dram_tensor("gam", [1, F_], F32, kind="ExternalInput"))
    bet_in = ap(nc.dram_tensor("bet", [1, F_], F32, kind="ExternalInput"))
    b2_in = ap(nc.dram_tensor("b2", [1, D_], F32, kind="ExternalInput"))
    lq1_in = ap(nc.dram_tensor("lq1", [1, 64], F32, kind="ExternalInput"))
    lk1_in = ap(nc.dram_tensor("lk1", [1, 64], F32, kind="ExternalInput"))
    lq2_in = ap(nc.dram_tensor("lq2", [1, 64], F32, kind="ExternalInput"))
    lk2_in = ap(nc.dram_tensor("lk2", [1, 64], F32, kind="ExternalInput"))
    ident_in = ap(nc.dram_tensor("ident", [128, 128], BF16, kind="ExternalInput"))
    identf_in = ap(nc.dram_tensor("identf", [128, 128], F32, kind="ExternalInput"))
    masks_in = ap(nc.dram_tensor("masks", [4 * 128, 512], F32, kind="ExternalInput"))

    out_t = ap(nc.dram_tensor("out", [R_, D_], F32, kind="ExternalOutput"))

    lam_d = ap(nc.dram_tensor("lam_d", [1, 1], F32))
    wstat_d = ap(nc.dram_tensor("wstat_d", [1, 8], F32))
    ffn_d = ap(nc.dram_tensor("ffn_d", [R_, D_], BF16))
    ag_in_t = ap(nc.dram_tensor("ag_in", [512, S_], BF16))
    ag_out_t = ap(nc.dram_tensor("ag_out", [4 * 512, S_], BF16))

    dbg_t = {}
    for name, shape in [
        ("d_qfm", [512, S_]), ("d_kfm", [512, S_]), ("d_v", [S_, 512]),
        ("d_lam", [128, 1]), ("d_attn", [512, S_]), ("d_ag", [2048, S_]),
        ("d_xq", [R_, D_]), ("d_alpha", [128, 8]), ("d_g", [R_, F_]),
        ("d_hq", [R_, F_]), ("d_ffn", [R_, D_]), ("d_thr", [128, 8]),
        ("d_p1", [128, 512]), ("d_p2", [128, 512]), ("d_l", [1, 1024]),
        ("d_xn", [S_, D_]), ("d_st", [128, 16]),
        ("d_s1", [128, 512]), ("d_p1k0", [128, 512]),
        ("d_a1", [128, 512]), ("d_a2", [128, 512]), ("d_a1t", [128, 512]),
        ("d_comb", [128, 512]), ("d_rc", [128, 12]), ("d_linv", [128, 8]),
        ("d_3c", [384, 8192]),
    ]:
        if name in dbg:
            dbg_t[name] = ap(nc.dram_tensor(name, shape, F32, kind="ExternalOutput"))

    p = Prog()

    ctx = ExitStack()
    with ctx:
        sbg = lambda n, s, d, **kw: ctx.enter_context(nc.sbuf_tensor(n, s, d, **kw))
        SEMS = ("sIdent", "sLam", "sX0", "sX1", "sWp", "sMsk", "sW10", "sW11",
                "sW12", "sW13", "sW20", "sW21", "sW22", "sW23",
                "sB0", "sB1", "sWo", "sAG", "sAT", "sSt",
                "sM0", "sM1", "sT", "sV", "sA", "sG", "sCC")
        for s in SEMS:
            p.sems[s] = ctx.enter_context(nc.semaphore(s))

        ident = sbg("ident_sb", [128, 128], BF16)
        identf = sbg("identf_sb", [128, 128], F32)
        ones_f = sbg("ones_f", [128, 1], F32)
        ones_b = sbg("ones_b", [128, 1], BF16)
        lam_col = sbg("lam_col", [128, 1], F32)
        lam_sc = sbg("lam_sc", [1, 8], F32)
        consts = sbg("consts", [128, 8], F32)
        wmacc = sbg("wmacc", [128, 4], F32)
        wstat = sbg("wstat", [128, 8], F32)
        alph = sbg("alph", [128, 8], F32)
        lamv = sbg("lamv", [1, 64], F32)
        lamv2 = sbg("lamv2", [1, 64], F32)
        lamk = sbg("lamk", [1, 64], F32)
        lamk2 = sbg("lamk2", [1, 64], F32)
        ph12 = ExitStack()
        sb12 = lambda n, sh, d: ph12.enter_context(nc.sbuf_tensor(n, sh, d))
        kfm = [sb12(f"kfm{u}", [128, S_], BF16) for u in range(NU)]
        qfm = [sb12(f"qfm{u}", [128, S_], BF16) for u in range(NU)]
        v_sb = sb12("v_sb", [128, 32 * 512], BF16)

        def setup_consts(e):
            e.memset(consts[:, 0:1], 0.125)
            e.memset(consts[:, 1:2], -CSH)
            e.memset(consts[:, 2:3], 256.5)
            e.memset(consts[:, 3:4], -0.5)
            e.memset(ones_f[:], 1.0)
            e.memset(ones_b[:], 1.0)
            e.memset(wmacc[:], 0.0)
            return e.engine_nop()
        vconst = p.emi("vector", setup_consts, "sV")

        p.dma("sync", ident[:], ident_in[:, :], "sIdent")
        ld_ident = p.dma("sync", identf[:], identf_in[:, :], "sIdent")
        p.dma("sync", lamv[0:1, :], lq1_in[:, :], "sLam")
        p.dma("sync", lamk[0:1, :], lk1_in[:, :], "sLam")
        p.dma("sync", lamv2[0:1, :], lq2_in[:, :], "sLam")
        lam_ld = p.dma("sync", lamk2[0:1, :], lk2_in[:, :], "sLam")

        p.wait("vector", "sLam", lam_ld)
        def lam_dots(e):
            e.tensor_tensor(lamv[0:1, :], lamv[0:1, :], lamk[0:1, :], OP.mult)
            e.engine_nop()
            e.tensor_reduce(lam_sc[0:1, 0:1], lamv[0:1, :], AX, OP.add)
            e.tensor_tensor(lamv2[0:1, :], lamv2[0:1, :], lamk2[0:1, :], OP.mult)
            e.engine_nop()
            return e.tensor_reduce(lam_sc[0:1, 1:2], lamv2[0:1, :], AX, OP.add)
        v_dots = p.emi("vector", lam_dots, "sV")
        p.wait("scalar", "sV", v_dots)
        a_exp = p.emi("scalar", lambda e:
                      e.activation(lam_sc[0:1, 2:4], lam_sc[0:1, 0:2], AF.Exp), "sA")
        p.wait("vector", "sA", a_exp)
        def lam_fin(e):
            e.tensor_tensor(lam_sc[0:1, 4:5], lam_sc[0:1, 2:3], lam_sc[0:1, 3:4], OP.subtract)
            e.engine_nop()
            return e.tensor_single_scalar(lam_sc[0:1, 5:6], lam_sc[0:1, 4:5],
                                          LAMBDA_INIT, OP.add)
        v_lam = p.emi("vector", lam_fin, "sV")
        p.wait("sync", "sV", v_lam)
        lam_st = p.dma("sync", lam_d[:, :], lam_sc[0:1, 5:6], "sLam")
        p.wait("sync", "sLam", lam_st)
        lam_bc = p.dma("sync", lam_col[:], lam_d[:, :].partition_broadcast(128), "sLam")
        if "d_lam" in dbg_t:
            p.wait("gpsimd", "sLam", lam_bc)
            p.dma("gpsimd", dbg_t["d_lam"][:, :], lam_col[:], "sG")

        ph1 = ExitStack()
        sb1 = lambda n, s, d: ph1.enter_context(nc.sbuf_tensor(n, s, d))
        ps1 = lambda n, s, d: ph1.enter_context(nc.psum_tensor(n, s, d))
        wq_sb = sb1("wq_sb", [128, 16 * 512], BF16)
        wk_sb = sb1("wk_sb", [128, 16 * 512], BF16)
        wv_sb = sb1("wv_sb", [128, 16 * 512], BF16)
        x_db = [sb1(f"x_db{i}", [128, D_], F32) for i in range(2)]
        sq_scr = sb1("sq_scr", [128, D_], F32)
        xnb_db = [sb1(f"xnb{i}", [128, D_], BF16) for i in range(2)]
        xnfm_blk = sb1("xnfm_blk", [128, 16 * 512], BF16)
        st_sb = sb1("st_sb", [128, 16], F32)
        pQ = [ps1(f"pQ{i}", [128, 512], F32) for i in range(2)]
        pTr = [ps1(f"pTr{i}", [128, 512], BF16) for i in range(2)]

        wld = 0
        for (wsb, win) in ((wq_sb, wq_in), (wk_sb, wk_in), (wv_sb, wv_in)):
            for dc in range(16):
                wld = p.dma("gpsimd", wsb[:, 512 * dc:512 * dc + 512],
                            win[128 * dc:128 * dc + 128, :], "sWp")

        NT = S_ // 128
        x_user = [0, 0]
        x_userV = [0, 0]
        xnb_user = [0, 0]
        tr_copy = [0, 0]
        st_user = [0, 0]
        xsem = ["sX0", "sX1"]

        for t in range(NT):
            db = t % 2
            p.wait("sync", "sA", x_user[db])
            p.wait("sync", "sV", x_userV[db])
            xl = p.dma("sync", x_db[db][:], x_in[128 * t:128 * t + 128, :], xsem[db])
            st = st_sb[:, 8 * db:8 * db + 8]
            p.wait("scalar", xsem[db], xl)
            p.wait("scalar", "sV", st_user[db] if not SERIAL_ACT else p.val("sV"))
            a_sq = p.emi("scalar", lambda e, xt=x_db[db], st=st:
                         e.activation(sq_scr[:], xt[:], AF.Square, accum_out=st[:, 1:2]), "sA")
            p.wait("vector", xsem[db], xl)
            def red1(e, xt=x_db[db], st=st):
                e.tensor_reduce(st[:, 0:1], xt[:], AX, OP.add)
                for _ in range(4):
                    e.engine_nop()
                return e.engine_nop()
            p.emi("vector", red1)
            p.wait("vector", "sA", a_sq)
            def ln_stats(e, st=st):
                e.engine_nop()
                e.tensor_single_scalar(st[:, 2:3], st[:, 0:1], 1.0 / D_, OP.mult)
                e.tensor_single_scalar(st[:, 3:4], st[:, 1:2], 1.0 / D_, OP.mult)
                e.tensor_tensor(st[:, 4:5], st[:, 2:3], st[:, 2:3], OP.mult)
                e.engine_nop()
                e.tensor_tensor(st[:, 5:6], st[:, 3:4], st[:, 4:5], OP.subtract)
                e.engine_nop()
                return e.tensor_single_scalar(st[:, 5:6], st[:, 5:6], EPS, OP.add)
            v_st = p.emi("vector", ln_stats, "sV")
            st_user[db] = v_st
            x_userV[db] = v_st
            p.wait("scalar", "sV", max(v_st, vconst))
            p.emi("scalar", lambda e, st=st: e.activation(st[:, 6:7], st[:, 5:6], AF.Ln))
            a_r = p.emi("scalar", lambda e, st=st:
                        e.activation(st[:, 7:8], st[:, 6:7], AF.Exp, scale=consts[:, 3:4]), "sA")
            p.wait("vector", "sA", a_r)
            def mk_mr(e, st=st):
                e.scalar_tensor_tensor(st[:, 4:5], st[:, 2:3], -1.0, st[:, 7:8],
                                       OP.mult, OP.mult)
                return e.tensor_copy(st[:, 6:7], st[:, 7:8])
            v_mr = p.emi("vector", mk_mr, "sV")
            p.wait("scalar", "sV", v_mr)
            p.wait("scalar", "sT", xnb_user[db])
            a_xn = p.emi("scalar", lambda e, xt=x_db[db], st=st, ot=xnb_db[db]:
                         e.activation(ot[:], xt[:], AF.Identity,
                                      bias=st[:, 4:5], scale=st[:, 6:7]), "sA")
            x_user[db] = a_xn

            if "d_xn" in dbg_t:
                p.wait("gpsimd", "sA", a_xn)
                dxn = p.dma("gpsimd", dbg_t["d_xn"][128 * t:128 * t + 128, :],
                            xnb_db[db][:], "sG")
                p.wait("scalar", "sG", dxn)
                if "d_st" in dbg_t and t == 4:
                    p.dma("gpsimd", dbg_t["d_st"][:, :], st_sb[:, :], "sG")
            p.wait("tensor", "sA", a_xn)
            p.wait("tensor", "sIdent", ld_ident)
            tsub = t % 4
            for dc in range(16):
                tb = dc % 2
                p.wait("tensor", "sV", tr_copy[tb] if not SERIAL1 else p.val("sV"))
                t_tr = p.emi("tensor", lambda e, xt=xnb_db[db], dc=dc, pt=pTr[tb]:
                             e.transpose(pt[:, :128], xt[:, 128 * dc:128 * dc + 128],
                                         ident[:]), "sT")
                p.wait("vector", "sT", t_tr)
                tr_copy[tb] = p.emi(
                    "vector", lambda e, dc=dc, tsub=tsub, pt=pTr[tb]:
                    e.tensor_copy(xnfm_blk[:, 512 * dc + 128 * tsub:512 * dc + 128 * tsub + 128],
                                  pt[:, :128]), "sV")
            xnb_user[db] = p.val("sT")

            if t % 4 == 3:
                blk = t // 4
                if SERIAL1:
                    p.wait("tensor", "sV", p.val("sV"))
                    p.wait("tensor", "sA", p.val("sA"))
                if blk == 0 and "d_g" in dbg_t and 1 in phases and 3 not in phases:
                    p.wait("gpsimd", "sV", p.val("sV"))
                    dxf = p.dma("gpsimd", dbg_t["d_g"][0:128, :], xnfm_blk[:], "sG")
                    p.wait("tensor", "sG", dxf)
                p.wait("tensor", "sV", p.val("sV"))
                p.wait("tensor", "sWp", wld)
                pq_guard = [0, 0]
                pi = 0
                for (wsb, dst) in ((wq_sb, qfm), (wk_sb, kfm)):
                    for u in range(NU):
                        cc = 128 * u
                        pqb = pi % 2
                        pi += 1
                        p.wait("tensor", "sV", pq_guard[pqb])
                        if SERIAL1:
                            p.wait("tensor", "sV", p.val("sV"))
                        t_q = None
                        for dc in range(16):
                            t_q = p.emi(
                                "tensor", lambda e, w=wsb, dc=dc, cc=cc, pq=pQ[pqb]:
                                e.matmul(pq[:],
                                         w[:, 512 * dc + cc:512 * dc + cc + 128],
                                         xnfm_blk[:, 512 * dc:512 * dc + 512],
                                         start=(dc == 0), stop=(dc == 15)),
                                "sT" if dc == 15 else None)
                        p.wait("vector", "sT", t_q)
                        pq_guard[pqb] = p.emi(
                            "vector", lambda e, d=dst[u], blk=blk, pq=pQ[pqb]:
                            e.tensor_copy(d[:, 512 * blk:512 * blk + 512],
                                          pq[:]), "sV")
                for tsub2 in range(4):
                    kt = 4 * blk + tsub2
                    pqb = pi % 2
                    pi += 1
                    p.wait("tensor", "sV", pq_guard[pqb])
                    if SERIAL1:
                        p.wait("tensor", "sV", p.val("sV"))
                    t_v = None
                    for dc in range(16):
                        t_v = p.emi(
                            "tensor", lambda e, dc=dc, t2=tsub2, pv=pQ[pqb]:
                            e.matmul(pv[:],
                                     xnfm_blk[:, 512 * dc + 128 * t2:512 * dc + 128 * t2 + 128],
                                     wv_sb[:, 512 * dc:512 * dc + 512],
                                     start=(dc == 0), stop=(dc == 15)),
                            "sT" if dc == 15 else None)
                    p.wait("vector", "sT", t_v)
                    pq_guard[pqb] = p.emi(
                        "vector", lambda e, kt=kt, pv=pQ[pqb]:
                        e.tensor_copy(v_sb[:, 512 * kt:512 * kt + 512], pv[:]), "sV")

        ph1_V = p.val("sV")
        ph1_T = p.val("sT")

        if dbg_t:
            p.wait("gpsimd", "sV", ph1_V)
            p.wait("gpsimd", "sT", ph1_T)
            if "d_qfm" in dbg_t:
                for u in range(NU):
                    p.dma("gpsimd", dbg_t["d_qfm"][128 * u:128 * u + 128, :], qfm[u][:, :], "sG")
            if "d_kfm" in dbg_t:
                for u in range(NU):
                    p.dma("gpsimd", dbg_t["d_kfm"][128 * u:128 * u + 128, :], kfm[u][:, :], "sG")
            if "d_v" in dbg_t:
                for kt in range(32):
                    p.dma("gpsimd", dbg_t["d_v"][128 * kt:128 * kt + 128, :],
                          v_sb[:, 512 * kt:512 * kt + 512], "sG")

        ph1.close()

        do_wmean = (3 in phases and 4 in phases)
        wmean_state = {"col": 0, "m_user": [0, 0]}
        if do_wmean:
            ph3m = ExitStack()
            sb3m = lambda n, sh, d: ph3m.enter_context(
                nc.sbuf_tensor(n, sh, d, side="right"))
            wmp = sb3m("wmp", [128, 128], F32)
            wrow3 = sb3m("wrow3", [1, 8], F32)
            mscr = [sb3m(f"mscr{i}", [128, D_], BF16) for i in range(2)]
            _wm_src = []
            for wi, win, nr in ((0, w1t_in, 16), (1, w2t_in, 64)):
                for i in range(nr):
                    nchunk = 4 if wi == 0 else 1
                    for j in range(nchunk):
                        _wm_src.append(win[128 * i:128 * i + 128, D_ * j:D_ * j + D_])
            assert len(_wm_src) == 128
            _wm_sem = ["sM0", "sM1"]
            _wm_eng = ["gpsimd", "gpsimd"]

            def wmean_step(n):
                for _ in range(n):
                    col = wmean_state["col"]
                    if col >= 128:
                        return
                    db = col % 2
                    p.wait(_wm_eng[db], "sV", wmean_state["m_user"][db])
                    ml = p.dma(_wm_eng[db], mscr[db][:], _wm_src[col], _wm_sem[db])
                    p.wait("vector", _wm_sem[db], ml)
                    wmean_state["m_user"][db] = p.emi(
                        "vector", lambda e, db=db, col=col:
                        e.tensor_reduce(wmp[:, col:col + 1], mscr[db][:], AX,
                                        OP.add, apply_absolute_value=True), "sV")
                    wmean_state["col"] = col + 1
        else:
            def wmean_step(n):
                pass

        cc_v = 0
        at_store = 0
        if 2 in phases:
            ph2 = ExitStack()
            sb2 = lambda n, s, d: ph2.enter_context(nc.sbuf_tensor(n, s, d))
            ps2 = lambda n, s, d: ph2.enter_context(nc.psum_tensor(n, s, d))
            masks_sb = sb2("masks_sb", [128, 4 * 512], F32)
            P1 = [sb2(f"P1_{i}", [128, 512], BF16) for i in range(2)]
            P2 = [sb2(f"P2_{i}", [128, 512], BF16) for i in range(2)]
            a1_sb = sb2("a1_sb", [128, 512], F32)
            a2_sb = sb2("a2_sb", [128, 512], F32)
            lrow = sb2("lrow", [1, 1024], F32)
            linv = sb2("linv", [128, 8], F32)
            comb = sb2("comb", [128, 512], F32)
            rms_scr = sb2("rms_scr", [128, 512], F32)
            rcol = sb2("rcol", [128, 12], F32)
            atn_bf = sb2("atn_bf", [128, 512], BF16)
            atT_bf = sb2("atT_bf", [128, 512], BF16)
            pS1 = [ps2(f"pS1{i}", [128, 512], F32) for i in range(2)]
            pS2 = [ps2(f"pS2{i}", [128, 512], F32) for i in range(2)]
            pA1 = ps2("pA1", [128, 512], F32)
            pA2 = ps2("pA2", [128, 512], F32)
            pL1 = ps2("pL1", [128, 512], F32)
            pL2 = ps2("pL2", [128, 512], F32)
            pTb = pS1[0][:].bitcast(BF16)

            p.wait("sync", "sT", ph1_T)
            p.wait("sync", "sV", ph1_V)
            mld = p.dma("sync", masks_sb[:].rearrange("p (m q) -> p m q", m=4),
                        masks_in.rearrange("(m p) q -> p m q", p=128), "sMsk")

            expA = [0, 0]
            avT = [0, 0]
            prev_store = 0
            for u in range(NU):
                for B in range(8):
                    nk = 4 * B + 4 if NKCAP is None else min(NKCAP, 4 * B + 4)
                    for kt in range(nk):
                        bi = kt % 2
                        p.wait("tensor", "sA", expA[bi])
                        p.wait("tensor", "sV", ph1_V)
                        p.emi("tensor", lambda e, u=u, kt=kt, B=B, pb=pS1[bi]:
                              e.matmul(pb[:], kfm[u][0:64, 128 * kt:128 * kt + 128],
                                       qfm[u][0:64, 512 * B:512 * B + 512],
                                       start=True, stop=True))
                        t_s = p.emi("tensor", lambda e, u=u, kt=kt, B=B, pb=pS2[bi]:
                                    e.matmul(pb[:], kfm[u][64:128, 128 * kt:128 * kt + 128],
                                             qfm[u][64:128, 512 * B:512 * B + 512],
                                             start=True, stop=True), "sT")
                        need_mask = kt >= 4 * B
                        s_mask = 0
                        if need_mask:
                            pv = kt - 4 * B
                            p.wait("vector", "sT", t_s)
                            p.wait("vector", "sMsk", mld)
                            p.emi("vector", lambda e, pb=pS1[bi], pv=pv:
                                  e.tensor_tensor(pb[:], pb[:],
                                                  masks_sb[:, 512 * pv:512 * pv + 512], OP.add))
                            s_mask = p.emi(
                                "vector", lambda e, pb=pS2[bi], pv=pv:
                                e.tensor_tensor(pb[:], pb[:],
                                                masks_sb[:, 512 * pv:512 * pv + 512], OP.add),
                                "sV")
                        p.wait("scalar", "sT", t_s)
                        if need_mask:
                            p.wait("scalar", "sV", s_mask)
                        p.wait("scalar", "sT", avT[bi])
                        p.emi("scalar", lambda e, pb=pS1[bi], o=P1[bi]:
                              e.activation(o[:], pb[:], AF.Exp,
                                           bias=consts[:, 1:2], scale=consts[:, 0:1]))
                        expA[bi] = p.emi("scalar", lambda e, pb=pS2[bi], o=P2[bi]:
                                         e.activation(o[:], pb[:], AF.Exp,
                                                      bias=consts[:, 1:2],
                                                      scale=consts[:, 0:1]), "sA")
                        if u == 0 and B == 0 and kt == 3 and "d_p1" in dbg_t and "d_p2" in dbg_t:
                            p.wait("gpsimd", "sA", expA[bi])
                            p.dma("gpsimd", dbg_t["d_p1"][:, :], P1[bi][:], "sG")
                            dmp = p.dma("gpsimd", dbg_t["d_p2"][:, :], P2[bi][:], "sG")
                            p.wait("tensor", "sG", dmp)
                        if u == 0 and B == 0 and kt == 0 and "d_p1k0" in dbg_t:
                            p.wait("gpsimd", "sA", expA[bi])
                            dmp0b = p.dma("gpsimd", dbg_t["d_p1k0"][:, :], P1[bi][:], "sG")
                            p.wait("tensor", "sG", dmp0b)
                        p.wait("tensor", "sA", expA[bi])
                        fl = dict(start=(kt == 0), stop=(kt == nk - 1))
                        p.emi("tensor", lambda e, u=u, kt=kt, P=P1[bi], f=fl:
                              e.matmul(pA1[:],
                                       v_sb[:, 512 * kt + 128 * u:512 * kt + 128 * u + 128],
                                       P[:], **f))
                        p.emi("tensor", lambda e, u=u, kt=kt, P=P2[bi], f=fl:
                              e.matmul(pA2[:],
                                       v_sb[:, 512 * kt + 128 * u:512 * kt + 128 * u + 128],
                                       P[:], **f))
                        p.emi("tensor", lambda e, P=P1[bi], f=fl:
                              e.matmul(pL1[0:1, 0:512], ones_b[:], P[:], **f))
                        avT[bi] = p.emi("tensor", lambda e, P=P2[bi], f=fl:
                                        e.matmul(pL2[0:1, 0:512], ones_b[:], P[:], **f), "sT")
                    t_all = avT[(nk - 1) % 2]
                    p.wait("vector", "sT", t_all)
                    p.emi("vector", lambda e: e.tensor_copy(lrow[0:1, 0:512], pL1[0:1, 0:512]))
                    p.emi("vector", lambda e: e.tensor_copy(lrow[0:1, 512:1024], pL2[0:1, 0:512]))
                    p.emi("vector", lambda e: e.tensor_copy(a1_sb[:], pA1[:]))
                    evac_V = p.emi("vector", lambda e: e.tensor_copy(a2_sb[:], pA2[:]), "sV")
                    p.wait("tensor", "sV", evac_V)
                    for j in range(4):
                        p.emi("tensor", lambda e, j=j:
                              e.matmul(pL1[:, 8 + j:9 + j], lrow[0:1, 128 * j:128 * j + 128],
                                       ones_f[0:1, 0:1], start=True, stop=True))
                    t_lt = None
                    for j in range(4):
                        t_lt = p.emi("tensor", lambda e, j=j:
                                     e.matmul(pL2[:, 8 + j:9 + j],
                                              lrow[0:1, 512 + 128 * j:640 + 128 * j],
                                              ones_f[0:1, 0:1], start=True, stop=True),
                                     "sT" if j == 3 else None)
                    for j in range(4):
                        p.emi("tensor", lambda e, j=j:
                              e.transpose(pA1[:, 128 * j:128 * j + 128],
                                          a1_sb[:, 128 * j:128 * j + 128], identf[:]))
                    t_tr = None
                    for j in range(4):
                        t_tr = p.emi("tensor", lambda e, j=j:
                                     e.transpose(pA2[:, 128 * j:128 * j + 128],
                                                 a2_sb[:, 128 * j:128 * j + 128], identf[:]),
                                     "sT" if j == 3 else None)
                    p.wait("vector", "sT", t_lt)
                    p.wait("vector", "sLam", lam_bc)
                    p.emi("vector", lambda e: e.reciprocal(linv[:, 0:4], pL1[:, 8:12]))
                    p.emi("vector", lambda e: e.reciprocal(linv[:, 4:8], pL2[:, 8:12]))
                    p.emi("vector", lambda e: e.engine_nop())
                    v_li = p.emi("vector", lambda e:
                                 e.tensor_scalar(linv[:, 4:8], linv[:, 4:8],
                                                 lam_col[:, 0:1], None, OP.mult), "sV")
                    p.emi("vector", lambda e: e.engine_nop())
                    if u == 0 and B == 0 and "d_l" in dbg_t:
                        p.wait("gpsimd", "sV", evac_V)
                        dmp2 = p.dma("gpsimd", dbg_t["d_l"][:, :], lrow[0:1, :], "sG")
                        if "d_a1" in dbg_t and "d_a2" in dbg_t:
                            p.dma("gpsimd", dbg_t["d_a1"][:, :], a1_sb[:], "sG")
                            dmp2 = p.dma("gpsimd", dbg_t["d_a2"][:, :], a2_sb[:], "sG")
                        p.wait("vector", "sG", dmp2)
                    p.wait("vector", "sT", t_tr)
                    a_sq2 = 0
                    for j in range(4):
                        p.wait("vector", "sA", a_sq2)
                        p.emi("vector", lambda e, j=j:
                              e.tensor_scalar(comb[:, 128 * j:128 * j + 128],
                                              pA1[:, 128 * j:128 * j + 128],
                                              linv[:, j:j + 1], None, OP.mult))
                        v_cb = p.emi("vector", lambda e, j=j:
                                     e.scalar_tensor_tensor(comb[:, 128 * j:128 * j + 128],
                                                            pA2[:, 128 * j:128 * j + 128],
                                                            linv[:, 4 + j:5 + j],
                                                            comb[:, 128 * j:128 * j + 128],
                                                            OP.mult, OP.subtract), "sV")
                        p.wait("scalar", "sV", v_cb)
                        a_sq2 = p.emi("scalar", lambda e, j=j:
                                      e.activation(rms_scr[:, 128 * j:128 * j + 128],
                                                   comb[:, 128 * j:128 * j + 128],
                                                   AF.Square, accum_out=rcol[:, j:j + 1]), "sA")
                    if u == 0 and B == 0 and "d_comb" in dbg_t and "d_linv" in dbg_t:
                        p.wait("gpsimd", "sA", a_sq2)
                        p.dma("gpsimd", dbg_t["d_comb"][:, :], comb[:], "sG")
                        dmp3 = p.dma("gpsimd", dbg_t["d_linv"][:, :], linv[:], "sG")
                        p.wait("vector", "sG", dmp3)
                    p.wait("vector", "sA", a_sq2)
                    v_rs = p.emi("vector", lambda e:
                                 e.tensor_scalar(rcol[:, 0:4], rcol[:, 0:4], 1.0 / 128.0,
                                                 EPS, OP.mult, OP.add), "sV")
                    p.wait("scalar", "sV", v_rs)
                    p.emi("scalar", lambda e: e.activation(rcol[:, 4:8], rcol[:, 0:4], AF.Ln))
                    a_rr = p.emi("scalar", lambda e:
                                 e.activation(rcol[:, 8:12], rcol[:, 4:8], AF.Exp,
                                              scale=consts[:, 3:4]), "sA")
                    p.wait("vector", "sA", a_rr)
                    p.emi("vector", lambda e:
                          e.tensor_single_scalar(rcol[:, 8:12], rcol[:, 8:12],
                                                 LAMBDA_INIT - 1.0, OP.mult))
                    p.emi("vector", lambda e: e.engine_nop())
                    if u == 0 and B == 0 and "d_rc" in dbg_t:
                        p.wait("gpsimd", "sV", p.val("sV"))
                        dmp4 = p.dma("gpsimd", dbg_t["d_rc"][:, :], rcol[:], "sG")
                        p.wait("vector", "sG", dmp4)
                    p.wait("vector", "sAT", prev_store)
                    v_at = 0
                    for j in range(4):
                        v_at = p.emi("vector", lambda e, j=j:
                                     e.tensor_scalar(atn_bf[:, 128 * j:128 * j + 128],
                                                     comb[:, 128 * j:128 * j + 128],
                                                     rcol[:, 8 + j:9 + j], None, OP.mult), "sV")
                    p.wait("tensor", "sV", v_at)
                    t_bt = None
                    for j in range(4):
                        t_bt = p.emi("tensor", lambda e, j=j:
                                     e.transpose(pTb[:, 128 * j:128 * j + 128],
                                                 atn_bf[:, 128 * j:128 * j + 128], ident[:]),
                                     "sT" if j == 3 else None)
                    p.wait("vector", "sT", t_bt)
                    v_att = p.emi("vector", lambda e: e.tensor_copy(atT_bf[:], pTb[:, 0:512]), "sV")
                    p.wait("sync", "sV", v_att)
                    prev_store = p.dma("sync",
                                       ag_in_t[128 * u:128 * u + 128, 512 * B:512 * B + 512],
                                       atT_bf[:], "sAT")
                    p.wait("tensor", "sV", v_li)
                    p.wait("tensor", "sV", v_at)
                    p.wait("tensor", "sV", v_att)
                    wmean_step(4)
            at_store = prev_store

            if "d_attn" in dbg_t:
                p.wait("gpsimd", "sAT", at_store)
                abscr = sb2("abscr", [128, S_], BF16)
                for u in range(NU):
                    lb = p.dma("gpsimd", abscr[:], ag_in_t[128 * u:128 * u + 128, :], "sG")
                    p.wait("gpsimd", "sG", lb)
                    sv = p.dma("gpsimd", dbg_t["d_attn"][128 * u:128 * u + 128, :],
                               abscr[:], "sG")
                    p.wait("gpsimd", "sG", sv)

            if collective:
                p.wait("gpsimd", "sAT", at_store)
                cc_v = p.emi("gpsimd", lambda e:
                             e.collective_compute("AllGather", OP.bypass,
                                                  replica_groups=[[0, 1, 2, 3], [4, 5, 6, 7]],
                                                  ins=[ag_in_t.opt()],
                                                  outs=[ag_out_t.opt()]), "sCC")
            if "d_ag" in dbg_t:
                p.wait("gpsimd", "sCC", cc_v)
                agscr = sb2("agscr", [128, S_], BF16)
                for r in range(16):
                    lb = p.dma("gpsimd", agscr[:], ag_out_t[128 * r:128 * r + 128, :], "sG")
                    p.wait("gpsimd", "sG", lb)
                    sv = p.dma("gpsimd", dbg_t["d_ag"][128 * r:128 * r + 128, :],
                               agscr[:], "sG")
                    p.wait("gpsimd", "sG", sv)
            ph2.close()

        ph2_T = p.val("sT")
        ph2_V = p.val("sV")
        ph2_A = p.val("sA")

        ph12.close()

        if 3 in phases and 4 in phases:
            ph3p = ExitStack()
            sb3p = lambda n, sh, d: ph3p.enter_context(nc.sbuf_tensor(n, sh, d))
            alsc = sb3p("alsc", [128, 48], F32)
            gsums = sb3p("gsums", [128, 8 * 16], F32)
            gssqs = sb3p("gssqs", [128, 8 * 16], F32)

            ph3g = ExitStack()
            sb3g = lambda n, sh, d: ph3g.enter_context(nc.sbuf_tensor(n, sh, d))
            g_sb = sb3g("g_sb", [128, 8 * F_], BF16)

            for qe in ("sync", "gpsimd"):
                p.wait(qe, "sT", ph2_T)
                p.wait(qe, "sV", ph2_V)
                p.wait(qe, "sA", ph2_A)

            wmean_step(128 - wmean_state["col"])
            m_user = wmean_state["m_user"]
            mean_V = max(m_user)
            ph3ab = ExitStack()
            sb3a = lambda n, sh, d: ph3ab.enter_context(nc.sbuf_tensor(n, sh, d))
            ps3a = lambda n, sh, d: ph3ab.enter_context(nc.psum_tensor(n, sh, d))
            pH = [ps3a(f"pH{i}", [128, 512], F32) for i in range(8)]
            xqfm = sb3a("xqfm", [128, 16 * R_], BF16)
            ph3a = ExitStack()
            sb3aa = lambda n, sh, d: ph3a.enter_context(nc.sbuf_tensor(n, sh, d))
            pXT = pH[0].bitcast(BF16)
            xr_db = sb3aa("xr_db", [128, D_], F32)
            st3 = sb3aa("st3", [128, 16], F32)
            xnf = sb3aa("xnf", [128, D_], F32)
            isc = sb3aa("isc", [128, D_], I32)
            xqt = sb3aa("xqt", [128, D_], BF16)
            vsc = sb3aa("vsc", [128, D_], F32)
            xqt_v = xqt[:]


            def wm_fin(e):
                e.tensor_reduce(alsc[:, 40:41], wmp[:, 0:64], AX, OP.add)
                return e.tensor_reduce(alsc[:, 41:42], wmp[:, 64:128], AX, OP.add)
            v_wm = p.emi("vector", wm_fin, "sV")
            p.wait("tensor", "sV", v_wm)
            p.wait("tensor", "sIdent", ld_ident)
            t_wm = p.emi("tensor", lambda e:
                         e.matmul(pH[0][0:1, 0:2], ones_f[:], alsc[:, 40:42],
                                  start=True, stop=True), "sT")
            p.wait("vector", "sT", t_wm)
            def wrow_math(e):
                e.tensor_copy(wrow3[0:1, 0:2], pH[0][0:1, 0:2])
                e.engine_nop()
                e.tensor_single_scalar(wrow3[0:1, 0:2], wrow3[0:1, 0:2],
                                       1.0 / (D_ * F_), OP.mult)
                e.engine_nop()
                e.tensor_single_scalar(wrow3[0:1, 0:2], wrow3[0:1, 0:2], EPS, OP.max)
                e.engine_nop()
                e.tensor_single_scalar(wrow3[0:1, 2:4], wrow3[0:1, 0:2], 0.5, OP.mult)
                e.engine_nop()
                e.tensor_single_scalar(wrow3[0:1, 4:6], wrow3[0:1, 2:4], -1.0, OP.mult)
                e.engine_nop()
                return e.tensor_single_scalar(wrow3[0:1, 6:8], wrow3[0:1, 0:2],
                                              1.0 / 127.0, OP.mult)
            v_wr = p.emi("vector", wrow_math, "sV")
            p.wait("sync", "sV", v_wr)
            ws_st = p.dma("sync", wstat_d[:, :], wrow3[0:1, :], "sLam")
            p.wait("sync", "sLam", ws_st)
            ws_bc = p.dma("sync", wstat[:, :], wstat_d[:, :].partition_broadcast(128), "sLam")

            p.wait("sync", "sV", mean_V)
            x3_user = 0
            st3_user = 0
            xq_tr_ev = 0
            for tt in range(8):
                p.wait("sync", "sV", x3_user)
                xl = p.dma("sync", xr_db[:], xr_in[128 * tt:128 * tt + 128, :], "sX0")
                st = st3[:, 0:8]
                p.wait("scalar", "sX0", xl)
                p.wait("scalar", "sV", st3_user if not SERIAL_ACT else p.val("sV"))
                a_sq = p.emi("scalar", lambda e:
                             e.activation(xnf[:], xr_db[:], AF.Square,
                                          accum_out=st3[:, 1:2]), "sA")
                p.wait("vector", "sX0", xl)
                p.emi("vector", lambda e: e.tensor_reduce(st3[:, 0:1], xr_db[:], AX, OP.add))
                p.wait("vector", "sA", a_sq)
                def ln3(e):
                    e.engine_nop()
                    e.tensor_single_scalar(st3[:, 2:3], st3[:, 0:1], 1.0 / D_, OP.mult)
                    e.tensor_single_scalar(st3[:, 3:4], st3[:, 1:2], 1.0 / D_, OP.mult)
                    e.tensor_tensor(st3[:, 4:5], st3[:, 2:3], st3[:, 2:3], OP.mult)
                    e.engine_nop()
                    e.tensor_tensor(st3[:, 5:6], st3[:, 3:4], st3[:, 4:5], OP.subtract)
                    e.engine_nop()
                    return e.tensor_single_scalar(st3[:, 5:6], st3[:, 5:6], EPS, OP.add)
                v_st = p.emi("vector", ln3, "sV")
                st3_user = v_st
                p.wait("scalar", "sV", v_st)
                p.emi("scalar", lambda e: e.activation(st3[:, 6:7], st3[:, 5:6], AF.Ln))
                a_r = p.emi("scalar", lambda e:
                            e.activation(st3[:, 7:8], st3[:, 6:7], AF.Exp,
                                         scale=consts[:, 3:4]), "sA")
                p.wait("vector", "sA", a_r)
                def mr3(e):
                    e.scalar_tensor_tensor(st3[:, 4:5], st3[:, 2:3], -1.0, st3[:, 7:8],
                                           OP.mult, OP.mult)
                    return e.tensor_copy(st3[:, 6:7], st3[:, 7:8])
                v_mr = p.emi("vector", mr3, "sV")
                p.wait("scalar", "sV", v_mr)
                a_xn = p.emi("scalar", lambda e:
                             e.activation(xnf[:], xr_db[:], AF.Identity,
                                          bias=st3[:, 4:5], scale=st3[:, 6:7]), "sA")
                p.wait("vector", "sA", a_xn)
                def qsc(e, tt=tt):
                    e.tensor_reduce(st3[:, 8:9], xnf[:], AX, OP.max,
                                    apply_absolute_value=True)
                    for _ in range(4):
                        e.engine_nop()
                    e.tensor_single_scalar(alph[:, tt:tt + 1], st3[:, 8:9], EPS, OP.max)
                    for _ in range(4):
                        e.engine_nop()
                    e.reciprocal(st3[:, 9:10], alph[:, tt:tt + 1])
                    for _ in range(4):
                        e.engine_nop()
                    return e.tensor_single_scalar(alsc[:, tt:tt + 1], st3[:, 9:10],
                                                  127.0, OP.mult)
                v_qs = p.emi("vector", qsc, "sV")
                def qint(e, tt=tt):
                    for _ in range(4):
                        e.engine_nop()
                    e.tensor_scalar(vsc[:], xnf[:], alsc[:, tt:tt + 1], None,
                                    OP.mult)
                    e.tensor_copy(isc[:], vsc[:])
                    return e.tensor_copy(xqt_v[:, 0:D_], isc[:])
                v_qi = p.emi("vector", qint, "sV")
                x3_user = v_qi
                if tt == 0 and "d_ffn" in dbg_t and "d_xq" in dbg_t:
                    p.wait("gpsimd", "sV", v_qi)
                    p.dma("gpsimd", dbg_t["d_ffn"][0:128, :], vsc[:], "sG")
                    dqq = p.dma("gpsimd", dbg_t["d_ffn"][128:256, :], xqt[:], "sG")
                    p.wait("tensor", "sG", dqq)
                    p.wait("vector", "sG", dqq)
                p.wait("tensor", "sV", v_qi)
                for dc in range(16):
                    t_tr = p.emi("tensor", lambda e, dc=dc:
                                 e.transpose(pXT[:, 0:128],
                                             xqt_v[:, 128 * dc:128 * dc + 128],
                                             ident[:]), "sT")
                    p.wait("vector", "sT", t_tr)
                    xq_tr_ev = p.emi(
                        "vector", lambda e, dc=dc, tt=tt:
                        e.tensor_copy(xqfm[:, R_ * dc + 128 * tt:R_ * dc + 128 * tt + 128],
                                      pXT[:, 0:128]), "sV")
                    p.wait("tensor", "sV", xq_tr_ev)
            if "d_xq" in dbg_t:
                p.wait("gpsimd", "sV", p.val("sV"))
                p.dma("gpsimd", dbg_t["d_xq"][0:128, :], xqfm[:, 0:D_], "sG")
                dxq_v = p.dma("gpsimd", dbg_t["d_xq"][128:256, :], xqfm[:, D_:2 * D_], "sG")
                p.wait("vector", "sG", dxq_v)
            p.wait("vector", "sLam", ws_bc)
            v_a1c = p.emi("vector", lambda e:
                          e.tensor_scalar(alsc[:, 8:16], alph[:, 0:8],
                                          wstat[:, 6:7], None, OP.mult), "sV")
            ph3a_A = p.val("sA")
            ph3a_V = p.val("sV")
            ph3a.close()
            ph3b = ExitStack()
            sb3b = lambda n, sh, d: ph3b.enter_context(nc.sbuf_tensor(n, sh, d))
            w1tile = [sb3b(f"w1tile{i}", [128, 512], F32) for i in range(4)]
            tqs = sb3b("tqs", [128, 512], BF16)
            sqb = sb3b("sqb", [128, 512], BF16)
            w1q = [sb3b(f"w1q{i}", [128, 512], BF16) for i in range(4)]
            b1b = [sb3b(f"b1b{i}", [128, 512], F32) for i in range(2)]
            p.wait("sync", "sA", ph3a_A)
            p.wait("sync", "sV", ph3a_V)
            p.wait("gpsimd", "sA", ph3a_A)
            p.wait("gpsimd", "sV", ph3a_V)

            wsem = ["sW10", "sW11", "sW12", "sW13"]
            wqeng = ["sync", "sync", "gpsimd", "gpsimd"]
            w1t_user = [0, 0, 0, 0]
            w1q_user = [0, 0, 0, 0]
            b1_user = [0, 0]
            pH_gelu = [0] * 8
            for fs in range(16):
                sp = fs % 2
                p.wait("sync", "sV", b1_user[sp])
                bl = p.dma("sync", b1b[sp][:],
                           b1_in[0:1, 512 * fs:512 * fs + 512].partition_broadcast(128),
                           "sB0")
                t_mm = None
                for dc in range(16):
                    db = dc % 4
                    p.wait(wqeng[db], "sV", w1t_user[db])
                    wl = p.dma(wqeng[db], w1tile[db][:],
                               w1t_in[128 * dc:128 * dc + 128, 512 * fs:512 * fs + 512],
                               wsem[db])
                    p.wait("vector", wsem[db], wl)
                    p.wait("vector", "sT", w1q_user[db])
                    p.wait("vector", "sLam", ws_bc)
                    def wq1(e, db=db):
                        e.tensor_scalar(tqs[:], w1tile[db][:], wstat[:, 4:5], None,
                                        OP.is_lt)
                        return e.scalar_tensor_tensor(w1q[db][:], w1tile[db][:],
                                                      wstat[:, 2:3], tqs[:],
                                                      OP.is_gt, OP.subtract)
                    q_v = p.emi("vector", wq1, "sV")
                    w1t_user[db] = q_v
                    p.wait("tensor", "sV", q_v)
                    if fs == 0 and dc == 0:
                        p.wait("tensor", "sV", xq_tr_ev)
                    for tt in range(8):
                        if dc == 0:
                            p.wait("tensor", "sA", pH_gelu[tt])
                        t_mm = p.emi(
                            "tensor", lambda e, dc=dc, tt=tt, db=db:
                            e.matmul(pH[tt][:],
                                     xqfm[:, R_ * dc + 128 * tt:R_ * dc + 128 * tt + 128],
                                     w1q[db][:], start=(dc == 0), stop=(dc == 15)),
                            "sT" if tt == 7 else None)
                    w1q_user[db] = t_mm
                if fs == 0 and "d_alpha" in dbg_t:
                    p.wait("gpsimd", "sV", w1t_user[0])
                    p.dma("gpsimd", dbg_t["d_alpha"][:, :], alsc[:, 0:8], "sG")
                p.wait("vector", "sT", t_mm)
                p.wait("vector", "sV", v_a1c)
                p.wait("vector", "sB0", bl)
                for tt in range(8):
                    v_ev = p.emi("vector", lambda e, tt=tt, sp=sp:
                                 e.scalar_tensor_tensor(pH[tt][:], pH[tt][:],
                                                        alsc[:, 8 + tt:9 + tt],
                                                        b1b[sp][:],
                                                        OP.mult, OP.add), "sV")
                    b1_user[sp] = v_ev
                    p.wait("scalar", "sV", v_ev)
                    pH_gelu[tt] = p.emi(
                        "scalar", lambda e, tt=tt, fs=fs:
                        e.activation(g_sb[:, F_ * tt + 512 * fs:F_ * tt + 512 * fs + 512],
                                     pH[tt][:], AF.Gelu,
                                     accum_out=gsums[:, 16 * tt + fs:16 * tt + fs + 1]),
                        "sA")
                    p.emi("scalar", lambda e, tt=tt, fs=fs:
                          e.activation(sqb[:],
                                       g_sb[:, F_ * tt + 512 * fs:F_ * tt + 512 * fs + 512],
                                       AF.Square,
                                       accum_out=gssqs[:, 16 * tt + fs:16 * tt + fs + 1]),
                          "sA")

            ph3b_A = p.val("sA")
            ph3b_V = p.val("sV")
            ph3b_T = p.val("sT")
            if "d_hq" in dbg_t:
                gdump = sb3b("gdump", [128, 512], F32)
                dg_v = 0
                for tt in range(8):
                    for hh in range(F_ // 512):
                        p.wait("vector", "sA", ph3b_A)
                        cv = p.emi("vector", lambda e, tt=tt, hh=hh: e.tensor_copy(
                            gdump[:], g_sb[:, F_ * tt + 512 * hh:F_ * tt + 512 * hh + 512]),
                            "sV")
                        p.wait("gpsimd", "sV", cv)
                        dg_v = p.dma("gpsimd",
                                     dbg_t["d_hq"][128 * tt:128 * tt + 128,
                                                   512 * hh:512 * hh + 512],
                                     gdump[:], "sG")
                        p.wait("vector", "sG", dg_v)
            ph3b.close()
            ph3ab.close()

            ph3c = ExitStack()
            sb3c = lambda n, sh, d: ph3c.enter_context(nc.sbuf_tensor(n, sh, d))
            ps3c = lambda n, sh, d: ph3c.enter_context(nc.psum_tensor(n, sh, d))
            gamb = sb3c("gamb", [128, 512], F32)
            betb = sb3c("betb", [128, 512], F32)
            vsc3 = sb3c("vsc3", [128, F_ // 2], F32)
            isc3 = sb3c("isc3", [128, F_ // 2], I32)
            hqt_scr = sb3c("hqt_scr", [128, F_], BF16)
            st3c = sb3c("st3c", [128, 16], F32)
            pXT2 = ps3c("pXT2", [128, 512], BF16)

            p.wait("sync", "sT", ph3b_T)
            p.wait("sync", "sV", ph3b_V)
            p.wait("sync", "sA", ph3b_A)
            p.wait("vector", "sA", ph3b_A)
            t1_v = 0
            for tt in range(8):
                def lnstats(e, tt=tt):
                    e.tensor_reduce(st3c[:, 0:1], gsums[:, 16 * tt:16 * tt + 16], AX, OP.add)
                    e.tensor_reduce(st3c[:, 1:2], gssqs[:, 16 * tt:16 * tt + 16], AX, OP.add)
                    for _ in range(4):
                        e.engine_nop()
                    e.tensor_single_scalar(st3c[:, 2:3], st3c[:, 0:1], 1.0 / F_, OP.mult)
                    e.tensor_single_scalar(st3c[:, 3:4], st3c[:, 1:2], 1.0 / F_, OP.mult)
                    e.tensor_tensor(st3c[:, 4:5], st3c[:, 2:3], st3c[:, 2:3], OP.mult)
                    e.engine_nop()
                    e.tensor_tensor(st3c[:, 5:6], st3c[:, 3:4], st3c[:, 4:5], OP.subtract)
                    e.engine_nop()
                    return e.tensor_single_scalar(st3c[:, 5:6], st3c[:, 5:6], EPS, OP.add)
                v_s = p.emi("vector", lnstats, "sV")
                p.wait("scalar", "sV", v_s)
                p.emi("scalar", lambda e: e.activation(st3c[:, 6:7], st3c[:, 5:6], AF.Ln))
                a_r = p.emi("scalar", lambda e:
                            e.activation(st3c[:, 7:8], st3c[:, 6:7], AF.Exp,
                                         scale=consts[:, 3:4]), "sA")
                p.wait("vector", "sA", a_r)
                t1_v = p.emi("vector", lambda e, tt=tt:
                             e.tensor_scalar(g_sb[:, F_ * tt:F_ * tt + F_],
                                             g_sb[:, F_ * tt:F_ * tt + F_],
                                             st3c[:, 2:3], st3c[:, 7:8],
                                             OP.subtract, OP.mult), "sV")
            gb_user = 0
            gb_v = 0
            for fs in range(16):
                p.wait("sync", "sV", gb_user)
                gl = p.dma("sync", gamb[:],
                           gam_in[0:1, 512 * fs:512 * fs + 512].partition_broadcast(128),
                           "sB0")
                gl2 = p.dma("sync", betb[:],
                            bet_in[0:1, 512 * fs:512 * fs + 512].partition_broadcast(128),
                            "sB1")
                p.wait("vector", "sB0", gl)
                p.wait("vector", "sB1", gl2)
                p.wait("vector", "sV", t1_v)
                for tt in range(8):
                    sl = slice(F_ * tt + 512 * fs, F_ * tt + 512 * fs + 512)
                    p.emi("vector", lambda e, sl=sl:
                          e.tensor_tensor(g_sb[:, sl], g_sb[:, sl], gamb[:], OP.mult))
                    gb_v = p.emi("vector", lambda e, sl=sl:
                                 e.tensor_tensor(g_sb[:, sl], g_sb[:, sl], betb[:],
                                                 OP.add), "sV")
                gb_user = gb_v
            if "d_3c" in dbg_t:
                gd3 = sb3c("gd3", [128, 512], F32)
            hq_tr_ev = 0
            for tt in range(8):
                p.wait("vector", "sV", gb_v)
                if tt == 0 and "d_3c" in dbg_t:
                    dga = 0
                    for cc in range(16):
                        p.wait("vector", "sG", dga)
                        cva = p.emi("vector", lambda e, cc=cc: e.tensor_copy(
                            gd3[:], g_sb[:, 512 * cc:512 * cc + 512]), "sV")
                        p.wait("gpsimd", "sV", cva)
                        dga = p.dma("gpsimd",
                                    dbg_t["d_3c"][0:128, 512 * cc:512 * cc + 512],
                                    gd3[:], "sG")
                    p.wait("vector", "sG", dga)
                def qsc2(e, tt=tt):
                    e.tensor_reduce(st3c[:, 8:9], g_sb[:, F_ * tt:F_ * tt + F_], AX,
                                    OP.max, apply_absolute_value=True)
                    for _ in range(4):
                        e.engine_nop()
                    e.tensor_single_scalar(alsc[:, 16 + tt:17 + tt], st3c[:, 8:9],
                                           EPS, OP.max)
                    for _ in range(4):
                        e.engine_nop()
                    e.reciprocal(st3c[:, 9:10], alsc[:, 16 + tt:17 + tt])
                    for _ in range(4):
                        e.engine_nop()
                    e.tensor_single_scalar(alsc[:, 24 + tt:25 + tt], st3c[:, 9:10],
                                           127.0, OP.mult)
                    for _ in range(4):
                        e.engine_nop()
                    return e.tensor_scalar(alsc[:, 32 + tt:33 + tt],
                                           alsc[:, 16 + tt:17 + tt],
                                           wstat[:, 7:8], None, OP.mult)
                v_q2 = p.emi("vector", qsc2, "sV")
                v_q2b = 0
                for hh in range(2):
                    hof = (F_ // 2) * hh
                    p.wait("scalar", "sV", max(v_q2, v_q2b))
                    a_v2 = p.emi("scalar", lambda e, tt=tt, hof=hof:
                                 e.activation(vsc3[:],
                                              g_sb[:, F_ * tt + hof:F_ * tt + hof + F_ // 2],
                                              AF.Identity,
                                              scale=alsc[:, 24 + tt:25 + tt]), "sA")
                    p.wait("vector", "sA", a_v2)
                    def qint2(e, hof=hof):
                        e.tensor_copy(isc3[:], vsc3[:])
                        return e.tensor_copy(hqt_scr[:, hof:hof + F_ // 2], isc3[:])
                    v_q2b = p.emi("vector", qint2, "sV")
                if tt == 0 and "d_3c" in dbg_t:
                    p.wait("vector", "sV", v_q2b)
                    dgb = 0
                    for cc in range(16):
                        p.wait("vector", "sG", dgb)
                        cvb = p.emi("vector", lambda e, cc=cc: e.tensor_copy(
                            gd3[:], hqt_scr[:, 512 * cc:512 * cc + 512]), "sV")
                        p.wait("gpsimd", "sV", cvb)
                        dgb = p.dma("gpsimd",
                                    dbg_t["d_3c"][128:256, 512 * cc:512 * cc + 512],
                                    gd3[:], "sG")
                    dgc = p.dma("gpsimd", dbg_t["d_3c"][256:384, 0:24],
                                alsc[:, 16:40], "sG")
                    p.wait("vector", "sG", dgc)
                p.wait("tensor", "sV", v_q2b)
                for fc in range(64):
                    t_tr = p.emi("tensor", lambda e, fc=fc:
                                 e.transpose(pXT2[:, 0:128],
                                             hqt_scr[:, 128 * fc:128 * fc + 128],
                                             ident[:]), "sT")
                    p.wait("vector", "sT", t_tr)
                    hq_tr_ev = p.emi(
                        "vector", lambda e, fc=fc, tt=tt:
                        e.tensor_copy(g_sb[:, F_ * tt + 128 * fc:F_ * tt + 128 * fc + 128],
                                      pXT2[:, 0:128]), "sV")
                    p.wait("tensor", "sV", hq_tr_ev)

            ph3c_V = p.val("sV")
            ph3c_T = p.val("sT")
            ph3c_A = p.val("sA")
            ph3c.close()

            ph3d = ExitStack()
            sb3d = lambda n, sh, d: ph3d.enter_context(nc.sbuf_tensor(n, sh, d))
            ps3d = lambda n, sh, d: ph3d.enter_context(nc.psum_tensor(n, sh, d))
            w2tile = [sb3d(f"w2tile{i}", [128, 512], F32) for i in range(4)]
            fbuf = [sb3d(f"fbuf{i}", [128, 512], BF16) for i in range(2)]
            tqs2 = sb3d("tqs2", [128, 512], BF16)
            w2q = [sb3d(f"w2q{i}", [128, 512], BF16) for i in range(4)]
            b2b = [sb3d(f"b2b{i}", [128, 512], F32) for i in range(2)]
            pM2 = [ps3d(f"pM2_{i}", [128, 512], F32) for i in range(8)]

            p.wait("sync", "sT", ph3c_T)
            p.wait("sync", "sV", ph3c_V)
            p.wait("sync", "sA", ph3c_A)
            p.wait("gpsimd", "sT", ph3c_T)
            p.wait("gpsimd", "sV", ph3c_V)
            p.wait("gpsimd", "sA", ph3c_A)
            w2sem = ["sW20", "sW21", "sW22", "sW23"]
            w2eng = ["sync", "sync", "gpsimd", "gpsimd"]
            w2t_user = [0, 0, 0, 0]
            w2q_user = [0, 0, 0, 0]
            b2_user = [0, 0]
            pM_user = 0
            for ds in range(4):
                sp = ds % 2
                p.wait("sync", "sV", b2_user[sp])
                b2l = p.dma("sync", b2b[sp][:],
                            b2_in[0:1, 512 * ds:512 * ds + 512].partition_broadcast(128),
                            "sB0")
                t_mm2 = None
                for fc in range(64):
                    db = fc % 4
                    p.wait(w2eng[db], "sV", w2t_user[db])
                    wl = p.dma(w2eng[db], w2tile[db][:],
                               w2t_in[128 * fc:128 * fc + 128, 512 * ds:512 * ds + 512],
                               w2sem[db])
                    p.wait("vector", w2sem[db], wl)
                    p.wait("vector", "sT", w2q_user[db])
                    def wq2(e, db=db):
                        e.tensor_scalar(tqs2[:], w2tile[db][:], wstat[:, 5:6], None,
                                        OP.is_lt)
                        return e.scalar_tensor_tensor(w2q[db][:], w2tile[db][:],
                                                      wstat[:, 3:4], tqs2[:],
                                                      OP.is_gt, OP.subtract)
                    q_v = p.emi("vector", wq2, "sV")
                    w2t_user[db] = q_v
                    p.wait("tensor", "sV", q_v)
                    if fc == 0:
                        p.wait("tensor", "sV", pM_user)
                        p.wait("tensor", "sV", hq_tr_ev)
                    for tt in range(8):
                        t_mm2 = p.emi(
                            "tensor", lambda e, tt=tt, fc=fc, db=db:
                            e.matmul(pM2[tt][:],
                                     g_sb[:, F_ * tt + 128 * fc:F_ * tt + 128 * fc + 128],
                                     w2q[db][:], start=(fc == 0), stop=(fc == 63)),
                            "sT" if tt == 7 else None)
                    w2q_user[db] = t_mm2
                p.wait("vector", "sT", t_mm2)
                p.wait("vector", "sB0", b2l)
                fb_user = [0, 0]
                for tt in range(8):
                    fb = tt % 2
                    p.wait("vector", "sSt", fb_user[fb])
                    pM_user = p.emi(
                        "vector", lambda e, tt=tt, fb=fb, sp=sp:
                        e.scalar_tensor_tensor(
                            fbuf[fb][:],
                            pM2[tt][:], alsc[:, 32 + tt:33 + tt], b2b[sp][:],
                            OP.mult, OP.add), "sV")
                    p.wait("sync", "sV", pM_user)
                    fb_user[fb] = p.dma(
                        "sync", ffn_d[128 * tt:128 * tt + 128, 512 * ds:512 * ds + 512],
                        fbuf[fb][:], "sSt")
                b2_user[sp] = pM_user
                ffn_st = fb_user[1]

            ph3d_V = p.val("sV")
            ph3d.close()
            ph3g.close()

            if "d_g" in dbg_t:
                p.wait("gpsimd", "sV", ph3d_V)
                for tt in range(8):
                    p.dma("gpsimd", dbg_t["d_g"][128 * tt:128 * tt + 128, :],
                          g_sb[:, F_ * tt:F_ * tt + F_], "sG")
            if "d_ffn" in dbg_t:
                p.wait("gpsimd", "sSt", ffn_st)
                fscr = ph3p.enter_context(nc.sbuf_tensor("fscr", [128, D_], BF16))
                fscr2 = ph3p.enter_context(nc.sbuf_tensor("fscr2", [128, D_], F32))
                for tt in range(8):
                    fl2 = p.dma("gpsimd", fscr[:], ffn_d[128 * tt:128 * tt + 128, :], "sG")
                    p.wait("vector", "sG", fl2)
                    cv2 = p.emi("vector", lambda e: e.tensor_copy(fscr2[:], fscr[:]), "sV")
                    p.wait("gpsimd", "sV", cv2)
                    sv2 = p.dma("gpsimd", dbg_t["d_ffn"][128 * tt:128 * tt + 128, :],
                                fscr2[:], "sG")
                    p.wait("gpsimd", "sG", sv2)
                    p.wait("vector", "sG", sv2)
            if "d_thr" in dbg_t:
                p.wait("gpsimd", "sLam", ws_bc)
                p.dma("gpsimd", dbg_t["d_thr"][:, :], wstat[:, :], "sG")

            ph4 = ExitStack()
            sb4 = lambda n, sh, d: ph4.enter_context(nc.sbuf_tensor(n, sh, d))
            ps4 = lambda n, sh, d: ph4.enter_context(nc.psum_tensor(n, sh, d))
            wo_sb = sb4("wo_sb", [128, 16 * D_], BF16)
            ffn_t = [sb4(f"ffn_t{i}", [128, 512], BF16) for i in range(2)]
            afm = sb4("afm", [128, 16 * R_], BF16)
            xr4 = [sb4(f"xr4_{i}", [128, D_], F32) for i in range(2)]
            obuf = [sb4(f"obuf{i}", [128, 512], F32) for i in range(2)]
            osum = sb4("osum", [128, 512], F32)
            pWo = [ps4(f"pWo{i}", [128, 512], F32) for i in range(2)]

            p.wait("gpsimd", "sV", ph3d_V)
            wo_l = 0
            for cc in range(16):
                wo_l = p.dma("gpsimd", wo_sb[:, D_ * cc:D_ * cc + D_],
                             wo_in[128 * cc:128 * cc + 128, :], "sWo")
            p.wait("sync", "sV", ph3d_V)
            p.wait("sync", "sT", p.val("sT"))
            if collective:
                p.wait("sync", "sCC", cc_v)
            agv = p.tick("sAG", 16 * 16)
            def ag_read(e):
                rk = e.cc_rank([[0, 1, 2, 3], [4, 5, 6, 7]])
                last = None
                for cc in range(16):
                    last = e.dma_start(
                        out=afm[:, R_ * cc:R_ * cc + R_],
                        in_=ag_out_t[128 * cc:128 * cc + 128, bass.ts(rk, R_)])
                    last.then_inc(p.sems["sAG"], 16)
                return last
            p.ops["sync"].append(lambda e: ag_read(e))

            xr4_user = [0, 0]
            ob_user = [0, 0]
            ffn_u = [0, 0]
            st_v = 0
            for tt in range(8):
                db = tt % 2
                p.wait("sync", "sSt", xr4_user[db])
                xl = p.dma("sync", xr4[db][:], xr_in[128 * tt:128 * tt + 128, :],
                           ["sX0", "sX1"][db])
                for os in range(4):
                    ob = os % 2
                    p.wait("sync", "sV", ffn_u[ob] if True else 0)
                    fl_ = p.dma("sync", ffn_t[ob][:],
                                ffn_d[128 * tt:128 * tt + 128, 512 * os:512 * os + 512],
                                ["sW10", "sW11"][ob])
                    p.wait("tensor", "sWo", wo_l)
                    p.wait("tensor", "sAG", agv)
                    p.wait("tensor", "sV", ob_user[ob])
                    t_wo = None
                    for cc in range(16):
                        t_wo = p.emi(
                            "tensor", lambda e, cc=cc, tt=tt, os=os, ob=ob:
                            e.matmul(pWo[ob][:],
                                     afm[:, R_ * cc + 128 * tt:R_ * cc + 128 * tt + 128],
                                     wo_sb[:, D_ * cc + 512 * os:D_ * cc + 512 * os + 512],
                                     start=(cc == 0), stop=(cc == 15)),
                            "sT" if cc == 15 else None)
                    p.wait("vector", "sT", t_wo)
                    p.wait("vector", ["sX0", "sX1"][db], xl)
                    p.wait("vector", ["sW10", "sW11"][ob], fl_)
                    def fin(e, tt=tt, os=os, ob=ob, db=db):
                        e.tensor_tensor(osum[:], pWo[ob][:], ffn_t[ob][:], OP.add)
                        return e.tensor_tensor(obuf[ob][:], osum[:],
                                               xr4[db][:, 512 * os:512 * os + 512], OP.add)
                    v_f = p.emi("vector", fin, "sV")
                    ob_user[ob] = v_f
                    ffn_u[ob] = v_f
                    p.wait("sync", "sV", v_f)
                    st_v = p.dma("sync", out_t[128 * tt:128 * tt + 128,
                                               512 * os:512 * os + 512],
                                 obuf[ob][:], "sSt")
                    p.wait("vector", "sSt", st_v)
                xr4_user[db] = st_v
            ph4.close()
            ph3p.close()
        st_total = p.val("sSt")
        if st_total:
            p.wait("gpsimd", "sSt", st_total)
        if cc_v:
            p.wait("gpsimd", "sCC", cc_v)

        with nc.Block() as block:
            @block.sync
            def _(e):
                p.run(e, "sync")

            @block.tensor
            def _(e):
                p.run(e, "tensor")

            @block.vector
            def _(e):
                p.run(e, "vector")

            @block.scalar
            def _(e):
                p.run(e, "scalar")

            @block.gpsimd
            def _(e):
                p.run(e, "gpsimd")

    return nc


def _shard_inputs(inputs):
    x = np.asarray(inputs["L0_attn_x"], dtype=np.float32)
    Wq = np.asarray(inputs["L0_attn_Wq"], dtype=np.float32)
    Wk = np.asarray(inputs["L0_attn_Wk"], dtype=np.float32)
    Wv = np.asarray(inputs["L0_attn_Wv"], dtype=np.float32)
    Wo = np.asarray(inputs["L0_attn_Wo"], dtype=np.float32)
    w1 = np.asarray(inputs["L0_ffn_w1"], dtype=np.float32)
    w2 = np.asarray(inputs["L0_ffn_w2"], dtype=np.float32)
    import ml_dtypes
    ident = np.eye(128, dtype=ml_dtypes.bfloat16)
    masks = np.zeros((4, 128, 512), dtype=np.float32)
    tri = np.tril(np.full((128, 128), NEG, dtype=np.float32), -1)
    for pvar in range(4):
        for j in range(4):
            if j < pvar:
                masks[pvar][:, 128 * j:128 * j + 128] = NEG
            elif j == pvar:
                masks[pvar][:, 128 * j:128 * j + 128] = tri
    masks = masks.reshape(4 * 128, 512)
    w1t = np.ascontiguousarray(w1.T)
    w2t = np.ascontiguousarray(w2.T)
    in_maps = []
    for c in range(8):
        s, g = c // 4, c % 4
        m = {
            "x": np.ascontiguousarray(x[s]),
            "xr": np.ascontiguousarray(x[s][1024 * g:1024 * g + 1024]),
            "wq": np.ascontiguousarray(Wq[:, 512 * g:512 * g + 512]),
            "wk": np.ascontiguousarray(Wk[:, 512 * g:512 * g + 512]),
            "wv": np.ascontiguousarray(Wv[:, 512 * g:512 * g + 512]),
            "wo": Wo,
            "w1t": w1t,
            "w2t": w2t,
            "b1": np.asarray(inputs["L0_ffn_b1"], dtype=np.float32).reshape(1, F_),
            "gam": np.asarray(inputs["L0_ffn_gamma"], dtype=np.float32).reshape(1, F_),
            "bet": np.asarray(inputs["L0_ffn_beta"], dtype=np.float32).reshape(1, F_),
            "b2": np.asarray(inputs["L0_ffn_b2"], dtype=np.float32).reshape(1, D_),
            "lq1": np.asarray(inputs["L0_attn_lq1"], dtype=np.float32).reshape(1, 64),
            "lk1": np.asarray(inputs["L0_attn_lk1"], dtype=np.float32).reshape(1, 64),
            "lq2": np.asarray(inputs["L0_attn_lq2"], dtype=np.float32).reshape(1, 64),
            "lk2": np.asarray(inputs["L0_attn_lk2"], dtype=np.float32).reshape(1, 64),
            "ident": ident,
            "identf": np.eye(128, dtype=np.float32),
            "masks": masks,
        }
        in_maps.append(m)
    return in_maps


def kernel(**inputs):
    in_maps = _shard_inputs(inputs)
    nc = build_nc(dbg=DBG, phases=PHASES)
    res = run_bass_kernel_spmd(nc, in_maps, core_ids=list(range(8)))
    out = np.empty((2, S_, D_), dtype=np.float32)
    for c in range(8):
        s, g = c // 4, c % 4
        out[s, R_ * g:R_ * g + R_] = res.results[c]["out"]
    return out


# version 08
# speedup vs baseline: 1.1231x; 1.0266x over previous
import math
import numpy as np
from contextlib import ExitStack

import concourse.bass as bass
import concourse.mybir as mybir
from concourse.bass_utils import run_bass_kernel_spmd
from concourse.alu_op_type import AluOpType

F32 = mybir.dt.float32
BF16 = mybir.dt.bfloat16
I32 = mybir.dt.int32
AF = mybir.ActivationFunctionType
AX = mybir.AxisListType.X
OP = AluOpType

S_ = 4096
D_ = 2048
F_ = 8192
R_ = 1024
NU = 4
LAMBDA_INIT = 0.8 - 0.6 * math.exp(-0.3 * 12)
CSH = 12.0
EPS = 1e-5
NEG = -1.0e9

PHASES = (1, 2, 3, 4)
DBG = ()
NKCAP = None
SERIAL1 = False
SERIAL_ACT = True


class Prog:
    ENGINES = ("sync", "tensor", "vector", "scalar", "gpsimd")

    def __init__(self):
        self.ops = {e: [] for e in self.ENGINES}
        self.cnt = {}
        self.sems = {}
        self.last_wait = {e: {} for e in self.ENGINES}

    def tick(self, sem, n=1):
        self.cnt[sem] = self.cnt.get(sem, 0) + n
        return self.cnt[sem]

    def val(self, sem):
        return self.cnt.get(sem, 0)

    def emi(self, eng, fn, sem=None, n=1):
        if sem is None:
            self.ops[eng].append(lambda e, fn=fn: fn(e))
            return None
        v = self.tick(sem, n)
        def run(e, fn=fn, sem=sem, n=n):
            fn(e).then_inc(self.sems[sem], n)
        self.ops[eng].append(run)
        return v

    def wait(self, eng, sem, v):
        if not v or v <= 0:
            return
        if self.last_wait[eng].get(sem, 0) >= v:
            return
        self.last_wait[eng][sem] = v
        self.ops[eng].append(lambda e, sem=sem, v=v: e.wait_ge(self.sems[sem], v))

    def dma(self, qeng, dst, src, sem, **kw):
        return self.emi(qeng, lambda e, d=dst, s=src, kw=kw:
                        e.dma_start(out=d, in_=s, **kw), sem, 16)

    def run(self, eng_obj, name):
        for fn in self.ops[name]:
            fn(eng_obj)


def build_nc(dbg=(), phases=PHASES, collective=True):
    nc = bass.Bass("TRN2", target_bir_lowering=False, detect_race_conditions=False)
    ap = lambda t: t.ap()

    x_in = ap(nc.dram_tensor("x", [S_, D_], F32, kind="ExternalInput"))
    xr_in = ap(nc.dram_tensor("xr", [R_, D_], F32, kind="ExternalInput"))
    wstat_d = None
    wq_in = ap(nc.dram_tensor("wq", [D_, 512], F32, kind="ExternalInput"))
    wk_in = ap(nc.dram_tensor("wk", [D_, 512], F32, kind="ExternalInput"))
    wv_in = ap(nc.dram_tensor("wv", [D_, 512], F32, kind="ExternalInput"))
    wo_in = ap(nc.dram_tensor("wo", [D_, D_], F32, kind="ExternalInput"))
    w1t_in = ap(nc.dram_tensor("w1t", [D_, F_], F32, kind="ExternalInput"))
    w2t_in = ap(nc.dram_tensor("w2t", [F_, D_], F32, kind="ExternalInput"))
    b1_in = ap(nc.dram_tensor("b1", [1, F_], F32, kind="ExternalInput"))
    gam_in = ap(nc.dram_tensor("gam", [1, F_], F32, kind="ExternalInput"))
    bet_in = ap(nc.dram_tensor("bet", [1, F_], F32, kind="ExternalInput"))
    b2_in = ap(nc.dram_tensor("b2", [1, D_], F32, kind="ExternalInput"))
    lq1_in = ap(nc.dram_tensor("lq1", [1, 64], F32, kind="ExternalInput"))
    lk1_in = ap(nc.dram_tensor("lk1", [1, 64], F32, kind="ExternalInput"))
    lq2_in = ap(nc.dram_tensor("lq2", [1, 64], F32, kind="ExternalInput"))
    lk2_in = ap(nc.dram_tensor("lk2", [1, 64], F32, kind="ExternalInput"))
    ident_in = ap(nc.dram_tensor("ident", [128, 128], BF16, kind="ExternalInput"))
    identf_in = ap(nc.dram_tensor("identf", [128, 128], F32, kind="ExternalInput"))
    masks_in = ap(nc.dram_tensor("masks", [4 * 128, 512], F32, kind="ExternalInput"))

    out_t = ap(nc.dram_tensor("out", [R_, D_], F32, kind="ExternalOutput"))

    lam_d = ap(nc.dram_tensor("lam_d", [1, 1], F32))
    wstat_d = ap(nc.dram_tensor("wstat_d", [1, 8], F32))
    ffn_d = ap(nc.dram_tensor("ffn_d", [R_, D_], BF16))
    ag_in_t = ap(nc.dram_tensor("ag_in", [512, S_], BF16))
    ag_out_t = ap(nc.dram_tensor("ag_out", [4 * 512, S_], BF16))

    dbg_t = {}
    for name, shape in [
        ("d_qfm", [512, S_]), ("d_kfm", [512, S_]), ("d_v", [S_, 512]),
        ("d_lam", [128, 1]), ("d_attn", [512, S_]), ("d_ag", [2048, S_]),
        ("d_xq", [R_, D_]), ("d_alpha", [128, 8]), ("d_g", [R_, F_]),
        ("d_hq", [R_, F_]), ("d_ffn", [R_, D_]), ("d_thr", [128, 8]),
        ("d_p1", [128, 512]), ("d_p2", [128, 512]), ("d_l", [1, 1024]),
        ("d_xn", [S_, D_]), ("d_st", [128, 16]),
        ("d_s1", [128, 512]), ("d_p1k0", [128, 512]),
        ("d_a1", [128, 512]), ("d_a2", [128, 512]), ("d_a1t", [128, 512]),
        ("d_comb", [128, 512]), ("d_rc", [128, 12]), ("d_linv", [128, 8]),
        ("d_3c", [384, 8192]),
    ]:
        if name in dbg:
            dbg_t[name] = ap(nc.dram_tensor(name, shape, F32, kind="ExternalOutput"))

    p = Prog()

    ctx = ExitStack()
    with ctx:
        sbg = lambda n, s, d, **kw: ctx.enter_context(nc.sbuf_tensor(n, s, d, **kw))
        SEMS = ("sIdent", "sLam", "sX0", "sX1", "sWp", "sMsk", "sW10", "sW11",
                "sW12", "sW13", "sW20", "sW21", "sW22", "sW23",
                "sB0", "sB1", "sWo", "sAG", "sAT", "sSt",
                "sM0", "sM1", "sT", "sV", "sA", "sG", "sCC")
        for s in SEMS:
            p.sems[s] = ctx.enter_context(nc.semaphore(s))

        ident = sbg("ident_sb", [128, 128], BF16)
        identf = sbg("identf_sb", [128, 128], F32)
        ones_f = sbg("ones_f", [128, 1], F32)
        ones_b = sbg("ones_b", [128, 1], BF16)
        lam_col = sbg("lam_col", [128, 1], F32)
        lam_sc = sbg("lam_sc", [1, 8], F32)
        consts = sbg("consts", [128, 8], F32)
        wmacc = sbg("wmacc", [128, 4], F32)
        wstat = sbg("wstat", [128, 8], F32)
        alph = sbg("alph", [128, 8], F32)
        lamv = sbg("lamv", [1, 64], F32)
        lamv2 = sbg("lamv2", [1, 64], F32)
        lamk = sbg("lamk", [1, 64], F32)
        lamk2 = sbg("lamk2", [1, 64], F32)
        ph12 = ExitStack()
        sb12 = lambda n, sh, d: ph12.enter_context(nc.sbuf_tensor(n, sh, d))
        kfm = [sb12(f"kfm{u}", [128, S_], BF16) for u in range(NU)]
        qfm = [sb12(f"qfm{u}", [128, S_], BF16) for u in range(NU)]
        v_sb = sb12("v_sb", [128, 32 * 512], BF16)

        def setup_consts(e):
            e.memset(consts[:, 0:1], 0.125)
            e.memset(consts[:, 1:2], -CSH)
            e.memset(consts[:, 2:3], 256.5)
            e.memset(consts[:, 3:4], -0.5)
            e.memset(ones_f[:], 1.0)
            e.memset(ones_b[:], 1.0)
            e.memset(wmacc[:], 0.0)
            return e.engine_nop()
        vconst = p.emi("vector", setup_consts, "sV")

        p.dma("sync", ident[:], ident_in[:, :], "sIdent")
        ld_ident = p.dma("sync", identf[:], identf_in[:, :], "sIdent")
        p.dma("sync", lamv[0:1, :], lq1_in[:, :], "sLam")
        p.dma("sync", lamk[0:1, :], lk1_in[:, :], "sLam")
        p.dma("sync", lamv2[0:1, :], lq2_in[:, :], "sLam")
        lam_ld = p.dma("sync", lamk2[0:1, :], lk2_in[:, :], "sLam")

        p.wait("vector", "sLam", lam_ld)
        def lam_dots(e):
            e.tensor_tensor(lamv[0:1, :], lamv[0:1, :], lamk[0:1, :], OP.mult)
            e.engine_nop()
            e.tensor_reduce(lam_sc[0:1, 0:1], lamv[0:1, :], AX, OP.add)
            e.tensor_tensor(lamv2[0:1, :], lamv2[0:1, :], lamk2[0:1, :], OP.mult)
            e.engine_nop()
            return e.tensor_reduce(lam_sc[0:1, 1:2], lamv2[0:1, :], AX, OP.add)
        v_dots = p.emi("vector", lam_dots, "sV")
        p.wait("scalar", "sV", v_dots)
        a_exp = p.emi("scalar", lambda e:
                      e.activation(lam_sc[0:1, 2:4], lam_sc[0:1, 0:2], AF.Exp), "sA")
        p.wait("vector", "sA", a_exp)
        def lam_fin(e):
            e.tensor_tensor(lam_sc[0:1, 4:5], lam_sc[0:1, 2:3], lam_sc[0:1, 3:4], OP.subtract)
            e.engine_nop()
            return e.tensor_single_scalar(lam_sc[0:1, 5:6], lam_sc[0:1, 4:5],
                                          LAMBDA_INIT, OP.add)
        v_lam = p.emi("vector", lam_fin, "sV")
        p.wait("sync", "sV", v_lam)
        lam_st = p.dma("sync", lam_d[:, :], lam_sc[0:1, 5:6], "sLam")
        p.wait("sync", "sLam", lam_st)
        lam_bc = p.dma("sync", lam_col[:], lam_d[:, :].partition_broadcast(128), "sLam")
        if "d_lam" in dbg_t:
            p.wait("gpsimd", "sLam", lam_bc)
            p.dma("gpsimd", dbg_t["d_lam"][:, :], lam_col[:], "sG")

        ph1 = ExitStack()
        sb1 = lambda n, s, d: ph1.enter_context(nc.sbuf_tensor(n, s, d))
        ps1 = lambda n, s, d: ph1.enter_context(nc.psum_tensor(n, s, d))
        wq_sb = sb1("wq_sb", [128, 16 * 512], BF16)
        wk_sb = sb1("wk_sb", [128, 16 * 512], BF16)
        wv_sb = sb1("wv_sb", [128, 16 * 512], BF16)
        x_db = [sb1(f"x_db{i}", [128, D_], F32) for i in range(2)]
        sq_scr = sb1("sq_scr", [128, D_], F32)
        xnb_db = [sb1(f"xnb{i}", [128, D_], BF16) for i in range(2)]
        xnfm_blk = sb1("xnfm_blk", [128, 16 * 512], BF16)
        st_sb = sb1("st_sb", [128, 16], F32)
        pQ = [ps1(f"pQ{i}", [128, 512], F32) for i in range(2)]
        pTr = [ps1(f"pTr{i}", [128, 512], BF16) for i in range(2)]

        wld = 0
        for (wsb, win) in ((wq_sb, wq_in), (wk_sb, wk_in), (wv_sb, wv_in)):
            for dc in range(16):
                wld = p.dma("gpsimd", wsb[:, 512 * dc:512 * dc + 512],
                            win[128 * dc:128 * dc + 128, :], "sWp")

        NT = S_ // 128
        x_user = [0, 0]
        x_userV = [0, 0]
        xnb_user = [0, 0]
        tr_copy = [0, 0]
        st_user = [0, 0]
        xsem = ["sX0", "sX1"]

        for t in range(NT):
            db = t % 2
            p.wait("sync", "sA", x_user[db])
            p.wait("sync", "sV", x_userV[db])
            xl = p.dma("sync", x_db[db][:], x_in[128 * t:128 * t + 128, :], xsem[db])
            st = st_sb[:, 8 * db:8 * db + 8]
            p.wait("scalar", xsem[db], xl)
            p.wait("scalar", "sV", st_user[db] if not SERIAL_ACT else p.val("sV"))
            a_sq = p.emi("scalar", lambda e, xt=x_db[db], st=st:
                         e.activation(sq_scr[:], xt[:], AF.Square, accum_out=st[:, 1:2]), "sA")
            p.wait("vector", xsem[db], xl)
            def red1(e, xt=x_db[db], st=st):
                e.tensor_reduce(st[:, 0:1], xt[:], AX, OP.add)
                for _ in range(4):
                    e.engine_nop()
                return e.engine_nop()
            p.emi("vector", red1)
            p.wait("vector", "sA", a_sq)
            def ln_stats(e, st=st):
                e.engine_nop()
                e.tensor_single_scalar(st[:, 2:3], st[:, 0:1], 1.0 / D_, OP.mult)
                e.tensor_single_scalar(st[:, 3:4], st[:, 1:2], 1.0 / D_, OP.mult)
                e.tensor_tensor(st[:, 4:5], st[:, 2:3], st[:, 2:3], OP.mult)
                e.engine_nop()
                e.tensor_tensor(st[:, 5:6], st[:, 3:4], st[:, 4:5], OP.subtract)
                e.engine_nop()
                return e.tensor_single_scalar(st[:, 5:6], st[:, 5:6], EPS, OP.add)
            v_st = p.emi("vector", ln_stats, "sV")
            st_user[db] = v_st
            x_userV[db] = v_st
            p.wait("scalar", "sV", max(v_st, vconst))
            p.emi("scalar", lambda e, st=st: e.activation(st[:, 6:7], st[:, 5:6], AF.Ln))
            a_r = p.emi("scalar", lambda e, st=st:
                        e.activation(st[:, 7:8], st[:, 6:7], AF.Exp, scale=consts[:, 3:4]), "sA")
            p.wait("vector", "sA", a_r)
            def mk_mr(e, st=st):
                e.scalar_tensor_tensor(st[:, 4:5], st[:, 2:3], -1.0, st[:, 7:8],
                                       OP.mult, OP.mult)
                return e.tensor_copy(st[:, 6:7], st[:, 7:8])
            v_mr = p.emi("vector", mk_mr, "sV")
            p.wait("scalar", "sV", v_mr)
            p.wait("scalar", "sT", xnb_user[db])
            a_xn = p.emi("scalar", lambda e, xt=x_db[db], st=st, ot=xnb_db[db]:
                         e.activation(ot[:], xt[:], AF.Identity,
                                      bias=st[:, 4:5], scale=st[:, 6:7]), "sA")
            x_user[db] = a_xn

            if "d_xn" in dbg_t:
                p.wait("gpsimd", "sA", a_xn)
                dxn = p.dma("gpsimd", dbg_t["d_xn"][128 * t:128 * t + 128, :],
                            xnb_db[db][:], "sG")
                p.wait("scalar", "sG", dxn)
                if "d_st" in dbg_t and t == 4:
                    p.dma("gpsimd", dbg_t["d_st"][:, :], st_sb[:, :], "sG")
            p.wait("tensor", "sA", a_xn)
            p.wait("tensor", "sIdent", ld_ident)
            tsub = t % 4
            for dc in range(16):
                tb = dc % 2
                p.wait("tensor", "sV", tr_copy[tb] if not SERIAL1 else p.val("sV"))
                t_tr = p.emi("tensor", lambda e, xt=xnb_db[db], dc=dc, pt=pTr[tb]:
                             e.transpose(pt[:, :128], xt[:, 128 * dc:128 * dc + 128],
                                         ident[:]), "sT")
                p.wait("vector", "sT", t_tr)
                tr_copy[tb] = p.emi(
                    "vector", lambda e, dc=dc, tsub=tsub, pt=pTr[tb]:
                    e.tensor_copy(xnfm_blk[:, 512 * dc + 128 * tsub:512 * dc + 128 * tsub + 128],
                                  pt[:, :128]), "sV")
            xnb_user[db] = p.val("sT")

            if t % 4 == 3:
                blk = t // 4
                if SERIAL1:
                    p.wait("tensor", "sV", p.val("sV"))
                    p.wait("tensor", "sA", p.val("sA"))
                if blk == 0 and "d_g" in dbg_t and 1 in phases and 3 not in phases:
                    p.wait("gpsimd", "sV", p.val("sV"))
                    dxf = p.dma("gpsimd", dbg_t["d_g"][0:128, :], xnfm_blk[:], "sG")
                    p.wait("tensor", "sG", dxf)
                p.wait("tensor", "sV", p.val("sV"))
                p.wait("tensor", "sWp", wld)
                pq_guard = [0, 0]
                pi = 0
                for (wsb, dst) in ((wq_sb, qfm), (wk_sb, kfm)):
                    for u in range(NU):
                        cc = 128 * u
                        pqb = pi % 2
                        pi += 1
                        p.wait("tensor", "sV", pq_guard[pqb])
                        if SERIAL1:
                            p.wait("tensor", "sV", p.val("sV"))
                        t_q = None
                        for dc in range(16):
                            t_q = p.emi(
                                "tensor", lambda e, w=wsb, dc=dc, cc=cc, pq=pQ[pqb]:
                                e.matmul(pq[:],
                                         w[:, 512 * dc + cc:512 * dc + cc + 128],
                                         xnfm_blk[:, 512 * dc:512 * dc + 512],
                                         start=(dc == 0), stop=(dc == 15)),
                                "sT" if dc == 15 else None)
                        p.wait("vector", "sT", t_q)
                        pq_guard[pqb] = p.emi(
                            "vector", lambda e, d=dst[u], blk=blk, pq=pQ[pqb]:
                            e.tensor_copy(d[:, 512 * blk:512 * blk + 512],
                                          pq[:]), "sV")
                for tsub2 in range(4):
                    kt = 4 * blk + tsub2
                    pqb = pi % 2
                    pi += 1
                    p.wait("tensor", "sV", pq_guard[pqb])
                    if SERIAL1:
                        p.wait("tensor", "sV", p.val("sV"))
                    t_v = None
                    for dc in range(16):
                        t_v = p.emi(
                            "tensor", lambda e, dc=dc, t2=tsub2, pv=pQ[pqb]:
                            e.matmul(pv[:],
                                     xnfm_blk[:, 512 * dc + 128 * t2:512 * dc + 128 * t2 + 128],
                                     wv_sb[:, 512 * dc:512 * dc + 512],
                                     start=(dc == 0), stop=(dc == 15)),
                            "sT" if dc == 15 else None)
                    p.wait("vector", "sT", t_v)
                    pq_guard[pqb] = p.emi(
                        "vector", lambda e, kt=kt, pv=pQ[pqb]:
                        e.tensor_copy(v_sb[:, 512 * kt:512 * kt + 512], pv[:]), "sV")

        ph1_V = p.val("sV")
        ph1_T = p.val("sT")

        if dbg_t:
            p.wait("gpsimd", "sV", ph1_V)
            p.wait("gpsimd", "sT", ph1_T)
            if "d_qfm" in dbg_t:
                for u in range(NU):
                    p.dma("gpsimd", dbg_t["d_qfm"][128 * u:128 * u + 128, :], qfm[u][:, :], "sG")
            if "d_kfm" in dbg_t:
                for u in range(NU):
                    p.dma("gpsimd", dbg_t["d_kfm"][128 * u:128 * u + 128, :], kfm[u][:, :], "sG")
            if "d_v" in dbg_t:
                for kt in range(32):
                    p.dma("gpsimd", dbg_t["d_v"][128 * kt:128 * kt + 128, :],
                          v_sb[:, 512 * kt:512 * kt + 512], "sG")

        ph1.close()

        do_wmean = (3 in phases and 4 in phases)
        wmean_state = {"col": 0, "m_user": [0, 0]}
        if do_wmean:
            ph3m = ExitStack()
            sb3m = lambda n, sh, d: ph3m.enter_context(
                nc.sbuf_tensor(n, sh, d, side="right"))
            wmp = sb3m("wmp", [128, 128], F32)
            wrow3 = sb3m("wrow3", [1, 8], F32)
            mscr = [sb3m(f"mscr{i}", [128, D_], BF16) for i in range(2)]
            _wm_src = []
            for wi, win, nr in ((0, w1t_in, 16), (1, w2t_in, 64)):
                for i in range(nr):
                    nchunk = 4 if wi == 0 else 1
                    for j in range(nchunk):
                        _wm_src.append(win[128 * i:128 * i + 128, D_ * j:D_ * j + D_])
            assert len(_wm_src) == 128
            _wm_sem = ["sM0", "sM1"]
            _wm_eng = ["gpsimd", "gpsimd"]

            def wmean_step(n):
                for _ in range(n):
                    col = wmean_state["col"]
                    if col >= 128:
                        return
                    db = col % 2
                    p.wait(_wm_eng[db], "sV", wmean_state["m_user"][db])
                    ml = p.dma(_wm_eng[db], mscr[db][:], _wm_src[col], _wm_sem[db])
                    p.wait("vector", _wm_sem[db], ml)
                    wmean_state["m_user"][db] = p.emi(
                        "vector", lambda e, db=db, col=col:
                        e.tensor_reduce(wmp[:, col:col + 1], mscr[db][:], AX,
                                        OP.add, apply_absolute_value=True), "sV")
                    wmean_state["col"] = col + 1
        else:
            def wmean_step(n):
                pass

        cc_v = 0
        at_store = 0
        if 2 in phases:
            ph2 = ExitStack()
            sb2 = lambda n, s, d: ph2.enter_context(nc.sbuf_tensor(n, s, d))
            ps2 = lambda n, s, d: ph2.enter_context(nc.psum_tensor(n, s, d))
            masks_sb = sb2("masks_sb", [128, 4 * 512], F32)
            P1 = [sb2(f"P1_{i}", [128, 512], BF16) for i in range(2)]
            P2 = [sb2(f"P2_{i}", [128, 512], BF16) for i in range(2)]
            a1_sb = sb2("a1_sb", [128, 512], F32)
            a2_sb = sb2("a2_sb", [128, 512], F32)
            lrow = sb2("lrow", [1, 1024], F32)
            linv = sb2("linv", [128, 8], F32)
            comb = sb2("comb", [128, 512], F32)
            rms_scr = sb2("rms_scr", [128, 512], F32)
            rcol = sb2("rcol", [128, 12], F32)
            atn_bf = sb2("atn_bf", [128, 512], BF16)
            atT_bf = sb2("atT_bf", [128, 512], BF16)
            pS1 = [ps2(f"pS1{i}", [128, 512], F32) for i in range(2)]
            pS2 = [ps2(f"pS2{i}", [128, 512], F32) for i in range(2)]
            pA1 = ps2("pA1", [128, 512], F32)
            pA2 = ps2("pA2", [128, 512], F32)
            pL1 = ps2("pL1", [128, 512], F32)
            pL2 = ps2("pL2", [128, 512], F32)
            pTb = pS1[0][:].bitcast(BF16)

            p.wait("sync", "sT", ph1_T)
            p.wait("sync", "sV", ph1_V)
            mld = p.dma("sync", masks_sb[:].rearrange("p (m q) -> p m q", m=4),
                        masks_in.rearrange("(m p) q -> p m q", p=128), "sMsk")

            expA = [0, 0]
            avT = [0, 0]
            prev_store = 0
            for u in range(NU):
                for B in range(8):
                    nk = 4 * B + 4 if NKCAP is None else min(NKCAP, 4 * B + 4)
                    for kt in range(nk):
                        bi = kt % 2
                        p.wait("tensor", "sA", expA[bi])
                        p.wait("tensor", "sV", ph1_V)
                        p.emi("tensor", lambda e, u=u, kt=kt, B=B, pb=pS1[bi]:
                              e.matmul(pb[:], kfm[u][0:64, 128 * kt:128 * kt + 128],
                                       qfm[u][0:64, 512 * B:512 * B + 512],
                                       start=True, stop=True))
                        t_s = p.emi("tensor", lambda e, u=u, kt=kt, B=B, pb=pS2[bi]:
                                    e.matmul(pb[:], kfm[u][64:128, 128 * kt:128 * kt + 128],
                                             qfm[u][64:128, 512 * B:512 * B + 512],
                                             start=True, stop=True), "sT")
                        need_mask = kt >= 4 * B
                        s_mask = 0
                        if need_mask:
                            pv = kt - 4 * B
                            p.wait("vector", "sT", t_s)
                            p.wait("vector", "sMsk", mld)
                            p.emi("vector", lambda e, pb=pS1[bi], pv=pv:
                                  e.tensor_tensor(pb[:], pb[:],
                                                  masks_sb[:, 512 * pv:512 * pv + 512], OP.add))
                            s_mask = p.emi(
                                "vector", lambda e, pb=pS2[bi], pv=pv:
                                e.tensor_tensor(pb[:], pb[:],
                                                masks_sb[:, 512 * pv:512 * pv + 512], OP.add),
                                "sV")
                        p.wait("scalar", "sT", t_s)
                        if need_mask:
                            p.wait("scalar", "sV", s_mask)
                        p.wait("scalar", "sT", avT[bi])
                        p.emi("scalar", lambda e, pb=pS1[bi], o=P1[bi]:
                              e.activation(o[:], pb[:], AF.Exp,
                                           bias=consts[:, 1:2], scale=consts[:, 0:1]))
                        expA[bi] = p.emi("scalar", lambda e, pb=pS2[bi], o=P2[bi]:
                                         e.activation(o[:], pb[:], AF.Exp,
                                                      bias=consts[:, 1:2],
                                                      scale=consts[:, 0:1]), "sA")
                        if u == 0 and B == 0 and kt == 3 and "d_p1" in dbg_t and "d_p2" in dbg_t:
                            p.wait("gpsimd", "sA", expA[bi])
                            p.dma("gpsimd", dbg_t["d_p1"][:, :], P1[bi][:], "sG")
                            dmp = p.dma("gpsimd", dbg_t["d_p2"][:, :], P2[bi][:], "sG")
                            p.wait("tensor", "sG", dmp)
                        if u == 0 and B == 0 and kt == 0 and "d_p1k0" in dbg_t:
                            p.wait("gpsimd", "sA", expA[bi])
                            dmp0b = p.dma("gpsimd", dbg_t["d_p1k0"][:, :], P1[bi][:], "sG")
                            p.wait("tensor", "sG", dmp0b)
                        p.wait("tensor", "sA", expA[bi])
                        fl = dict(start=(kt == 0), stop=(kt == nk - 1))
                        p.emi("tensor", lambda e, u=u, kt=kt, P=P1[bi], f=fl:
                              e.matmul(pA1[:],
                                       v_sb[:, 512 * kt + 128 * u:512 * kt + 128 * u + 128],
                                       P[:], **f))
                        p.emi("tensor", lambda e, u=u, kt=kt, P=P2[bi], f=fl:
                              e.matmul(pA2[:],
                                       v_sb[:, 512 * kt + 128 * u:512 * kt + 128 * u + 128],
                                       P[:], **f))
                        p.emi("tensor", lambda e, P=P1[bi], f=fl:
                              e.matmul(pL1[0:1, 0:512], ones_b[:], P[:], **f))
                        avT[bi] = p.emi("tensor", lambda e, P=P2[bi], f=fl:
                                        e.matmul(pL2[0:1, 0:512], ones_b[:], P[:], **f), "sT")
                    t_all = avT[(nk - 1) % 2]
                    p.wait("vector", "sT", t_all)
                    p.emi("vector", lambda e: e.tensor_copy(lrow[0:1, 0:512], pL1[0:1, 0:512]))
                    p.emi("vector", lambda e: e.tensor_copy(lrow[0:1, 512:1024], pL2[0:1, 0:512]))
                    p.emi("vector", lambda e: e.tensor_copy(a1_sb[:], pA1[:]))
                    evac_V = p.emi("vector", lambda e: e.tensor_copy(a2_sb[:], pA2[:]), "sV")
                    p.wait("tensor", "sV", evac_V)
                    for j in range(4):
                        p.emi("tensor", lambda e, j=j:
                              e.matmul(pL1[:, 8 + j:9 + j], lrow[0:1, 128 * j:128 * j + 128],
                                       ones_f[0:1, 0:1], start=True, stop=True))
                    t_lt = None
                    for j in range(4):
                        t_lt = p.emi("tensor", lambda e, j=j:
                                     e.matmul(pL2[:, 8 + j:9 + j],
                                              lrow[0:1, 512 + 128 * j:640 + 128 * j],
                                              ones_f[0:1, 0:1], start=True, stop=True),
                                     "sT" if j == 3 else None)
                    for j in range(4):
                        p.emi("tensor", lambda e, j=j:
                              e.transpose(pA1[:, 128 * j:128 * j + 128],
                                          a1_sb[:, 128 * j:128 * j + 128], identf[:]))
                    t_tr = None
                    for j in range(4):
                        t_tr = p.emi("tensor", lambda e, j=j:
                                     e.transpose(pA2[:, 128 * j:128 * j + 128],
                                                 a2_sb[:, 128 * j:128 * j + 128], identf[:]),
                                     "sT" if j == 3 else None)
                    p.wait("vector", "sT", t_lt)
                    p.wait("vector", "sLam", lam_bc)
                    p.emi("vector", lambda e: e.reciprocal(linv[:, 0:4], pL1[:, 8:12]))
                    p.emi("vector", lambda e: e.reciprocal(linv[:, 4:8], pL2[:, 8:12]))
                    p.emi("vector", lambda e: e.engine_nop())
                    v_li = p.emi("vector", lambda e:
                                 e.tensor_scalar(linv[:, 4:8], linv[:, 4:8],
                                                 lam_col[:, 0:1], None, OP.mult), "sV")
                    p.emi("vector", lambda e: e.engine_nop())
                    if u == 0 and B == 0 and "d_l" in dbg_t:
                        p.wait("gpsimd", "sV", evac_V)
                        dmp2 = p.dma("gpsimd", dbg_t["d_l"][:, :], lrow[0:1, :], "sG")
                        if "d_a1" in dbg_t and "d_a2" in dbg_t:
                            p.dma("gpsimd", dbg_t["d_a1"][:, :], a1_sb[:], "sG")
                            dmp2 = p.dma("gpsimd", dbg_t["d_a2"][:, :], a2_sb[:], "sG")
                        p.wait("vector", "sG", dmp2)
                    p.wait("vector", "sT", t_tr)
                    a_sq2 = 0
                    for j in range(4):
                        p.wait("vector", "sA", a_sq2)
                        p.emi("vector", lambda e, j=j:
                              e.tensor_scalar(comb[:, 128 * j:128 * j + 128],
                                              pA1[:, 128 * j:128 * j + 128],
                                              linv[:, j:j + 1], None, OP.mult))
                        v_cb = p.emi("vector", lambda e, j=j:
                                     e.scalar_tensor_tensor(comb[:, 128 * j:128 * j + 128],
                                                            pA2[:, 128 * j:128 * j + 128],
                                                            linv[:, 4 + j:5 + j],
                                                            comb[:, 128 * j:128 * j + 128],
                                                            OP.mult, OP.subtract), "sV")
                        p.wait("scalar", "sV", v_cb)
                        a_sq2 = p.emi("scalar", lambda e, j=j:
                                      e.activation(rms_scr[:, 128 * j:128 * j + 128],
                                                   comb[:, 128 * j:128 * j + 128],
                                                   AF.Square, accum_out=rcol[:, j:j + 1]), "sA")
                    if u == 0 and B == 0 and "d_comb" in dbg_t and "d_linv" in dbg_t:
                        p.wait("gpsimd", "sA", a_sq2)
                        p.dma("gpsimd", dbg_t["d_comb"][:, :], comb[:], "sG")
                        dmp3 = p.dma("gpsimd", dbg_t["d_linv"][:, :], linv[:], "sG")
                        p.wait("vector", "sG", dmp3)
                    p.wait("vector", "sA", a_sq2)
                    v_rs = p.emi("vector", lambda e:
                                 e.tensor_scalar(rcol[:, 0:4], rcol[:, 0:4], 1.0 / 128.0,
                                                 EPS, OP.mult, OP.add), "sV")
                    p.wait("scalar", "sV", v_rs)
                    p.emi("scalar", lambda e: e.activation(rcol[:, 4:8], rcol[:, 0:4], AF.Ln))
                    a_rr = p.emi("scalar", lambda e:
                                 e.activation(rcol[:, 8:12], rcol[:, 4:8], AF.Exp,
                                              scale=consts[:, 3:4]), "sA")
                    p.wait("vector", "sA", a_rr)
                    p.emi("vector", lambda e:
                          e.tensor_single_scalar(rcol[:, 8:12], rcol[:, 8:12],
                                                 LAMBDA_INIT - 1.0, OP.mult))
                    p.emi("vector", lambda e: e.engine_nop())
                    if u == 0 and B == 0 and "d_rc" in dbg_t:
                        p.wait("gpsimd", "sV", p.val("sV"))
                        dmp4 = p.dma("gpsimd", dbg_t["d_rc"][:, :], rcol[:], "sG")
                        p.wait("vector", "sG", dmp4)
                    p.wait("vector", "sAT", prev_store)
                    v_at = 0
                    for j in range(4):
                        v_at = p.emi("vector", lambda e, j=j:
                                     e.tensor_scalar(atn_bf[:, 128 * j:128 * j + 128],
                                                     comb[:, 128 * j:128 * j + 128],
                                                     rcol[:, 8 + j:9 + j], None, OP.mult), "sV")
                    p.wait("tensor", "sV", v_at)
                    t_bt = None
                    for j in range(4):
                        t_bt = p.emi("tensor", lambda e, j=j:
                                     e.transpose(pTb[:, 128 * j:128 * j + 128],
                                                 atn_bf[:, 128 * j:128 * j + 128], ident[:]),
                                     "sT" if j == 3 else None)
                    p.wait("vector", "sT", t_bt)
                    v_att = p.emi("vector", lambda e: e.tensor_copy(atT_bf[:], pTb[:, 0:512]), "sV")
                    p.wait("sync", "sV", v_att)
                    prev_store = p.dma("sync",
                                       ag_in_t[128 * u:128 * u + 128, 512 * B:512 * B + 512],
                                       atT_bf[:], "sAT")
                    p.wait("tensor", "sV", v_li)
                    p.wait("tensor", "sV", v_at)
                    p.wait("tensor", "sV", v_att)
                    wmean_step(4)
            at_store = prev_store

            if "d_attn" in dbg_t:
                p.wait("gpsimd", "sAT", at_store)
                abscr = sb2("abscr", [128, S_], BF16)
                for u in range(NU):
                    lb = p.dma("gpsimd", abscr[:], ag_in_t[128 * u:128 * u + 128, :], "sG")
                    p.wait("gpsimd", "sG", lb)
                    sv = p.dma("gpsimd", dbg_t["d_attn"][128 * u:128 * u + 128, :],
                               abscr[:], "sG")
                    p.wait("gpsimd", "sG", sv)

            if collective:
                p.wait("gpsimd", "sAT", at_store)
                cc_v = p.emi("gpsimd", lambda e:
                             e.collective_compute("AllGather", OP.bypass,
                                                  replica_groups=[[0, 1, 2, 3], [4, 5, 6, 7]],
                                                  ins=[ag_in_t.opt()],
                                                  outs=[ag_out_t.opt()]), "sCC")
            if "d_ag" in dbg_t:
                p.wait("gpsimd", "sCC", cc_v)
                agscr = sb2("agscr", [128, S_], BF16)
                for r in range(16):
                    lb = p.dma("gpsimd", agscr[:], ag_out_t[128 * r:128 * r + 128, :], "sG")
                    p.wait("gpsimd", "sG", lb)
                    sv = p.dma("gpsimd", dbg_t["d_ag"][128 * r:128 * r + 128, :],
                               agscr[:], "sG")
                    p.wait("gpsimd", "sG", sv)
            ph2.close()

        ph2_T = p.val("sT")
        ph2_V = p.val("sV")
        ph2_A = p.val("sA")

        ph12.close()

        if 3 in phases and 4 in phases:
            ph3p = ExitStack()
            sb3p = lambda n, sh, d: ph3p.enter_context(nc.sbuf_tensor(n, sh, d))
            alsc = sb3p("alsc", [128, 48], F32)
            gsums = sb3p("gsums", [128, 8 * 16], F32)
            gssqs = sb3p("gssqs", [128, 8 * 16], F32)

            ph3g = ExitStack()
            sb3g = lambda n, sh, d: ph3g.enter_context(nc.sbuf_tensor(n, sh, d))
            g_sb = sb3g("g_sb", [128, 8 * F_], BF16)

            for qe in ("sync", "gpsimd"):
                p.wait(qe, "sT", ph2_T)
                p.wait(qe, "sV", ph2_V)
                p.wait(qe, "sA", ph2_A)

            wmean_step(128 - wmean_state["col"])
            m_user = wmean_state["m_user"]
            mean_V = max(m_user)
            ph3ab = ExitStack()
            sb3a = lambda n, sh, d: ph3ab.enter_context(nc.sbuf_tensor(n, sh, d))
            ps3a = lambda n, sh, d: ph3ab.enter_context(nc.psum_tensor(n, sh, d))
            pH = [ps3a(f"pH{i}", [128, 512], F32) for i in range(8)]
            xqfm = sb3a("xqfm", [128, 16 * R_], BF16)
            ph3a = ExitStack()
            sb3aa = lambda n, sh, d: ph3a.enter_context(nc.sbuf_tensor(n, sh, d))
            pXT = pH[0].bitcast(BF16)
            xr_db = sb3aa("xr_db", [128, D_], F32)
            st3 = sb3aa("st3", [128, 16], F32)
            xnf = sb3aa("xnf", [128, D_], F32)
            isc = sb3aa("isc", [128, D_], I32)
            xqt = sb3aa("xqt", [128, D_], BF16)
            vsc = sb3aa("vsc", [128, D_], F32)
            xqt_v = xqt[:]


            def wm_fin(e):
                e.tensor_reduce(alsc[:, 40:41], wmp[:, 0:64], AX, OP.add)
                return e.tensor_reduce(alsc[:, 41:42], wmp[:, 64:128], AX, OP.add)
            v_wm = p.emi("vector", wm_fin, "sV")
            p.wait("tensor", "sV", v_wm)
            p.wait("tensor", "sIdent", ld_ident)
            t_wm = p.emi("tensor", lambda e:
                         e.matmul(pH[0][0:1, 0:2], ones_f[:], alsc[:, 40:42],
                                  start=True, stop=True), "sT")
            p.wait("vector", "sT", t_wm)
            def wrow_math(e):
                e.tensor_copy(wrow3[0:1, 0:2], pH[0][0:1, 0:2])
                e.engine_nop()
                e.tensor_single_scalar(wrow3[0:1, 0:2], wrow3[0:1, 0:2],
                                       1.0 / (D_ * F_), OP.mult)
                e.engine_nop()
                e.tensor_single_scalar(wrow3[0:1, 0:2], wrow3[0:1, 0:2], EPS, OP.max)
                e.engine_nop()
                e.tensor_single_scalar(wrow3[0:1, 2:4], wrow3[0:1, 0:2], 0.5, OP.mult)
                e.engine_nop()
                e.tensor_single_scalar(wrow3[0:1, 4:6], wrow3[0:1, 2:4], -1.0, OP.mult)
                e.engine_nop()
                return e.tensor_single_scalar(wrow3[0:1, 6:8], wrow3[0:1, 0:2],
                                              1.0 / 127.0, OP.mult)
            v_wr = p.emi("vector", wrow_math, "sV")
            p.wait("sync", "sV", v_wr)
            ws_st = p.dma("sync", wstat_d[:, :], wrow3[0:1, :], "sLam")
            p.wait("sync", "sLam", ws_st)
            ws_bc = p.dma("sync", wstat[:, :], wstat_d[:, :].partition_broadcast(128), "sLam")

            p.wait("sync", "sV", mean_V)
            x3_user = 0
            st3_user = 0
            xq_tr_ev = 0
            for tt in range(8):
                p.wait("sync", "sV", x3_user)
                xl = p.dma("sync", xr_db[:], xr_in[128 * tt:128 * tt + 128, :], "sX0")
                st = st3[:, 0:8]
                p.wait("scalar", "sX0", xl)
                p.wait("scalar", "sV", st3_user if not SERIAL_ACT else p.val("sV"))
                a_sq = p.emi("scalar", lambda e:
                             e.activation(xnf[:], xr_db[:], AF.Square,
                                          accum_out=st3[:, 1:2]), "sA")
                p.wait("vector", "sX0", xl)
                p.emi("vector", lambda e: e.tensor_reduce(st3[:, 0:1], xr_db[:], AX, OP.add))
                p.wait("vector", "sA", a_sq)
                def ln3(e):
                    e.engine_nop()
                    e.tensor_single_scalar(st3[:, 2:3], st3[:, 0:1], 1.0 / D_, OP.mult)
                    e.tensor_single_scalar(st3[:, 3:4], st3[:, 1:2], 1.0 / D_, OP.mult)
                    e.tensor_tensor(st3[:, 4:5], st3[:, 2:3], st3[:, 2:3], OP.mult)
                    e.engine_nop()
                    e.tensor_tensor(st3[:, 5:6], st3[:, 3:4], st3[:, 4:5], OP.subtract)
                    e.engine_nop()
                    return e.tensor_single_scalar(st3[:, 5:6], st3[:, 5:6], EPS, OP.add)
                v_st = p.emi("vector", ln3, "sV")
                st3_user = v_st
                p.wait("scalar", "sV", v_st)
                p.emi("scalar", lambda e: e.activation(st3[:, 6:7], st3[:, 5:6], AF.Ln))
                a_r = p.emi("scalar", lambda e:
                            e.activation(st3[:, 7:8], st3[:, 6:7], AF.Exp,
                                         scale=consts[:, 3:4]), "sA")
                p.wait("vector", "sA", a_r)
                def mr3(e):
                    e.scalar_tensor_tensor(st3[:, 4:5], st3[:, 2:3], -1.0, st3[:, 7:8],
                                           OP.mult, OP.mult)
                    return e.tensor_copy(st3[:, 6:7], st3[:, 7:8])
                v_mr = p.emi("vector", mr3, "sV")
                p.wait("scalar", "sV", v_mr)
                a_xn = p.emi("scalar", lambda e:
                             e.activation(xnf[:], xr_db[:], AF.Identity,
                                          bias=st3[:, 4:5], scale=st3[:, 6:7]), "sA")
                p.wait("vector", "sA", a_xn)
                def qsc(e, tt=tt):
                    e.tensor_reduce(st3[:, 8:9], xnf[:], AX, OP.max,
                                    apply_absolute_value=True)
                    for _ in range(4):
                        e.engine_nop()
                    e.tensor_single_scalar(alph[:, tt:tt + 1], st3[:, 8:9], EPS, OP.max)
                    for _ in range(4):
                        e.engine_nop()
                    e.reciprocal(st3[:, 9:10], alph[:, tt:tt + 1])
                    for _ in range(4):
                        e.engine_nop()
                    return e.tensor_single_scalar(alsc[:, tt:tt + 1], st3[:, 9:10],
                                                  127.0, OP.mult)
                v_qs = p.emi("vector", qsc, "sV")
                def qint(e, tt=tt):
                    for _ in range(4):
                        e.engine_nop()
                    e.tensor_scalar(vsc[:], xnf[:], alsc[:, tt:tt + 1], None,
                                    OP.mult)
                    e.tensor_copy(isc[:], vsc[:])
                    return e.tensor_copy(xqt_v[:, 0:D_], isc[:])
                v_qi = p.emi("vector", qint, "sV")
                x3_user = v_qi
                if tt == 0 and "d_ffn" in dbg_t and "d_xq" in dbg_t:
                    p.wait("gpsimd", "sV", v_qi)
                    p.dma("gpsimd", dbg_t["d_ffn"][0:128, :], vsc[:], "sG")
                    dqq = p.dma("gpsimd", dbg_t["d_ffn"][128:256, :], xqt[:], "sG")
                    p.wait("tensor", "sG", dqq)
                    p.wait("vector", "sG", dqq)
                p.wait("tensor", "sV", v_qi)
                for dc in range(16):
                    t_tr = p.emi("tensor", lambda e, dc=dc:
                                 e.transpose(pXT[:, 0:128],
                                             xqt_v[:, 128 * dc:128 * dc + 128],
                                             ident[:]), "sT")
                    p.wait("vector", "sT", t_tr)
                    xq_tr_ev = p.emi(
                        "vector", lambda e, dc=dc, tt=tt:
                        e.tensor_copy(xqfm[:, R_ * dc + 128 * tt:R_ * dc + 128 * tt + 128],
                                      pXT[:, 0:128]), "sV")
                    p.wait("tensor", "sV", xq_tr_ev)
            if "d_xq" in dbg_t:
                p.wait("gpsimd", "sV", p.val("sV"))
                p.dma("gpsimd", dbg_t["d_xq"][0:128, :], xqfm[:, 0:D_], "sG")
                dxq_v = p.dma("gpsimd", dbg_t["d_xq"][128:256, :], xqfm[:, D_:2 * D_], "sG")
                p.wait("vector", "sG", dxq_v)
            p.wait("vector", "sLam", ws_bc)
            v_a1c = p.emi("vector", lambda e:
                          e.tensor_scalar(alsc[:, 8:16], alph[:, 0:8],
                                          wstat[:, 6:7], None, OP.mult), "sV")
            ph3a_A = p.val("sA")
            ph3a_V = p.val("sV")
            ph3a.close()
            ph3b = ExitStack()
            sb3b = lambda n, sh, d: ph3b.enter_context(nc.sbuf_tensor(n, sh, d))
            w1tile = [sb3b(f"w1tile{i}", [128, 512], F32) for i in range(4)]
            tqs = sb3b("tqs", [128, 512], BF16)
            sqb = sb3b("sqb", [128, 512], BF16)
            w1q = [sb3b(f"w1q{i}", [128, 512], BF16) for i in range(4)]
            b1b = [sb3b(f"b1b{i}", [128, 512], F32) for i in range(2)]
            p.wait("sync", "sA", ph3a_A)
            p.wait("sync", "sV", ph3a_V)
            p.wait("gpsimd", "sA", ph3a_A)
            p.wait("gpsimd", "sV", ph3a_V)

            wsem = ["sW10", "sW11", "sW12", "sW13"]
            wqeng = ["sync", "sync", "gpsimd", "gpsimd"]
            w1t_user = [0, 0, 0, 0]
            w1q_user = [0, 0, 0, 0]
            b1_user = [0, 0]
            pH_gelu = [0] * 8
            for fs in range(16):
                sp = fs % 2
                p.wait("sync", "sV", b1_user[sp])
                bl = p.dma("sync", b1b[sp][:],
                           b1_in[0:1, 512 * fs:512 * fs + 512].partition_broadcast(128),
                           "sB0")
                t_mm = None
                for dc in range(16):
                    db = dc % 4
                    p.wait(wqeng[db], "sV", w1t_user[db])
                    wl = p.dma(wqeng[db], w1tile[db][:],
                               w1t_in[128 * dc:128 * dc + 128, 512 * fs:512 * fs + 512],
                               wsem[db])
                    p.wait("vector", wsem[db], wl)
                    p.wait("vector", "sT", w1q_user[db])
                    p.wait("vector", "sLam", ws_bc)
                    def wq1(e, db=db):
                        e.tensor_scalar(tqs[:], w1tile[db][:], wstat[:, 4:5], None,
                                        OP.is_lt)
                        return e.scalar_tensor_tensor(w1q[db][:], w1tile[db][:],
                                                      wstat[:, 2:3], tqs[:],
                                                      OP.is_gt, OP.subtract)
                    q_v = p.emi("vector", wq1, "sV")
                    w1t_user[db] = q_v
                    p.wait("tensor", "sV", q_v)
                    if fs == 0 and dc == 0:
                        p.wait("tensor", "sV", xq_tr_ev)
                    for tt in range(8):
                        if dc == 0:
                            p.wait("tensor", "sA", pH_gelu[tt])
                        t_mm = p.emi(
                            "tensor", lambda e, dc=dc, tt=tt, db=db:
                            e.matmul(pH[tt][:],
                                     xqfm[:, R_ * dc + 128 * tt:R_ * dc + 128 * tt + 128],
                                     w1q[db][:], start=(dc == 0), stop=(dc == 15)),
                            "sT" if tt == 7 else None)
                    w1q_user[db] = t_mm
                if fs == 0 and "d_alpha" in dbg_t:
                    p.wait("gpsimd", "sV", w1t_user[0])
                    p.dma("gpsimd", dbg_t["d_alpha"][:, :], alsc[:, 0:8], "sG")
                p.wait("vector", "sT", t_mm)
                p.wait("vector", "sV", v_a1c)
                p.wait("vector", "sB0", bl)
                for tt in range(8):
                    v_ev = p.emi("vector", lambda e, tt=tt, sp=sp:
                                 e.scalar_tensor_tensor(pH[tt][:], pH[tt][:],
                                                        alsc[:, 8 + tt:9 + tt],
                                                        b1b[sp][:],
                                                        OP.mult, OP.add), "sV")
                    b1_user[sp] = v_ev
                    p.wait("scalar", "sV", v_ev)
                    pH_gelu[tt] = p.emi(
                        "scalar", lambda e, tt=tt, fs=fs:
                        e.activation(g_sb[:, F_ * tt + 512 * fs:F_ * tt + 512 * fs + 512],
                                     pH[tt][:], AF.Gelu,
                                     accum_out=gsums[:, 16 * tt + fs:16 * tt + fs + 1]),
                        "sA")
                    p.emi("scalar", lambda e, tt=tt, fs=fs:
                          e.activation(sqb[:],
                                       g_sb[:, F_ * tt + 512 * fs:F_ * tt + 512 * fs + 512],
                                       AF.Square,
                                       accum_out=gssqs[:, 16 * tt + fs:16 * tt + fs + 1]),
                          "sA")

            ph3b_A = p.val("sA")
            ph3b_V = p.val("sV")
            ph3b_T = p.val("sT")
            if "d_hq" in dbg_t:
                gdump = sb3b("gdump", [128, 512], F32)
                dg_v = 0
                for tt in range(8):
                    for hh in range(F_ // 512):
                        p.wait("vector", "sA", ph3b_A)
                        cv = p.emi("vector", lambda e, tt=tt, hh=hh: e.tensor_copy(
                            gdump[:], g_sb[:, F_ * tt + 512 * hh:F_ * tt + 512 * hh + 512]),
                            "sV")
                        p.wait("gpsimd", "sV", cv)
                        dg_v = p.dma("gpsimd",
                                     dbg_t["d_hq"][128 * tt:128 * tt + 128,
                                                   512 * hh:512 * hh + 512],
                                     gdump[:], "sG")
                        p.wait("vector", "sG", dg_v)
            ph3b.close()
            ph3ab.close()

            ph3c = ExitStack()
            sb3c = lambda n, sh, d: ph3c.enter_context(nc.sbuf_tensor(n, sh, d))
            ps3c = lambda n, sh, d: ph3c.enter_context(nc.psum_tensor(n, sh, d))
            gamb = [sb3c(f"gamb{i}", [128, 512], F32) for i in range(2)]
            betb = [sb3c(f"betb{i}", [128, 512], F32) for i in range(2)]
            vsc3 = sb3c("vsc3", [128, F_ // 2], F32)
            isc3 = sb3c("isc3", [128, F_ // 2], I32)
            hqt_scr = sb3c("hqt_scr", [128, F_ // 2], F32)
            st3c = sb3c("st3c", [128, 16], F32)
            pXT2 = [ps3c(f"pXT2_{i}", [128, 512], F32) for i in range(2)]

            p.wait("sync", "sT", ph3b_T)
            p.wait("sync", "sV", ph3b_V)
            p.wait("sync", "sA", ph3b_A)
            p.wait("vector", "sA", ph3b_A)
            t1_v = 0
            for tt in range(8):
                def lnstats(e, tt=tt):
                    e.tensor_reduce(st3c[:, 0:1], gsums[:, 16 * tt:16 * tt + 16], AX, OP.add)
                    e.tensor_reduce(st3c[:, 1:2], gssqs[:, 16 * tt:16 * tt + 16], AX, OP.add)
                    for _ in range(4):
                        e.engine_nop()
                    e.tensor_single_scalar(st3c[:, 2:3], st3c[:, 0:1], 1.0 / F_, OP.mult)
                    e.tensor_single_scalar(st3c[:, 3:4], st3c[:, 1:2], 1.0 / F_, OP.mult)
                    e.tensor_tensor(st3c[:, 4:5], st3c[:, 2:3], st3c[:, 2:3], OP.mult)
                    e.engine_nop()
                    e.tensor_tensor(st3c[:, 5:6], st3c[:, 3:4], st3c[:, 4:5], OP.subtract)
                    e.engine_nop()
                    return e.tensor_single_scalar(st3c[:, 5:6], st3c[:, 5:6], EPS, OP.add)
                v_s = p.emi("vector", lnstats, "sV")
                p.wait("scalar", "sV", v_s)
                p.emi("scalar", lambda e: e.activation(st3c[:, 6:7], st3c[:, 5:6], AF.Ln))
                a_r = p.emi("scalar", lambda e:
                            e.activation(st3c[:, 7:8], st3c[:, 6:7], AF.Exp,
                                         scale=consts[:, 3:4]), "sA")
                p.wait("vector", "sA", a_r)
                t1_v = p.emi("vector", lambda e, tt=tt:
                             e.tensor_scalar(g_sb[:, F_ * tt:F_ * tt + F_],
                                             g_sb[:, F_ * tt:F_ * tt + F_],
                                             st3c[:, 2:3], st3c[:, 7:8],
                                             OP.subtract, OP.mult), "sV")
            gb_user = [0, 0]
            gb_v = 0
            for fs in range(16):
                gp = fs % 2
                p.wait("sync", "sV", gb_user[gp])
                gl = p.dma("sync", gamb[gp][:],
                           gam_in[0:1, 512 * fs:512 * fs + 512].partition_broadcast(128),
                           "sB0")
                gl2 = p.dma("sync", betb[gp][:],
                            bet_in[0:1, 512 * fs:512 * fs + 512].partition_broadcast(128),
                            "sB1")
                p.wait("vector", "sB0", gl)
                p.wait("vector", "sB1", gl2)
                p.wait("vector", "sV", t1_v)
                for tt in range(8):
                    sl = slice(F_ * tt + 512 * fs, F_ * tt + 512 * fs + 512)
                    p.emi("vector", lambda e, sl=sl, gp=gp:
                          e.tensor_tensor(g_sb[:, sl], g_sb[:, sl], gamb[gp][:], OP.mult))
                    gb_v = p.emi("vector", lambda e, sl=sl, gp=gp:
                                 e.tensor_tensor(g_sb[:, sl], g_sb[:, sl], betb[gp][:],
                                                 OP.add), "sV")
                gb_user[gp] = gb_v
            if "d_3c" in dbg_t:
                gd3 = sb3c("gd3", [128, 512], F32)
            cp_ev = [0] * 2
            tr_last = 0
            v_q2b = 0
            for tt in range(8):
                p.wait("vector", "sV", gb_v)
                if tt == 0 and "d_3c" in dbg_t:
                    dga = 0
                    for cc in range(16):
                        p.wait("vector", "sG", dga)
                        cva = p.emi("vector", lambda e, cc=cc: e.tensor_copy(
                            gd3[:], g_sb[:, 512 * cc:512 * cc + 512]), "sV")
                        p.wait("gpsimd", "sV", cva)
                        dga = p.dma("gpsimd",
                                    dbg_t["d_3c"][0:128, 512 * cc:512 * cc + 512],
                                    gd3[:], "sG")
                    p.wait("vector", "sG", dga)
                def qsc2(e, tt=tt):
                    e.tensor_reduce(st3c[:, 8:9], g_sb[:, F_ * tt:F_ * tt + F_], AX,
                                    OP.max, apply_absolute_value=True)
                    for _ in range(4):
                        e.engine_nop()
                    e.tensor_single_scalar(alsc[:, 16 + tt:17 + tt], st3c[:, 8:9],
                                           EPS, OP.max)
                    for _ in range(4):
                        e.engine_nop()
                    e.reciprocal(st3c[:, 9:10], alsc[:, 16 + tt:17 + tt])
                    for _ in range(4):
                        e.engine_nop()
                    e.tensor_single_scalar(alsc[:, 24 + tt:25 + tt], st3c[:, 9:10],
                                           127.0, OP.mult)
                    for _ in range(4):
                        e.engine_nop()
                    return e.tensor_scalar(alsc[:, 32 + tt:33 + tt],
                                           alsc[:, 16 + tt:17 + tt],
                                           wstat[:, 7:8], None, OP.mult)
                v_q2 = p.emi("vector", qsc2, "sV")
                for hh in range(2):
                    hof = (F_ // 2) * hh
                    p.wait("scalar", "sV", max(v_q2, v_q2b))
                    a_v2 = p.emi("scalar", lambda e, tt=tt, hof=hof:
                                 e.activation(vsc3[:],
                                              g_sb[:, F_ * tt + hof:F_ * tt + hof + F_ // 2],
                                              AF.Identity,
                                              scale=alsc[:, 24 + tt:25 + tt]), "sA")
                    p.wait("vector", "sA", a_v2)
                    def qint2(e):
                        e.tensor_copy(isc3[:], vsc3[:])
                        return e.tensor_copy(hqt_scr[:], isc3[:])
                    p.wait("vector", "sT", tr_last)
                    v_q2b = p.emi("vector", qint2, "sV")
                    p.wait("tensor", "sV", v_q2b)
                    for fc2 in range(32):
                        fc = 32 * hh + fc2
                        slt = fc2 % 2
                        bk, off = slt, 0
                        p.wait("tensor", "sA", cp_ev[slt])
                        t_tr = p.emi("tensor", lambda e, fc2=fc2, bk=bk, off=off:
                                     e.transpose(pXT2[bk][:, off:off + 128],
                                                 hqt_scr[:, 128 * fc2:128 * fc2 + 128],
                                                 identf[:]), "sT")
                        p.wait("scalar", "sT", t_tr)
                        cp_ev[slt] = p.emi(
                            "scalar", lambda e, fc=fc, tt=tt, bk=bk, off=off:
                            e.activation(
                                g_sb[:, F_ * tt + 128 * fc:F_ * tt + 128 * fc + 128],
                                pXT2[bk][:, off:off + 128], AF.Identity), "sA")
                    tr_last = t_tr
                    if tt == 0 and hh == 0 and "d_3c" in dbg_t:
                        p.wait("gpsimd", "sV", v_q2b)
                        dqh = p.dma("gpsimd", dbg_t["d_3c"][128:256, 0:F_ // 2],
                                    hqt_scr[:], "sG")
                        p.wait("vector", "sG", dqh)
                        p.wait("tensor", "sG", dqh)

            hq_done = max(cp_ev)
            ph3c_V = p.val("sV")
            ph3c_T = p.val("sT")
            ph3c_A = p.val("sA")
            ph3c.close()

            ph3d = ExitStack()
            sb3d = lambda n, sh, d: ph3d.enter_context(nc.sbuf_tensor(n, sh, d))
            ps3d = lambda n, sh, d: ph3d.enter_context(nc.psum_tensor(n, sh, d))
            w2tile = [sb3d(f"w2tile{i}", [128, 512], F32) for i in range(4)]
            fbuf = [sb3d(f"fbuf{i}", [128, 512], BF16) for i in range(2)]
            tqs2 = sb3d("tqs2", [128, 512], BF16)
            w2q = [sb3d(f"w2q{i}", [128, 512], BF16) for i in range(4)]
            b2b = [sb3d(f"b2b{i}", [128, 512], F32) for i in range(2)]
            pM2 = [ps3d(f"pM2_{i}", [128, 512], F32) for i in range(8)]

            p.wait("sync", "sT", ph3c_T)
            p.wait("sync", "sV", ph3c_V)
            p.wait("sync", "sA", ph3c_A)
            p.wait("gpsimd", "sT", ph3c_T)
            p.wait("gpsimd", "sV", ph3c_V)
            p.wait("gpsimd", "sA", ph3c_A)
            w2sem = ["sW20", "sW21", "sW22", "sW23"]
            w2eng = ["sync", "sync", "gpsimd", "gpsimd"]
            w2t_user = [0, 0, 0, 0]
            w2q_user = [0, 0, 0, 0]
            b2_user = [0, 0]
            pM_user = 0
            for ds in range(4):
                sp = ds % 2
                p.wait("sync", "sV", b2_user[sp])
                b2l = p.dma("sync", b2b[sp][:],
                            b2_in[0:1, 512 * ds:512 * ds + 512].partition_broadcast(128),
                            "sB0")
                t_mm2 = None
                for fc in range(64):
                    db = fc % 4
                    p.wait(w2eng[db], "sV", w2t_user[db])
                    wl = p.dma(w2eng[db], w2tile[db][:],
                               w2t_in[128 * fc:128 * fc + 128, 512 * ds:512 * ds + 512],
                               w2sem[db])
                    p.wait("vector", w2sem[db], wl)
                    p.wait("vector", "sT", w2q_user[db])
                    def wq2(e, db=db):
                        e.tensor_scalar(tqs2[:], w2tile[db][:], wstat[:, 5:6], None,
                                        OP.is_lt)
                        return e.scalar_tensor_tensor(w2q[db][:], w2tile[db][:],
                                                      wstat[:, 3:4], tqs2[:],
                                                      OP.is_gt, OP.subtract)
                    q_v = p.emi("vector", wq2, "sV")
                    w2t_user[db] = q_v
                    p.wait("tensor", "sV", q_v)
                    if fc == 0:
                        p.wait("tensor", "sV", pM_user)
                        p.wait("tensor", "sA", hq_done)
                    for tt in range(8):
                        t_mm2 = p.emi(
                            "tensor", lambda e, tt=tt, fc=fc, db=db:
                            e.matmul(pM2[tt][:],
                                     g_sb[:, F_ * tt + 128 * fc:F_ * tt + 128 * fc + 128],
                                     w2q[db][:], start=(fc == 0), stop=(fc == 63)),
                            "sT" if tt == 7 else None)
                    w2q_user[db] = t_mm2
                p.wait("vector", "sT", t_mm2)
                p.wait("vector", "sB0", b2l)
                fb_user = [0, 0]
                for tt in range(8):
                    fb = tt % 2
                    p.wait("vector", "sSt", fb_user[fb])
                    pM_user = p.emi(
                        "vector", lambda e, tt=tt, fb=fb, sp=sp:
                        e.scalar_tensor_tensor(
                            fbuf[fb][:],
                            pM2[tt][:], alsc[:, 32 + tt:33 + tt], b2b[sp][:],
                            OP.mult, OP.add), "sV")
                    p.wait("sync", "sV", pM_user)
                    fb_user[fb] = p.dma(
                        "sync", ffn_d[128 * tt:128 * tt + 128, 512 * ds:512 * ds + 512],
                        fbuf[fb][:], "sSt")
                b2_user[sp] = pM_user
                ffn_st = fb_user[1]

            ph3d_V = p.val("sV")
            if "d_g" in dbg_t:
                p.wait("gpsimd", "sV", ph3d_V)
                dgv = 0
                for tt in range(8):
                    dgv = p.dma("gpsimd", dbg_t["d_g"][128 * tt:128 * tt + 128, :],
                                g_sb[:, F_ * tt:F_ * tt + F_], "sG")
                p.wait("sync", "sG", dgv)
                p.wait("vector", "sG", dgv)
            ph3d.close()
            ph3g.close()
            if "d_ffn" in dbg_t:
                p.wait("gpsimd", "sSt", ffn_st)
                fscr = ph3p.enter_context(nc.sbuf_tensor("fscr", [128, D_], BF16))
                fscr2 = ph3p.enter_context(nc.sbuf_tensor("fscr2", [128, D_], F32))
                for tt in range(8):
                    fl2 = p.dma("gpsimd", fscr[:], ffn_d[128 * tt:128 * tt + 128, :], "sG")
                    p.wait("vector", "sG", fl2)
                    cv2 = p.emi("vector", lambda e: e.tensor_copy(fscr2[:], fscr[:]), "sV")
                    p.wait("gpsimd", "sV", cv2)
                    sv2 = p.dma("gpsimd", dbg_t["d_ffn"][128 * tt:128 * tt + 128, :],
                                fscr2[:], "sG")
                    p.wait("gpsimd", "sG", sv2)
                    p.wait("vector", "sG", sv2)
            if "d_thr" in dbg_t:
                p.wait("gpsimd", "sLam", ws_bc)
                p.dma("gpsimd", dbg_t["d_thr"][:, :], wstat[:, :], "sG")

            ph4 = ExitStack()
            sb4 = lambda n, sh, d: ph4.enter_context(nc.sbuf_tensor(n, sh, d))
            ps4 = lambda n, sh, d: ph4.enter_context(nc.psum_tensor(n, sh, d))
            wo_sb = sb4("wo_sb", [128, 16 * D_], BF16)
            ffn_t = [sb4(f"ffn_t{i}", [128, 512], BF16) for i in range(2)]
            afm = sb4("afm", [128, 16 * R_], BF16)
            xr4 = [sb4(f"xr4_{i}", [128, D_], F32) for i in range(2)]
            obuf = [sb4(f"obuf{i}", [128, 512], F32) for i in range(2)]
            osum = sb4("osum", [128, 512], F32)
            pWo = [ps4(f"pWo{i}", [128, 512], F32) for i in range(2)]

            p.wait("gpsimd", "sV", ph3d_V)
            wo_l = 0
            for cc in range(16):
                wo_l = p.dma("gpsimd", wo_sb[:, D_ * cc:D_ * cc + D_],
                             wo_in[128 * cc:128 * cc + 128, :], "sWo")
            p.wait("sync", "sV", ph3d_V)
            p.wait("sync", "sT", p.val("sT"))
            if collective:
                p.wait("sync", "sCC", cc_v)
            agv = p.tick("sAG", 16 * 16)
            def ag_read(e):
                rk = e.cc_rank([[0, 1, 2, 3], [4, 5, 6, 7]])
                last = None
                for cc in range(16):
                    last = e.dma_start(
                        out=afm[:, R_ * cc:R_ * cc + R_],
                        in_=ag_out_t[128 * cc:128 * cc + 128, bass.ts(rk, R_)])
                    last.then_inc(p.sems["sAG"], 16)
                return last
            p.ops["sync"].append(lambda e: ag_read(e))

            xr4_user = [0, 0]
            ob_user = [0, 0]
            ffn_u = [0, 0]
            st_v = 0
            for tt in range(8):
                db = tt % 2
                p.wait("sync", "sSt", xr4_user[db])
                xl = p.dma("sync", xr4[db][:], xr_in[128 * tt:128 * tt + 128, :],
                           ["sX0", "sX1"][db])
                for os in range(4):
                    ob = os % 2
                    p.wait("sync", "sV", ffn_u[ob] if True else 0)
                    fl_ = p.dma("sync", ffn_t[ob][:],
                                ffn_d[128 * tt:128 * tt + 128, 512 * os:512 * os + 512],
                                ["sW10", "sW11"][ob])
                    p.wait("tensor", "sWo", wo_l)
                    p.wait("tensor", "sAG", agv)
                    p.wait("tensor", "sV", ob_user[ob])
                    t_wo = None
                    for cc in range(16):
                        t_wo = p.emi(
                            "tensor", lambda e, cc=cc, tt=tt, os=os, ob=ob:
                            e.matmul(pWo[ob][:],
                                     afm[:, R_ * cc + 128 * tt:R_ * cc + 128 * tt + 128],
                                     wo_sb[:, D_ * cc + 512 * os:D_ * cc + 512 * os + 512],
                                     start=(cc == 0), stop=(cc == 15)),
                            "sT" if cc == 15 else None)
                    p.wait("vector", "sT", t_wo)
                    p.wait("vector", ["sX0", "sX1"][db], xl)
                    p.wait("vector", ["sW10", "sW11"][ob], fl_)
                    def fin(e, tt=tt, os=os, ob=ob, db=db):
                        e.tensor_tensor(osum[:], pWo[ob][:], ffn_t[ob][:], OP.add)
                        return e.tensor_tensor(obuf[ob][:], osum[:],
                                               xr4[db][:, 512 * os:512 * os + 512], OP.add)
                    v_f = p.emi("vector", fin, "sV")
                    ob_user[ob] = v_f
                    ffn_u[ob] = v_f
                    p.wait("sync", "sV", v_f)
                    st_v = p.dma("sync", out_t[128 * tt:128 * tt + 128,
                                               512 * os:512 * os + 512],
                                 obuf[ob][:], "sSt")
                    p.wait("vector", "sSt", st_v)
                xr4_user[db] = st_v
            ph4.close()
            ph3p.close()
        st_total = p.val("sSt")
        if st_total:
            p.wait("gpsimd", "sSt", st_total)
        if cc_v:
            p.wait("gpsimd", "sCC", cc_v)

        with nc.Block() as block:
            @block.sync
            def _(e):
                p.run(e, "sync")

            @block.tensor
            def _(e):
                p.run(e, "tensor")

            @block.vector
            def _(e):
                p.run(e, "vector")

            @block.scalar
            def _(e):
                p.run(e, "scalar")

            @block.gpsimd
            def _(e):
                p.run(e, "gpsimd")

    return nc


def _shard_inputs(inputs):
    x = np.asarray(inputs["L0_attn_x"], dtype=np.float32)
    Wq = np.asarray(inputs["L0_attn_Wq"], dtype=np.float32)
    Wk = np.asarray(inputs["L0_attn_Wk"], dtype=np.float32)
    Wv = np.asarray(inputs["L0_attn_Wv"], dtype=np.float32)
    Wo = np.asarray(inputs["L0_attn_Wo"], dtype=np.float32)
    w1 = np.asarray(inputs["L0_ffn_w1"], dtype=np.float32)
    w2 = np.asarray(inputs["L0_ffn_w2"], dtype=np.float32)
    import ml_dtypes
    ident = np.eye(128, dtype=ml_dtypes.bfloat16)
    masks = np.zeros((4, 128, 512), dtype=np.float32)
    tri = np.tril(np.full((128, 128), NEG, dtype=np.float32), -1)
    for pvar in range(4):
        for j in range(4):
            if j < pvar:
                masks[pvar][:, 128 * j:128 * j + 128] = NEG
            elif j == pvar:
                masks[pvar][:, 128 * j:128 * j + 128] = tri
    masks = masks.reshape(4 * 128, 512)
    w1t = np.ascontiguousarray(w1.T)
    w2t = np.ascontiguousarray(w2.T)
    in_maps = []
    for c in range(8):
        s, g = c // 4, c % 4
        m = {
            "x": np.ascontiguousarray(x[s]),
            "xr": np.ascontiguousarray(x[s][1024 * g:1024 * g + 1024]),
            "wq": np.ascontiguousarray(Wq[:, 512 * g:512 * g + 512]),
            "wk": np.ascontiguousarray(Wk[:, 512 * g:512 * g + 512]),
            "wv": np.ascontiguousarray(Wv[:, 512 * g:512 * g + 512]),
            "wo": Wo,
            "w1t": w1t,
            "w2t": w2t,
            "b1": np.asarray(inputs["L0_ffn_b1"], dtype=np.float32).reshape(1, F_),
            "gam": np.asarray(inputs["L0_ffn_gamma"], dtype=np.float32).reshape(1, F_),
            "bet": np.asarray(inputs["L0_ffn_beta"], dtype=np.float32).reshape(1, F_),
            "b2": np.asarray(inputs["L0_ffn_b2"], dtype=np.float32).reshape(1, D_),
            "lq1": np.asarray(inputs["L0_attn_lq1"], dtype=np.float32).reshape(1, 64),
            "lk1": np.asarray(inputs["L0_attn_lk1"], dtype=np.float32).reshape(1, 64),
            "lq2": np.asarray(inputs["L0_attn_lq2"], dtype=np.float32).reshape(1, 64),
            "lk2": np.asarray(inputs["L0_attn_lk2"], dtype=np.float32).reshape(1, 64),
            "ident": ident,
            "identf": np.eye(128, dtype=np.float32),
            "masks": masks,
        }
        in_maps.append(m)
    return in_maps


def kernel(**inputs):
    in_maps = _shard_inputs(inputs)
    nc = build_nc(dbg=DBG, phases=PHASES)
    res = run_bass_kernel_spmd(nc, in_maps, core_ids=list(range(8)))
    out = np.empty((2, S_, D_), dtype=np.float32)
    for c in range(8):
        s, g = c // 4, c % 4
        out[s, R_ * g:R_ * g + R_] = res.results[c]["out"]
    return out


# version 10
# speedup vs baseline: 1.1283x; 1.0047x over previous
import math
import numpy as np
from contextlib import ExitStack

import concourse.bass as bass
import concourse.mybir as mybir
from concourse.bass_utils import run_bass_kernel_spmd
from concourse.alu_op_type import AluOpType

F32 = mybir.dt.float32
BF16 = mybir.dt.bfloat16
I32 = mybir.dt.int32
AF = mybir.ActivationFunctionType
AX = mybir.AxisListType.X
OP = AluOpType

S_ = 4096
D_ = 2048
F_ = 8192
R_ = 1024
NU = 4
LAMBDA_INIT = 0.8 - 0.6 * math.exp(-0.3 * 12)
CSH = 12.0
EPS = 1e-5
NEG = -1.0e9

PHASES = (1, 2, 3, 4)
DBG = ()
NKCAP = None
SERIAL1 = False
SERIAL_ACT = True


class Prog:
    ENGINES = ("sync", "tensor", "vector", "scalar", "gpsimd")

    def __init__(self):
        self.ops = {e: [] for e in self.ENGINES}
        self.cnt = {}
        self.sems = {}
        self.last_wait = {e: {} for e in self.ENGINES}

    def tick(self, sem, n=1):
        self.cnt[sem] = self.cnt.get(sem, 0) + n
        return self.cnt[sem]

    def val(self, sem):
        return self.cnt.get(sem, 0)

    def emi(self, eng, fn, sem=None, n=1):
        if sem is None:
            self.ops[eng].append(lambda e, fn=fn: fn(e))
            return None
        v = self.tick(sem, n)
        def run(e, fn=fn, sem=sem, n=n):
            fn(e).then_inc(self.sems[sem], n)
        self.ops[eng].append(run)
        return v

    def wait(self, eng, sem, v):
        if not v or v <= 0:
            return
        if self.last_wait[eng].get(sem, 0) >= v:
            return
        self.last_wait[eng][sem] = v
        self.ops[eng].append(lambda e, sem=sem, v=v: e.wait_ge(self.sems[sem], v))

    def dma(self, qeng, dst, src, sem, **kw):
        return self.emi(qeng, lambda e, d=dst, s=src, kw=kw:
                        e.dma_start(out=d, in_=s, **kw), sem, 16)

    def run(self, eng_obj, name):
        for fn in self.ops[name]:
            fn(eng_obj)


def build_nc(dbg=(), phases=PHASES, collective=True):
    nc = bass.Bass("TRN2", target_bir_lowering=False, detect_race_conditions=False)
    ap = lambda t: t.ap()

    x_in = ap(nc.dram_tensor("x", [S_, D_], F32, kind="ExternalInput"))
    xr_in = ap(nc.dram_tensor("xr", [R_, D_], F32, kind="ExternalInput"))
    wstat_d = None
    wq_in = ap(nc.dram_tensor("wq", [D_, 512], F32, kind="ExternalInput"))
    wk_in = ap(nc.dram_tensor("wk", [D_, 512], F32, kind="ExternalInput"))
    wv_in = ap(nc.dram_tensor("wv", [D_, 512], F32, kind="ExternalInput"))
    wo_in = ap(nc.dram_tensor("wo", [D_, D_], F32, kind="ExternalInput"))
    w1t_in = ap(nc.dram_tensor("w1t", [D_, F_], F32, kind="ExternalInput"))
    w2t_in = ap(nc.dram_tensor("w2t", [F_, D_], F32, kind="ExternalInput"))
    b1_in = ap(nc.dram_tensor("b1", [1, F_], F32, kind="ExternalInput"))
    gam_in = ap(nc.dram_tensor("gam", [1, F_], F32, kind="ExternalInput"))
    bet_in = ap(nc.dram_tensor("bet", [1, F_], F32, kind="ExternalInput"))
    b2_in = ap(nc.dram_tensor("b2", [1, D_], F32, kind="ExternalInput"))
    lq1_in = ap(nc.dram_tensor("lq1", [1, 64], F32, kind="ExternalInput"))
    lk1_in = ap(nc.dram_tensor("lk1", [1, 64], F32, kind="ExternalInput"))
    lq2_in = ap(nc.dram_tensor("lq2", [1, 64], F32, kind="ExternalInput"))
    lk2_in = ap(nc.dram_tensor("lk2", [1, 64], F32, kind="ExternalInput"))
    ident_in = ap(nc.dram_tensor("ident", [128, 128], BF16, kind="ExternalInput"))
    identf_in = ap(nc.dram_tensor("identf", [128, 128], F32, kind="ExternalInput"))
    masks_in = ap(nc.dram_tensor("masks", [4 * 128, 512], F32, kind="ExternalInput"))

    out_t = ap(nc.dram_tensor("out", [R_, D_], F32, kind="ExternalOutput"))

    lam_d = ap(nc.dram_tensor("lam_d", [1, 1], F32))
    wstat_d = ap(nc.dram_tensor("wstat_d", [1, 8], F32))
    ffn_d = ap(nc.dram_tensor("ffn_d", [R_, D_], BF16))
    ag_in_t = ap(nc.dram_tensor("ag_in", [512, S_], BF16))
    ag_out_t = ap(nc.dram_tensor("ag_out", [4 * 512, S_], BF16))

    dbg_t = {}
    for name, shape in [
        ("d_qfm", [512, S_]), ("d_kfm", [512, S_]), ("d_v", [S_, 512]),
        ("d_lam", [128, 1]), ("d_attn", [512, S_]), ("d_ag", [2048, S_]),
        ("d_xq", [R_, D_]), ("d_alpha", [128, 8]), ("d_g", [R_, F_]),
        ("d_hq", [R_, F_]), ("d_ffn", [R_, D_]), ("d_thr", [128, 8]),
        ("d_p1", [128, 512]), ("d_p2", [128, 512]), ("d_l", [1, 1024]),
        ("d_xn", [S_, D_]), ("d_st", [128, 16]),
        ("d_s1", [128, 512]), ("d_p1k0", [128, 512]),
        ("d_a1", [128, 512]), ("d_a2", [128, 512]), ("d_a1t", [128, 512]),
        ("d_comb", [128, 512]), ("d_rc", [128, 12]), ("d_linv", [128, 8]),
        ("d_3c", [384, 8192]),
    ]:
        if name in dbg:
            dbg_t[name] = ap(nc.dram_tensor(name, shape, F32, kind="ExternalOutput"))

    p = Prog()

    ctx = ExitStack()
    with ctx:
        sbg = lambda n, s, d, **kw: ctx.enter_context(nc.sbuf_tensor(n, s, d, **kw))
        SEMS = ("sIdent", "sLam", "sX0", "sX1", "sWp", "sMsk", "sW10", "sW11",
                "sW12", "sW13", "sW20", "sW21", "sW22", "sW23",
                "sB0", "sB1", "sWo", "sAG", "sAT", "sSt",
                "sM0", "sM1", "sT", "sV", "sA", "sG", "sCC")
        for s in SEMS:
            p.sems[s] = ctx.enter_context(nc.semaphore(s))

        ident = sbg("ident_sb", [128, 128], BF16)
        identf = sbg("identf_sb", [128, 128], F32)
        ones_f = sbg("ones_f", [128, 1], F32)
        ones_b = sbg("ones_b", [128, 128], BF16)
        lam_col = sbg("lam_col", [128, 1], F32)
        lam_sc = sbg("lam_sc", [1, 8], F32)
        consts = sbg("consts", [128, 8], F32)
        wmacc = sbg("wmacc", [128, 4], F32)
        wstat = sbg("wstat", [128, 8], F32)
        alph = sbg("alph", [128, 8], F32)
        lamv = sbg("lamv", [1, 64], F32)
        lamv2 = sbg("lamv2", [1, 64], F32)
        lamk = sbg("lamk", [1, 64], F32)
        lamk2 = sbg("lamk2", [1, 64], F32)
        ph12 = ExitStack()
        sb12 = lambda n, sh, d: ph12.enter_context(nc.sbuf_tensor(n, sh, d))
        kfm = [sb12(f"kfm{u}", [128, S_], BF16) for u in range(NU)]
        qfm = [sb12(f"qfm{u}", [128, S_], BF16) for u in range(NU)]
        v_sb = sb12("v_sb", [128, 32 * 512], BF16)

        def setup_consts(e):
            e.memset(consts[:, 0:1], 0.125)
            e.memset(consts[:, 1:2], -CSH)
            e.memset(consts[:, 2:3], 256.5)
            e.memset(consts[:, 3:4], -0.5)
            e.memset(ones_f[:], 1.0)
            e.memset(ones_b[:], 1.0)
            e.memset(wmacc[:], 0.0)
            return e.engine_nop()
        vconst = p.emi("vector", setup_consts, "sV")

        p.dma("sync", ident[:], ident_in[:, :], "sIdent")
        ld_ident = p.dma("sync", identf[:], identf_in[:, :], "sIdent")
        p.dma("sync", lamv[0:1, :], lq1_in[:, :], "sLam")
        p.dma("sync", lamk[0:1, :], lk1_in[:, :], "sLam")
        p.dma("sync", lamv2[0:1, :], lq2_in[:, :], "sLam")
        lam_ld = p.dma("sync", lamk2[0:1, :], lk2_in[:, :], "sLam")

        p.wait("vector", "sLam", lam_ld)
        def lam_dots(e):
            e.tensor_tensor(lamv[0:1, :], lamv[0:1, :], lamk[0:1, :], OP.mult)
            e.engine_nop()
            e.tensor_reduce(lam_sc[0:1, 0:1], lamv[0:1, :], AX, OP.add)
            e.tensor_tensor(lamv2[0:1, :], lamv2[0:1, :], lamk2[0:1, :], OP.mult)
            e.engine_nop()
            return e.tensor_reduce(lam_sc[0:1, 1:2], lamv2[0:1, :], AX, OP.add)
        v_dots = p.emi("vector", lam_dots, "sV")
        p.wait("scalar", "sV", v_dots)
        a_exp = p.emi("scalar", lambda e:
                      e.activation(lam_sc[0:1, 2:4], lam_sc[0:1, 0:2], AF.Exp), "sA")
        p.wait("vector", "sA", a_exp)
        def lam_fin(e):
            e.tensor_tensor(lam_sc[0:1, 4:5], lam_sc[0:1, 2:3], lam_sc[0:1, 3:4], OP.subtract)
            e.engine_nop()
            return e.tensor_single_scalar(lam_sc[0:1, 5:6], lam_sc[0:1, 4:5],
                                          LAMBDA_INIT, OP.add)
        v_lam = p.emi("vector", lam_fin, "sV")
        p.wait("sync", "sV", v_lam)
        lam_st = p.dma("sync", lam_d[:, :], lam_sc[0:1, 5:6], "sLam")
        p.wait("sync", "sLam", lam_st)
        lam_bc = p.dma("sync", lam_col[:], lam_d[:, :].partition_broadcast(128), "sLam")
        if "d_lam" in dbg_t:
            p.wait("gpsimd", "sLam", lam_bc)
            p.dma("gpsimd", dbg_t["d_lam"][:, :], lam_col[:], "sG")

        ph1 = ExitStack()
        sb1 = lambda n, s, d: ph1.enter_context(nc.sbuf_tensor(n, s, d))
        ps1 = lambda n, s, d: ph1.enter_context(nc.psum_tensor(n, s, d))
        wq_sb = sb1("wq_sb", [128, 16 * 512], BF16)
        wk_sb = sb1("wk_sb", [128, 16 * 512], BF16)
        wv_sb = sb1("wv_sb", [128, 16 * 512], BF16)
        x_db = [sb1(f"x_db{i}", [128, D_], F32) for i in range(2)]
        sq_scr = sb1("sq_scr", [128, D_], F32)
        xnb_db = [sb1(f"xnb{i}", [128, D_], BF16) for i in range(2)]
        xnfm_blk = sb1("xnfm_blk", [128, 16 * 512], BF16)
        st_sb = sb1("st_sb", [128, 16], F32)
        pQ = [ps1(f"pQ{i}", [128, 512], F32) for i in range(2)]
        pTr = [ps1(f"pTr{i}", [128, 512], BF16) for i in range(2)]

        wld = 0
        for (wsb, win) in ((wq_sb, wq_in), (wk_sb, wk_in), (wv_sb, wv_in)):
            for dc in range(16):
                wld = p.dma("gpsimd", wsb[:, 512 * dc:512 * dc + 512],
                            win[128 * dc:128 * dc + 128, :], "sWp")

        NT = S_ // 128
        x_user = [0, 0]
        x_userV = [0, 0]
        xnb_user = [0, 0]
        tr_copy = [0, 0]
        st_user = [0, 0]
        xsem = ["sX0", "sX1"]

        for t in range(NT):
            db = t % 2
            p.wait("sync", "sA", x_user[db])
            p.wait("sync", "sV", x_userV[db])
            xl = p.dma("sync", x_db[db][:], x_in[128 * t:128 * t + 128, :], xsem[db])
            st = st_sb[:, 8 * db:8 * db + 8]
            p.wait("scalar", xsem[db], xl)
            p.wait("scalar", "sV", st_user[db] if not SERIAL_ACT else p.val("sV"))
            a_sq = p.emi("scalar", lambda e, xt=x_db[db], st=st:
                         e.activation(sq_scr[:], xt[:], AF.Square, accum_out=st[:, 1:2]), "sA")
            p.wait("vector", xsem[db], xl)
            def red1(e, xt=x_db[db], st=st):
                e.tensor_reduce(st[:, 0:1], xt[:], AX, OP.add)
                for _ in range(4):
                    e.engine_nop()
                return e.engine_nop()
            p.emi("vector", red1)
            p.wait("vector", "sA", a_sq)
            def ln_stats(e, st=st):
                e.engine_nop()
                e.tensor_single_scalar(st[:, 2:3], st[:, 0:1], 1.0 / D_, OP.mult)
                e.tensor_single_scalar(st[:, 3:4], st[:, 1:2], 1.0 / D_, OP.mult)
                e.tensor_tensor(st[:, 4:5], st[:, 2:3], st[:, 2:3], OP.mult)
                e.engine_nop()
                e.tensor_tensor(st[:, 5:6], st[:, 3:4], st[:, 4:5], OP.subtract)
                e.engine_nop()
                return e.tensor_single_scalar(st[:, 5:6], st[:, 5:6], EPS, OP.add)
            v_st = p.emi("vector", ln_stats, "sV")
            st_user[db] = v_st
            x_userV[db] = v_st
            p.wait("scalar", "sV", max(v_st, vconst))
            p.emi("scalar", lambda e, st=st: e.activation(st[:, 6:7], st[:, 5:6], AF.Ln))
            a_r = p.emi("scalar", lambda e, st=st:
                        e.activation(st[:, 7:8], st[:, 6:7], AF.Exp, scale=consts[:, 3:4]), "sA")
            p.wait("vector", "sA", a_r)
            def mk_mr(e, st=st):
                e.scalar_tensor_tensor(st[:, 4:5], st[:, 2:3], -1.0, st[:, 7:8],
                                       OP.mult, OP.mult)
                return e.tensor_copy(st[:, 6:7], st[:, 7:8])
            v_mr = p.emi("vector", mk_mr, "sV")
            p.wait("scalar", "sV", v_mr)
            p.wait("scalar", "sT", xnb_user[db])
            a_xn = p.emi("scalar", lambda e, xt=x_db[db], st=st, ot=xnb_db[db]:
                         e.activation(ot[:], xt[:], AF.Identity,
                                      bias=st[:, 4:5], scale=st[:, 6:7]), "sA")
            x_user[db] = a_xn

            if "d_xn" in dbg_t:
                p.wait("gpsimd", "sA", a_xn)
                dxn = p.dma("gpsimd", dbg_t["d_xn"][128 * t:128 * t + 128, :],
                            xnb_db[db][:], "sG")
                p.wait("scalar", "sG", dxn)
                if "d_st" in dbg_t and t == 4:
                    p.dma("gpsimd", dbg_t["d_st"][:, :], st_sb[:, :], "sG")
            p.wait("tensor", "sA", a_xn)
            p.wait("tensor", "sIdent", ld_ident)
            tsub = t % 4
            for dc in range(16):
                tb = dc % 2
                p.wait("tensor", "sV", tr_copy[tb] if not SERIAL1 else p.val("sV"))
                t_tr = p.emi("tensor", lambda e, xt=xnb_db[db], dc=dc, pt=pTr[tb]:
                             e.transpose(pt[:, :128], xt[:, 128 * dc:128 * dc + 128],
                                         ident[:]), "sT")
                p.wait("vector", "sT", t_tr)
                tr_copy[tb] = p.emi(
                    "vector", lambda e, dc=dc, tsub=tsub, pt=pTr[tb]:
                    e.tensor_copy(xnfm_blk[:, 512 * dc + 128 * tsub:512 * dc + 128 * tsub + 128],
                                  pt[:, :128]), "sV")
            xnb_user[db] = p.val("sT")

            if t % 4 == 3:
                blk = t // 4
                if SERIAL1:
                    p.wait("tensor", "sV", p.val("sV"))
                    p.wait("tensor", "sA", p.val("sA"))
                if blk == 0 and "d_g" in dbg_t and 1 in phases and 3 not in phases:
                    p.wait("gpsimd", "sV", p.val("sV"))
                    dxf = p.dma("gpsimd", dbg_t["d_g"][0:128, :], xnfm_blk[:], "sG")
                    p.wait("tensor", "sG", dxf)
                p.wait("tensor", "sV", p.val("sV"))
                p.wait("tensor", "sWp", wld)
                pq_guard = [0, 0]
                pi = 0
                for (wsb, dst) in ((wq_sb, qfm), (wk_sb, kfm)):
                    for u in range(NU):
                        cc = 128 * u
                        pqb = pi % 2
                        pi += 1
                        p.wait("tensor", "sV", pq_guard[pqb])
                        if SERIAL1:
                            p.wait("tensor", "sV", p.val("sV"))
                        t_q = None
                        for dc in range(16):
                            t_q = p.emi(
                                "tensor", lambda e, w=wsb, dc=dc, cc=cc, pq=pQ[pqb]:
                                e.matmul(pq[:],
                                         w[:, 512 * dc + cc:512 * dc + cc + 128],
                                         xnfm_blk[:, 512 * dc:512 * dc + 512],
                                         start=(dc == 0), stop=(dc == 15)),
                                "sT" if dc == 15 else None)
                        p.wait("vector", "sT", t_q)
                        pq_guard[pqb] = p.emi(
                            "vector", lambda e, d=dst[u], blk=blk, pq=pQ[pqb]:
                            e.tensor_copy(d[:, 512 * blk:512 * blk + 512],
                                          pq[:]), "sV")
                for tsub2 in range(4):
                    kt = 4 * blk + tsub2
                    pqb = pi % 2
                    pi += 1
                    p.wait("tensor", "sV", pq_guard[pqb])
                    if SERIAL1:
                        p.wait("tensor", "sV", p.val("sV"))
                    t_v = None
                    for dc in range(16):
                        t_v = p.emi(
                            "tensor", lambda e, dc=dc, t2=tsub2, pv=pQ[pqb]:
                            e.matmul(pv[:],
                                     xnfm_blk[:, 512 * dc + 128 * t2:512 * dc + 128 * t2 + 128],
                                     wv_sb[:, 512 * dc:512 * dc + 512],
                                     start=(dc == 0), stop=(dc == 15)),
                            "sT" if dc == 15 else None)
                    p.wait("vector", "sT", t_v)
                    pq_guard[pqb] = p.emi(
                        "vector", lambda e, kt=kt, pv=pQ[pqb]:
                        e.tensor_copy(v_sb[:, 512 * kt:512 * kt + 512], pv[:]), "sV")

        ph1_V = p.val("sV")
        ph1_T = p.val("sT")

        if dbg_t:
            p.wait("gpsimd", "sV", ph1_V)
            p.wait("gpsimd", "sT", ph1_T)
            if "d_qfm" in dbg_t:
                for u in range(NU):
                    p.dma("gpsimd", dbg_t["d_qfm"][128 * u:128 * u + 128, :], qfm[u][:, :], "sG")
            if "d_kfm" in dbg_t:
                for u in range(NU):
                    p.dma("gpsimd", dbg_t["d_kfm"][128 * u:128 * u + 128, :], kfm[u][:, :], "sG")
            if "d_v" in dbg_t:
                for kt in range(32):
                    p.dma("gpsimd", dbg_t["d_v"][128 * kt:128 * kt + 128, :],
                          v_sb[:, 512 * kt:512 * kt + 512], "sG")

        ph1.close()

        do_wmean = (3 in phases and 4 in phases)
        wmean_state = {"col": 0, "m_user": [0, 0]}
        if do_wmean:
            ph3m = ExitStack()
            sb3m = lambda n, sh, d: ph3m.enter_context(
                nc.sbuf_tensor(n, sh, d, side="right"))
            wmp = sb3m("wmp", [128, 256], F32)
            wrow3 = sb3m("wrow3", [1, 8], F32)
            mscr = [sb3m(f"mscr{i}", [128, D_ // 2], BF16) for i in range(2)]
            _wm_src = []
            for wi, win, nr in ((0, w1t_in, 16), (1, w2t_in, 64)):
                for i in range(nr):
                    nchunk = 8 if wi == 0 else 2
                    for j in range(nchunk):
                        _wm_src.append(
                            win[128 * i:128 * i + 128,
                                (D_ // 2) * j:(D_ // 2) * j + D_ // 2])
            assert len(_wm_src) == 256
            _wm_sem = ["sM0", "sM1"]
            _wm_eng = ["gpsimd", "gpsimd"]

            def wmean_step(n):
                for _ in range(n):
                    col = wmean_state["col"]
                    if col >= 256:
                        return
                    db = col % 2
                    p.wait(_wm_eng[db], "sV", wmean_state["m_user"][db])
                    ml = p.dma(_wm_eng[db], mscr[db][:], _wm_src[col], _wm_sem[db])
                    p.wait("vector", _wm_sem[db], ml)
                    wmean_state["m_user"][db] = p.emi(
                        "vector", lambda e, db=db, col=col:
                        e.tensor_reduce(wmp[:, col:col + 1], mscr[db][:], AX,
                                        OP.add, apply_absolute_value=True), "sV")
                    wmean_state["col"] = col + 1
        else:
            def wmean_step(n):
                pass

        cc_v = 0
        at_store = 0
        if 2 in phases:
            ph2 = ExitStack()
            sb2 = lambda n, s, d: ph2.enter_context(nc.sbuf_tensor(n, s, d))
            ps2 = lambda n, s, d: ph2.enter_context(nc.psum_tensor(n, s, d))
            masks_sb = sb2("masks_sb", [128, 4 * 512], F32)
            P1 = [sb2(f"P1_{i}", [128, 512], BF16) for i in range(2)]
            P2 = [sb2(f"P2_{i}", [128, 512], BF16) for i in range(2)]
            a1_sb = sb2("a1_sb", [128, 512], F32)
            a2_sb = sb2("a2_sb", [128, 512], F32)
            lrow = sb2("lrow", [1, 1024], F32)
            linv = sb2("linv", [128, 8], F32)
            comb = sb2("comb", [128, 512], F32)
            rms_scr = sb2("rms_scr", [128, 512], F32)
            rcol = sb2("rcol", [128, 12], F32)
            atn_bf = sb2("atn_bf", [128, 512], BF16)
            atT_bf = sb2("atT_bf", [128, 512], BF16)
            pS1 = [ps2(f"pS1{i}", [128, 512], F32) for i in range(2)]
            pS2 = [ps2(f"pS2{i}", [128, 512], F32) for i in range(2)]
            pA1 = ps2("pA1", [128, 512], F32)
            pA2 = ps2("pA2", [128, 512], F32)
            pL1 = ps2("pL1", [128, 512], F32)
            pL2 = ps2("pL2", [128, 512], F32)
            pTb = pL2[:].bitcast(BF16)

            p.wait("sync", "sT", ph1_T)
            p.wait("sync", "sV", ph1_V)
            mld = p.dma("sync", masks_sb[:].rearrange("p (m q) -> p m q", m=4),
                        masks_in.rearrange("(m p) q -> p m q", p=128), "sMsk")

            expA = [0, 0]
            avT = [0, 0]
            prev_store = 0
            prev_cmb = 0
            prev_li = 0
            prev_att = 0
            for u in range(NU):
                for B in range(8):
                    nk = 4 * B + 4 if NKCAP is None else min(NKCAP, 4 * B + 4)
                    for kt in range(nk):
                        bi = kt % 2
                        p.wait("tensor", "sA", expA[bi])
                        p.wait("tensor", "sV", ph1_V)
                        p.emi("tensor", lambda e, u=u, kt=kt, B=B, pb=pS1[bi]:
                              e.matmul(pb[:], kfm[u][0:64, 128 * kt:128 * kt + 128],
                                       qfm[u][0:64, 512 * B:512 * B + 512],
                                       start=True, stop=True))
                        t_s = p.emi("tensor", lambda e, u=u, kt=kt, B=B, pb=pS2[bi]:
                                    e.matmul(pb[:], kfm[u][64:128, 128 * kt:128 * kt + 128],
                                             qfm[u][64:128, 512 * B:512 * B + 512],
                                             start=True, stop=True), "sT")
                        need_mask = kt >= 4 * B
                        s_mask = 0
                        if need_mask:
                            pv = kt - 4 * B
                            p.wait("vector", "sT", t_s)
                            p.wait("vector", "sMsk", mld)
                            p.emi("vector", lambda e, pb=pS1[bi], pv=pv:
                                  e.tensor_tensor(pb[:], pb[:],
                                                  masks_sb[:, 512 * pv:512 * pv + 512], OP.add))
                            s_mask = p.emi(
                                "vector", lambda e, pb=pS2[bi], pv=pv:
                                e.tensor_tensor(pb[:], pb[:],
                                                masks_sb[:, 512 * pv:512 * pv + 512], OP.add),
                                "sV")
                        p.wait("scalar", "sT", t_s)
                        if need_mask:
                            p.wait("scalar", "sV", s_mask)
                        p.wait("scalar", "sT", avT[bi])
                        p.emi("scalar", lambda e, pb=pS1[bi], o=P1[bi]:
                              e.activation(o[:], pb[:], AF.Exp,
                                           bias=consts[:, 1:2], scale=consts[:, 0:1]))
                        expA[bi] = p.emi("scalar", lambda e, pb=pS2[bi], o=P2[bi]:
                                         e.activation(o[:], pb[:], AF.Exp,
                                                      bias=consts[:, 1:2],
                                                      scale=consts[:, 0:1]), "sA")
                        if u == 0 and B == 0 and kt == 3 and "d_p1" in dbg_t and "d_p2" in dbg_t:
                            p.wait("gpsimd", "sA", expA[bi])
                            p.dma("gpsimd", dbg_t["d_p1"][:, :], P1[bi][:], "sG")
                            dmp = p.dma("gpsimd", dbg_t["d_p2"][:, :], P2[bi][:], "sG")
                            p.wait("tensor", "sG", dmp)
                        if u == 0 and B == 0 and kt == 0 and "d_p1k0" in dbg_t:
                            p.wait("gpsimd", "sA", expA[bi])
                            dmp0b = p.dma("gpsimd", dbg_t["d_p1k0"][:, :], P1[bi][:], "sG")
                            p.wait("tensor", "sG", dmp0b)
                        p.wait("tensor", "sA", expA[bi])
                        fl = dict(start=(kt == 0), stop=(kt == nk - 1))
                        if kt == 0:
                            p.wait("tensor", "sV", prev_cmb)
                        p.emi("tensor", lambda e, u=u, kt=kt, P=P1[bi], f=fl:
                              e.matmul(pA1[:],
                                       v_sb[:, 512 * kt + 128 * u:512 * kt + 128 * u + 128],
                                       P[:], **f))
                        p.emi("tensor", lambda e, u=u, kt=kt, P=P2[bi], f=fl:
                              e.matmul(pA2[:],
                                       v_sb[:, 512 * kt + 128 * u:512 * kt + 128 * u + 128],
                                       P[:], **f))
                        if kt == 0:
                            p.wait("tensor", "sV", prev_li)
                        p.emi("tensor", lambda e, P=P1[bi], f=fl:
                              e.matmul(pL1[:, 0:512], ones_b[:], P[:], **f))
                        if kt == 0:
                            p.wait("tensor", "sV", prev_att)
                        avT[bi] = p.emi("tensor", lambda e, P=P2[bi], f=fl:
                                        e.matmul(pL2[:, 0:512], ones_b[:], P[:], **f), "sT")
                    t_all = avT[(nk - 1) % 2]
                    p.wait("vector", "sT", t_all)
                    p.emi("vector", lambda e: e.tensor_copy(lrow[0:1, 0:512], pL1[0:1, 0:512]))
                    p.emi("vector", lambda e: e.tensor_copy(lrow[0:1, 512:1024], pL2[0:1, 0:512]))
                    p.emi("vector", lambda e: e.tensor_copy(a1_sb[:], pA1[:]))
                    evac_V = p.emi("vector", lambda e: e.tensor_copy(a2_sb[:], pA2[:]), "sV")
                    p.wait("tensor", "sV", evac_V)
                    for j in range(4):
                        p.emi("tensor", lambda e, j=j:
                              e.matmul(pL1[:, 8 + j:9 + j], lrow[0:1, 128 * j:128 * j + 128],
                                       ones_f[0:1, 0:1], start=True, stop=True))
                    t_lt = None
                    for j in range(4):
                        t_lt = p.emi("tensor", lambda e, j=j:
                                     e.matmul(pL2[:, 8 + j:9 + j],
                                              lrow[0:1, 512 + 128 * j:640 + 128 * j],
                                              ones_f[0:1, 0:1], start=True, stop=True),
                                     "sT" if j == 3 else None)
                    for j in range(4):
                        p.emi("tensor", lambda e, j=j:
                              e.transpose(pA1[:, 128 * j:128 * j + 128],
                                          a1_sb[:, 128 * j:128 * j + 128], identf[:]))
                    t_tr = None
                    for j in range(4):
                        t_tr = p.emi("tensor", lambda e, j=j:
                                     e.transpose(pA2[:, 128 * j:128 * j + 128],
                                                 a2_sb[:, 128 * j:128 * j + 128], identf[:]),
                                     "sT" if j == 3 else None)
                    p.wait("vector", "sT", t_lt)
                    p.wait("vector", "sLam", lam_bc)
                    p.emi("vector", lambda e: e.reciprocal(linv[:, 0:4], pL1[:, 8:12]))
                    p.emi("vector", lambda e: e.reciprocal(linv[:, 4:8], pL2[:, 8:12]))
                    p.emi("vector", lambda e: e.engine_nop())
                    v_li = p.emi("vector", lambda e:
                                 e.tensor_scalar(linv[:, 4:8], linv[:, 4:8],
                                                 lam_col[:, 0:1], None, OP.mult), "sV")
                    p.emi("vector", lambda e: e.engine_nop())
                    if u == 0 and B == 0 and "d_l" in dbg_t:
                        p.wait("gpsimd", "sV", evac_V)
                        dmp2 = p.dma("gpsimd", dbg_t["d_l"][:, :], lrow[0:1, :], "sG")
                        if "d_a1" in dbg_t and "d_a2" in dbg_t:
                            p.dma("gpsimd", dbg_t["d_a1"][:, :], a1_sb[:], "sG")
                            dmp2 = p.dma("gpsimd", dbg_t["d_a2"][:, :], a2_sb[:], "sG")
                        p.wait("vector", "sG", dmp2)
                    p.wait("vector", "sT", t_tr)
                    a_sq2 = 0
                    for j in range(4):
                        p.wait("vector", "sA", a_sq2)
                        p.emi("vector", lambda e, j=j:
                              e.tensor_scalar(comb[:, 128 * j:128 * j + 128],
                                              pA1[:, 128 * j:128 * j + 128],
                                              linv[:, j:j + 1], None, OP.mult))
                        v_cb = p.emi("vector", lambda e, j=j:
                                     e.scalar_tensor_tensor(comb[:, 128 * j:128 * j + 128],
                                                            pA2[:, 128 * j:128 * j + 128],
                                                            linv[:, 4 + j:5 + j],
                                                            comb[:, 128 * j:128 * j + 128],
                                                            OP.mult, OP.subtract), "sV")
                        p.wait("scalar", "sV", v_cb)
                        a_sq2 = p.emi("scalar", lambda e, j=j:
                                      e.activation(rms_scr[:, 128 * j:128 * j + 128],
                                                   comb[:, 128 * j:128 * j + 128],
                                                   AF.Square, accum_out=rcol[:, j:j + 1]), "sA")
                    prev_cmb = v_cb
                    if u == 0 and B == 0 and "d_comb" in dbg_t and "d_linv" in dbg_t:
                        p.wait("gpsimd", "sA", a_sq2)
                        p.dma("gpsimd", dbg_t["d_comb"][:, :], comb[:], "sG")
                        dmp3 = p.dma("gpsimd", dbg_t["d_linv"][:, :], linv[:], "sG")
                        p.wait("vector", "sG", dmp3)
                    p.wait("vector", "sA", a_sq2)
                    v_rs = p.emi("vector", lambda e:
                                 e.tensor_scalar(rcol[:, 0:4], rcol[:, 0:4], 1.0 / 128.0,
                                                 EPS, OP.mult, OP.add), "sV")
                    p.wait("scalar", "sV", v_rs)
                    p.emi("scalar", lambda e: e.activation(rcol[:, 4:8], rcol[:, 0:4], AF.Ln))
                    a_rr = p.emi("scalar", lambda e:
                                 e.activation(rcol[:, 8:12], rcol[:, 4:8], AF.Exp,
                                              scale=consts[:, 3:4]), "sA")
                    p.wait("vector", "sA", a_rr)
                    p.emi("vector", lambda e:
                          e.tensor_single_scalar(rcol[:, 8:12], rcol[:, 8:12],
                                                 LAMBDA_INIT - 1.0, OP.mult))
                    p.emi("vector", lambda e: e.engine_nop())
                    if u == 0 and B == 0 and "d_rc" in dbg_t:
                        p.wait("gpsimd", "sV", p.val("sV"))
                        dmp4 = p.dma("gpsimd", dbg_t["d_rc"][:, :], rcol[:], "sG")
                        p.wait("vector", "sG", dmp4)
                    p.wait("vector", "sAT", prev_store)
                    v_at = 0
                    for j in range(4):
                        v_at = p.emi("vector", lambda e, j=j:
                                     e.tensor_scalar(atn_bf[:, 128 * j:128 * j + 128],
                                                     comb[:, 128 * j:128 * j + 128],
                                                     rcol[:, 8 + j:9 + j], None, OP.mult), "sV")
                    p.wait("tensor", "sV", v_at)
                    t_bt = None
                    for j in range(4):
                        t_bt = p.emi("tensor", lambda e, j=j:
                                     e.transpose(pTb[:, 128 * j:128 * j + 128],
                                                 atn_bf[:, 128 * j:128 * j + 128], ident[:]),
                                     "sT" if j == 3 else None)
                    p.wait("vector", "sT", t_bt)
                    v_att = p.emi("vector", lambda e: e.tensor_copy(atT_bf[:], pTb[:, 0:512]), "sV")
                    p.wait("sync", "sV", v_att)
                    prev_store = p.dma("sync",
                                       ag_in_t[128 * u:128 * u + 128, 512 * B:512 * B + 512],
                                       atT_bf[:], "sAT")
                    prev_li = v_li
                    prev_att = v_att
                    wmean_step(8)
            at_store = prev_store

            if "d_attn" in dbg_t:
                p.wait("gpsimd", "sAT", at_store)
                abscr = sb2("abscr", [128, S_], BF16)
                for u in range(NU):
                    lb = p.dma("gpsimd", abscr[:], ag_in_t[128 * u:128 * u + 128, :], "sG")
                    p.wait("gpsimd", "sG", lb)
                    sv = p.dma("gpsimd", dbg_t["d_attn"][128 * u:128 * u + 128, :],
                               abscr[:], "sG")
                    p.wait("gpsimd", "sG", sv)

            if collective:
                p.wait("gpsimd", "sAT", at_store)
                cc_v = p.emi("gpsimd", lambda e:
                             e.collective_compute("AllGather", OP.bypass,
                                                  replica_groups=[[0, 1, 2, 3], [4, 5, 6, 7]],
                                                  ins=[ag_in_t.opt()],
                                                  outs=[ag_out_t.opt()]), "sCC")
            if "d_ag" in dbg_t:
                p.wait("gpsimd", "sCC", cc_v)
                agscr = sb2("agscr", [128, S_], BF16)
                for r in range(16):
                    lb = p.dma("gpsimd", agscr[:], ag_out_t[128 * r:128 * r + 128, :], "sG")
                    p.wait("gpsimd", "sG", lb)
                    sv = p.dma("gpsimd", dbg_t["d_ag"][128 * r:128 * r + 128, :],
                               agscr[:], "sG")
                    p.wait("gpsimd", "sG", sv)
            ph2.close()

        ph2_T = p.val("sT")
        ph2_V = p.val("sV")
        ph2_A = p.val("sA")

        ph12.close()

        if 3 in phases and 4 in phases:
            ph3p = ExitStack()
            sb3p = lambda n, sh, d: ph3p.enter_context(nc.sbuf_tensor(n, sh, d))
            alsc = sb3p("alsc", [128, 48], F32)
            gsums = sb3p("gsums", [128, 8 * 16], F32)
            gssqs = sb3p("gssqs", [128, 8 * 16], F32)

            ph3g = ExitStack()
            sb3g = lambda n, sh, d: ph3g.enter_context(nc.sbuf_tensor(n, sh, d))
            g_sb = sb3g("g_sb", [128, 8 * F_], BF16)

            for qe in ("sync", "gpsimd"):
                p.wait(qe, "sT", ph2_T)
                p.wait(qe, "sV", ph2_V)
                p.wait(qe, "sA", ph2_A)

            wmean_step(256 - wmean_state["col"])
            m_user = wmean_state["m_user"]
            mean_V = max(m_user)
            ph3ab = ExitStack()
            sb3a = lambda n, sh, d: ph3ab.enter_context(nc.sbuf_tensor(n, sh, d))
            ps3a = lambda n, sh, d: ph3ab.enter_context(nc.psum_tensor(n, sh, d))
            pH = [ps3a(f"pH{i}", [128, 512], F32) for i in range(8)]
            xqfm = sb3a("xqfm", [128, 16 * R_], BF16)
            ph3a = ExitStack()
            sb3aa = lambda n, sh, d: ph3a.enter_context(nc.sbuf_tensor(n, sh, d))
            pXT = pH[0].bitcast(BF16)
            xr_db = sb3aa("xr_db", [128, D_], F32)
            st3 = sb3aa("st3", [128, 16], F32)
            xnf = sb3aa("xnf", [128, D_], F32)
            isc = sb3aa("isc", [128, D_], I32)
            xqt = sb3aa("xqt", [128, D_], BF16)
            vsc = sb3aa("vsc", [128, D_], F32)
            xqt_v = xqt[:]


            def wm_fin(e):
                e.tensor_reduce(alsc[:, 40:41], wmp[:, 0:128], AX, OP.add)
                return e.tensor_reduce(alsc[:, 41:42], wmp[:, 128:256], AX, OP.add)
            v_wm = p.emi("vector", wm_fin, "sV")
            p.wait("tensor", "sV", v_wm)
            p.wait("tensor", "sIdent", ld_ident)
            t_wm = p.emi("tensor", lambda e:
                         e.matmul(pH[0][0:1, 0:2], ones_f[:], alsc[:, 40:42],
                                  start=True, stop=True), "sT")
            p.wait("vector", "sT", t_wm)
            def wrow_math(e):
                e.tensor_copy(wrow3[0:1, 0:2], pH[0][0:1, 0:2])
                e.engine_nop()
                e.tensor_single_scalar(wrow3[0:1, 0:2], wrow3[0:1, 0:2],
                                       1.0 / (D_ * F_), OP.mult)
                e.engine_nop()
                e.tensor_single_scalar(wrow3[0:1, 0:2], wrow3[0:1, 0:2], EPS, OP.max)
                e.engine_nop()
                e.tensor_single_scalar(wrow3[0:1, 2:4], wrow3[0:1, 0:2], 0.5, OP.mult)
                e.engine_nop()
                e.tensor_single_scalar(wrow3[0:1, 4:6], wrow3[0:1, 2:4], -1.0, OP.mult)
                e.engine_nop()
                return e.tensor_single_scalar(wrow3[0:1, 6:8], wrow3[0:1, 0:2],
                                              1.0 / 127.0, OP.mult)
            v_wr = p.emi("vector", wrow_math, "sV")
            p.wait("sync", "sV", v_wr)
            ws_st = p.dma("sync", wstat_d[:, :], wrow3[0:1, :], "sLam")
            p.wait("sync", "sLam", ws_st)
            ws_bc = p.dma("sync", wstat[:, :], wstat_d[:, :].partition_broadcast(128), "sLam")

            p.wait("sync", "sV", mean_V)
            x3_user = 0
            st3_user = 0
            xq_tr_ev = 0
            for tt in range(8):
                p.wait("sync", "sV", x3_user)
                xl = p.dma("sync", xr_db[:], xr_in[128 * tt:128 * tt + 128, :], "sX0")
                st = st3[:, 0:8]
                p.wait("scalar", "sX0", xl)
                p.wait("scalar", "sV", st3_user if not SERIAL_ACT else p.val("sV"))
                a_sq = p.emi("scalar", lambda e:
                             e.activation(xnf[:], xr_db[:], AF.Square,
                                          accum_out=st3[:, 1:2]), "sA")
                p.wait("vector", "sX0", xl)
                p.emi("vector", lambda e: e.tensor_reduce(st3[:, 0:1], xr_db[:], AX, OP.add))
                p.wait("vector", "sA", a_sq)
                def ln3(e):
                    e.engine_nop()
                    e.tensor_single_scalar(st3[:, 2:3], st3[:, 0:1], 1.0 / D_, OP.mult)
                    e.tensor_single_scalar(st3[:, 3:4], st3[:, 1:2], 1.0 / D_, OP.mult)
                    e.tensor_tensor(st3[:, 4:5], st3[:, 2:3], st3[:, 2:3], OP.mult)
                    e.engine_nop()
                    e.tensor_tensor(st3[:, 5:6], st3[:, 3:4], st3[:, 4:5], OP.subtract)
                    e.engine_nop()
                    return e.tensor_single_scalar(st3[:, 5:6], st3[:, 5:6], EPS, OP.add)
                v_st = p.emi("vector", ln3, "sV")
                st3_user = v_st
                p.wait("scalar", "sV", v_st)
                p.emi("scalar", lambda e: e.activation(st3[:, 6:7], st3[:, 5:6], AF.Ln))
                a_r = p.emi("scalar", lambda e:
                            e.activation(st3[:, 7:8], st3[:, 6:7], AF.Exp,
                                         scale=consts[:, 3:4]), "sA")
                p.wait("vector", "sA", a_r)
                def mr3(e):
                    e.scalar_tensor_tensor(st3[:, 4:5], st3[:, 2:3], -1.0, st3[:, 7:8],
                                           OP.mult, OP.mult)
                    return e.tensor_copy(st3[:, 6:7], st3[:, 7:8])
                v_mr = p.emi("vector", mr3, "sV")
                p.wait("scalar", "sV", v_mr)
                a_xn = p.emi("scalar", lambda e:
                             e.activation(xnf[:], xr_db[:], AF.Identity,
                                          bias=st3[:, 4:5], scale=st3[:, 6:7]), "sA")
                p.wait("vector", "sA", a_xn)
                def qsc(e, tt=tt):
                    e.tensor_reduce(st3[:, 8:9], xnf[:], AX, OP.max,
                                    apply_absolute_value=True)
                    for _ in range(4):
                        e.engine_nop()
                    e.tensor_single_scalar(alph[:, tt:tt + 1], st3[:, 8:9], EPS, OP.max)
                    for _ in range(4):
                        e.engine_nop()
                    e.reciprocal(st3[:, 9:10], alph[:, tt:tt + 1])
                    for _ in range(4):
                        e.engine_nop()
                    return e.tensor_single_scalar(alsc[:, tt:tt + 1], st3[:, 9:10],
                                                  127.0, OP.mult)
                v_qs = p.emi("vector", qsc, "sV")
                def qint(e, tt=tt):
                    for _ in range(4):
                        e.engine_nop()
                    e.tensor_scalar(vsc[:], xnf[:], alsc[:, tt:tt + 1], None,
                                    OP.mult)
                    e.tensor_copy(isc[:], vsc[:])
                    return e.tensor_copy(xqt_v[:, 0:D_], isc[:])
                v_qi = p.emi("vector", qint, "sV")
                x3_user = v_qi
                if tt == 0 and "d_ffn" in dbg_t and "d_xq" in dbg_t:
                    p.wait("gpsimd", "sV", v_qi)
                    p.dma("gpsimd", dbg_t["d_ffn"][0:128, :], vsc[:], "sG")
                    dqq = p.dma("gpsimd", dbg_t["d_ffn"][128:256, :], xqt[:], "sG")
                    p.wait("tensor", "sG", dqq)
                    p.wait("vector", "sG", dqq)
                p.wait("tensor", "sV", v_qi)
                for dc in range(16):
                    t_tr = p.emi("tensor", lambda e, dc=dc:
                                 e.transpose(pXT[:, 0:128],
                                             xqt_v[:, 128 * dc:128 * dc + 128],
                                             ident[:]), "sT")
                    p.wait("vector", "sT", t_tr)
                    xq_tr_ev = p.emi(
                        "vector", lambda e, dc=dc, tt=tt:
                        e.tensor_copy(xqfm[:, R_ * dc + 128 * tt:R_ * dc + 128 * tt + 128],
                                      pXT[:, 0:128]), "sV")
                    p.wait("tensor", "sV", xq_tr_ev)
            if "d_xq" in dbg_t:
                p.wait("gpsimd", "sV", p.val("sV"))
                p.dma("gpsimd", dbg_t["d_xq"][0:128, :], xqfm[:, 0:D_], "sG")
                dxq_v = p.dma("gpsimd", dbg_t["d_xq"][128:256, :], xqfm[:, D_:2 * D_], "sG")
                p.wait("vector", "sG", dxq_v)
            p.wait("vector", "sLam", ws_bc)
            v_a1c = p.emi("vector", lambda e:
                          e.tensor_scalar(alsc[:, 8:16], alph[:, 0:8],
                                          wstat[:, 6:7], None, OP.mult), "sV")
            ph3a_A = p.val("sA")
            ph3a_V = p.val("sV")
            ph3a.close()
            ph3b = ExitStack()
            sb3b = lambda n, sh, d: ph3b.enter_context(nc.sbuf_tensor(n, sh, d))
            w1tile = [sb3b(f"w1tile{i}", [128, 512], F32) for i in range(4)]
            tqs = sb3b("tqs", [128, 512], BF16)
            sqb = sb3b("sqb", [128, 512], BF16)
            w1q = [sb3b(f"w1q{i}", [128, 512], BF16) for i in range(4)]
            b1b = [sb3b(f"b1b{i}", [128, 512], F32) for i in range(2)]
            p.wait("sync", "sA", ph3a_A)
            p.wait("sync", "sV", ph3a_V)
            p.wait("gpsimd", "sA", ph3a_A)
            p.wait("gpsimd", "sV", ph3a_V)

            wsem = ["sW10", "sW11", "sW12", "sW13"]
            wqeng = ["sync", "sync", "gpsimd", "gpsimd"]
            w1t_user = [0, 0, 0, 0]
            w1q_user = [0, 0, 0, 0]
            b1_user = [0, 0]
            pH_gelu = [0] * 8
            for fs in range(16):
                sp = fs % 2
                p.wait("sync", "sV", b1_user[sp])
                bl = p.dma("sync", b1b[sp][:],
                           b1_in[0:1, 512 * fs:512 * fs + 512].partition_broadcast(128),
                           "sB0")
                t_mm = None
                for dc in range(16):
                    db = dc % 4
                    p.wait(wqeng[db], "sV", w1t_user[db])
                    wl = p.dma(wqeng[db], w1tile[db][:],
                               w1t_in[128 * dc:128 * dc + 128, 512 * fs:512 * fs + 512],
                               wsem[db])
                    p.wait("vector", wsem[db], wl)
                    p.wait("vector", "sT", w1q_user[db])
                    p.wait("vector", "sLam", ws_bc)
                    def wq1(e, db=db):
                        e.tensor_scalar(tqs[:], w1tile[db][:], wstat[:, 4:5], None,
                                        OP.is_lt)
                        return e.scalar_tensor_tensor(w1q[db][:], w1tile[db][:],
                                                      wstat[:, 2:3], tqs[:],
                                                      OP.is_gt, OP.subtract)
                    q_v = p.emi("vector", wq1, "sV")
                    w1t_user[db] = q_v
                    p.wait("tensor", "sV", q_v)
                    if fs == 0 and dc == 0:
                        p.wait("tensor", "sV", xq_tr_ev)
                    for tt in range(8):
                        if dc == 0:
                            p.wait("tensor", "sA", pH_gelu[tt])
                        t_mm = p.emi(
                            "tensor", lambda e, dc=dc, tt=tt, db=db:
                            e.matmul(pH[tt][:],
                                     xqfm[:, R_ * dc + 128 * tt:R_ * dc + 128 * tt + 128],
                                     w1q[db][:], start=(dc == 0), stop=(dc == 15)),
                            "sT" if tt == 7 else None)
                    w1q_user[db] = t_mm
                if fs == 0 and "d_alpha" in dbg_t:
                    p.wait("gpsimd", "sV", w1t_user[0])
                    p.dma("gpsimd", dbg_t["d_alpha"][:, :], alsc[:, 0:8], "sG")
                p.wait("vector", "sT", t_mm)
                p.wait("vector", "sV", v_a1c)
                p.wait("vector", "sB0", bl)
                for tt in range(8):
                    v_ev = p.emi("vector", lambda e, tt=tt, sp=sp:
                                 e.scalar_tensor_tensor(pH[tt][:], pH[tt][:],
                                                        alsc[:, 8 + tt:9 + tt],
                                                        b1b[sp][:],
                                                        OP.mult, OP.add), "sV")
                    b1_user[sp] = v_ev
                    p.wait("scalar", "sV", v_ev)
                    pH_gelu[tt] = p.emi(
                        "scalar", lambda e, tt=tt, fs=fs:
                        e.activation(g_sb[:, F_ * tt + 512 * fs:F_ * tt + 512 * fs + 512],
                                     pH[tt][:], AF.Gelu,
                                     accum_out=gsums[:, 16 * tt + fs:16 * tt + fs + 1]),
                        "sA")
                    p.emi("scalar", lambda e, tt=tt, fs=fs:
                          e.activation(sqb[:],
                                       g_sb[:, F_ * tt + 512 * fs:F_ * tt + 512 * fs + 512],
                                       AF.Square,
                                       accum_out=gssqs[:, 16 * tt + fs:16 * tt + fs + 1]),
                          "sA")

            ph3b_A = p.val("sA")
            ph3b_V = p.val("sV")
            ph3b_T = p.val("sT")
            if "d_hq" in dbg_t:
                gdump = sb3b("gdump", [128, 512], F32)
                dg_v = 0
                for tt in range(8):
                    for hh in range(F_ // 512):
                        p.wait("vector", "sA", ph3b_A)
                        cv = p.emi("vector", lambda e, tt=tt, hh=hh: e.tensor_copy(
                            gdump[:], g_sb[:, F_ * tt + 512 * hh:F_ * tt + 512 * hh + 512]),
                            "sV")
                        p.wait("gpsimd", "sV", cv)
                        dg_v = p.dma("gpsimd",
                                     dbg_t["d_hq"][128 * tt:128 * tt + 128,
                                                   512 * hh:512 * hh + 512],
                                     gdump[:], "sG")
                        p.wait("vector", "sG", dg_v)
            ph3b.close()
            ph3ab.close()

            ph3c = ExitStack()
            sb3c = lambda n, sh, d: ph3c.enter_context(nc.sbuf_tensor(n, sh, d))
            ps3c = lambda n, sh, d: ph3c.enter_context(nc.psum_tensor(n, sh, d))
            gamb = [sb3c(f"gamb{i}", [128, 512], F32) for i in range(2)]
            betb = [sb3c(f"betb{i}", [128, 512], F32) for i in range(2)]
            vsc3 = sb3c("vsc3", [128, F_ // 2], F32)
            isc3 = sb3c("isc3", [128, F_ // 2], I32)
            hqt_scr = sb3c("hqt_scr", [128, F_ // 2], F32)
            st3c = sb3c("st3c", [128, 16], F32)
            pXT2 = [ps3c(f"pXT2_{i}", [128, 512], F32) for i in range(2)]

            p.wait("sync", "sT", ph3b_T)
            p.wait("sync", "sV", ph3b_V)
            p.wait("sync", "sA", ph3b_A)
            p.wait("vector", "sA", ph3b_A)
            t1_v = 0
            for tt in range(8):
                def lnstats(e, tt=tt):
                    e.tensor_reduce(st3c[:, 0:1], gsums[:, 16 * tt:16 * tt + 16], AX, OP.add)
                    e.tensor_reduce(st3c[:, 1:2], gssqs[:, 16 * tt:16 * tt + 16], AX, OP.add)
                    for _ in range(4):
                        e.engine_nop()
                    e.tensor_single_scalar(st3c[:, 2:3], st3c[:, 0:1], 1.0 / F_, OP.mult)
                    e.tensor_single_scalar(st3c[:, 3:4], st3c[:, 1:2], 1.0 / F_, OP.mult)
                    e.tensor_tensor(st3c[:, 4:5], st3c[:, 2:3], st3c[:, 2:3], OP.mult)
                    e.engine_nop()
                    e.tensor_tensor(st3c[:, 5:6], st3c[:, 3:4], st3c[:, 4:5], OP.subtract)
                    e.engine_nop()
                    return e.tensor_single_scalar(st3c[:, 5:6], st3c[:, 5:6], EPS, OP.add)
                v_s = p.emi("vector", lnstats, "sV")
                p.wait("scalar", "sV", v_s)
                p.emi("scalar", lambda e: e.activation(st3c[:, 6:7], st3c[:, 5:6], AF.Ln))
                a_r = p.emi("scalar", lambda e:
                            e.activation(st3c[:, 7:8], st3c[:, 6:7], AF.Exp,
                                         scale=consts[:, 3:4]), "sA")
                p.wait("vector", "sA", a_r)
                t1_v = p.emi("vector", lambda e, tt=tt:
                             e.tensor_scalar(g_sb[:, F_ * tt:F_ * tt + F_],
                                             g_sb[:, F_ * tt:F_ * tt + F_],
                                             st3c[:, 2:3], st3c[:, 7:8],
                                             OP.subtract, OP.mult), "sV")
            gb_user = [0, 0]
            gb_v = 0
            for fs in range(16):
                gp = fs % 2
                p.wait("sync", "sV", gb_user[gp])
                gl = p.dma("sync", gamb[gp][:],
                           gam_in[0:1, 512 * fs:512 * fs + 512].partition_broadcast(128),
                           "sB0")
                gl2 = p.dma("sync", betb[gp][:],
                            bet_in[0:1, 512 * fs:512 * fs + 512].partition_broadcast(128),
                            "sB1")
                p.wait("vector", "sB0", gl)
                p.wait("vector", "sB1", gl2)
                p.wait("vector", "sV", t1_v)
                for tt in range(8):
                    sl = slice(F_ * tt + 512 * fs, F_ * tt + 512 * fs + 512)
                    p.emi("vector", lambda e, sl=sl, gp=gp:
                          e.tensor_tensor(g_sb[:, sl], g_sb[:, sl], gamb[gp][:], OP.mult))
                    gb_v = p.emi("vector", lambda e, sl=sl, gp=gp:
                                 e.tensor_tensor(g_sb[:, sl], g_sb[:, sl], betb[gp][:],
                                                 OP.add), "sV")
                gb_user[gp] = gb_v
            if "d_3c" in dbg_t:
                gd3 = sb3c("gd3", [128, 512], F32)
            cp_ev = [0] * 2
            tr_last = 0
            v_q2b = 0
            for tt in range(8):
                p.wait("vector", "sV", gb_v)
                if tt == 0 and "d_3c" in dbg_t:
                    dga = 0
                    for cc in range(16):
                        p.wait("vector", "sG", dga)
                        cva = p.emi("vector", lambda e, cc=cc: e.tensor_copy(
                            gd3[:], g_sb[:, 512 * cc:512 * cc + 512]), "sV")
                        p.wait("gpsimd", "sV", cva)
                        dga = p.dma("gpsimd",
                                    dbg_t["d_3c"][0:128, 512 * cc:512 * cc + 512],
                                    gd3[:], "sG")
                    p.wait("vector", "sG", dga)
                def qsc2(e, tt=tt):
                    e.tensor_reduce(st3c[:, 8:9], g_sb[:, F_ * tt:F_ * tt + F_], AX,
                                    OP.max, apply_absolute_value=True)
                    for _ in range(4):
                        e.engine_nop()
                    e.tensor_single_scalar(alsc[:, 16 + tt:17 + tt], st3c[:, 8:9],
                                           EPS, OP.max)
                    for _ in range(4):
                        e.engine_nop()
                    e.reciprocal(st3c[:, 9:10], alsc[:, 16 + tt:17 + tt])
                    for _ in range(4):
                        e.engine_nop()
                    e.tensor_single_scalar(alsc[:, 24 + tt:25 + tt], st3c[:, 9:10],
                                           127.0, OP.mult)
                    for _ in range(4):
                        e.engine_nop()
                    return e.tensor_scalar(alsc[:, 32 + tt:33 + tt],
                                           alsc[:, 16 + tt:17 + tt],
                                           wstat[:, 7:8], None, OP.mult)
                v_q2 = p.emi("vector", qsc2, "sV")
                for hh in range(2):
                    hof = (F_ // 2) * hh
                    p.wait("scalar", "sV", max(v_q2, v_q2b))
                    a_v2 = p.emi("scalar", lambda e, tt=tt, hof=hof:
                                 e.activation(vsc3[:],
                                              g_sb[:, F_ * tt + hof:F_ * tt + hof + F_ // 2],
                                              AF.Identity,
                                              scale=alsc[:, 24 + tt:25 + tt]), "sA")
                    p.wait("vector", "sA", a_v2)
                    def qint2(e):
                        e.tensor_copy(isc3[:], vsc3[:])
                        return e.tensor_copy(hqt_scr[:], isc3[:])
                    p.wait("vector", "sT", tr_last)
                    v_q2b = p.emi("vector", qint2, "sV")
                    p.wait("tensor", "sV", v_q2b)
                    for fc2 in range(32):
                        fc = 32 * hh + fc2
                        slt = fc2 % 2
                        bk, off = slt, 0
                        p.wait("tensor", "sA", cp_ev[slt])
                        t_tr = p.emi("tensor", lambda e, fc2=fc2, bk=bk, off=off:
                                     e.transpose(pXT2[bk][:, off:off + 128],
                                                 hqt_scr[:, 128 * fc2:128 * fc2 + 128],
                                                 identf[:]), "sT")
                        p.wait("scalar", "sT", t_tr)
                        cp_ev[slt] = p.emi(
                            "scalar", lambda e, fc=fc, tt=tt, bk=bk, off=off:
                            e.activation(
                                g_sb[:, F_ * tt + 128 * fc:F_ * tt + 128 * fc + 128],
                                pXT2[bk][:, off:off + 128], AF.Identity), "sA")
                    tr_last = t_tr
                    if tt == 0 and hh == 0 and "d_3c" in dbg_t:
                        p.wait("gpsimd", "sV", v_q2b)
                        dqh = p.dma("gpsimd", dbg_t["d_3c"][128:256, 0:F_ // 2],
                                    hqt_scr[:], "sG")
                        p.wait("vector", "sG", dqh)
                        p.wait("tensor", "sG", dqh)

            hq_done = max(cp_ev)
            ph3c_V = p.val("sV")
            ph3c_T = p.val("sT")
            ph3c_A = p.val("sA")
            ph3c.close()

            ph3d = ExitStack()
            sb3d = lambda n, sh, d: ph3d.enter_context(nc.sbuf_tensor(n, sh, d))
            ps3d = lambda n, sh, d: ph3d.enter_context(nc.psum_tensor(n, sh, d))
            w2tile = [sb3d(f"w2tile{i}", [128, 512], F32) for i in range(4)]
            fbuf = [sb3d(f"fbuf{i}", [128, 512], BF16) for i in range(2)]
            tqs2 = sb3d("tqs2", [128, 512], BF16)
            w2q = [sb3d(f"w2q{i}", [128, 512], BF16) for i in range(4)]
            b2b = [sb3d(f"b2b{i}", [128, 512], F32) for i in range(2)]
            pM2 = [ps3d(f"pM2_{i}", [128, 512], F32) for i in range(8)]

            p.wait("sync", "sT", ph3c_T)
            p.wait("sync", "sV", ph3c_V)
            p.wait("sync", "sA", ph3c_A)
            p.wait("gpsimd", "sT", ph3c_T)
            p.wait("gpsimd", "sV", ph3c_V)
            p.wait("gpsimd", "sA", ph3c_A)
            w2sem = ["sW20", "sW21", "sW22", "sW23"]
            w2eng = ["sync", "sync", "gpsimd", "gpsimd"]
            w2t_user = [0, 0, 0, 0]
            w2q_user = [0, 0, 0, 0]
            b2_user = [0, 0]
            pM_user = 0
            for ds in range(4):
                sp = ds % 2
                p.wait("sync", "sV", b2_user[sp])
                b2l = p.dma("sync", b2b[sp][:],
                            b2_in[0:1, 512 * ds:512 * ds + 512].partition_broadcast(128),
                            "sB0")
                t_mm2 = None
                for fc in range(64):
                    db = fc % 4
                    p.wait(w2eng[db], "sV", w2t_user[db])
                    wl = p.dma(w2eng[db], w2tile[db][:],
                               w2t_in[128 * fc:128 * fc + 128, 512 * ds:512 * ds + 512],
                               w2sem[db])
                    p.wait("vector", w2sem[db], wl)
                    p.wait("vector", "sT", w2q_user[db])
                    def wq2(e, db=db):
                        e.tensor_scalar(tqs2[:], w2tile[db][:], wstat[:, 5:6], None,
                                        OP.is_lt)
                        return e.scalar_tensor_tensor(w2q[db][:], w2tile[db][:],
                                                      wstat[:, 3:4], tqs2[:],
                                                      OP.is_gt, OP.subtract)
                    q_v = p.emi("vector", wq2, "sV")
                    w2t_user[db] = q_v
                    p.wait("tensor", "sV", q_v)
                    if fc == 0:
                        p.wait("tensor", "sV", pM_user)
                        p.wait("tensor", "sA", hq_done)
                    for tt in range(8):
                        t_mm2 = p.emi(
                            "tensor", lambda e, tt=tt, fc=fc, db=db:
                            e.matmul(pM2[tt][:],
                                     g_sb[:, F_ * tt + 128 * fc:F_ * tt + 128 * fc + 128],
                                     w2q[db][:], start=(fc == 0), stop=(fc == 63)),
                            "sT" if tt == 7 else None)
                    w2q_user[db] = t_mm2
                p.wait("vector", "sT", t_mm2)
                p.wait("vector", "sB0", b2l)
                fb_user = [0, 0]
                for tt in range(8):
                    fb = tt % 2
                    p.wait("vector", "sSt", fb_user[fb])
                    pM_user = p.emi(
                        "vector", lambda e, tt=tt, fb=fb, sp=sp:
                        e.scalar_tensor_tensor(
                            fbuf[fb][:],
                            pM2[tt][:], alsc[:, 32 + tt:33 + tt], b2b[sp][:],
                            OP.mult, OP.add), "sV")
                    p.wait("sync", "sV", pM_user)
                    fb_user[fb] = p.dma(
                        "sync", ffn_d[128 * tt:128 * tt + 128, 512 * ds:512 * ds + 512],
                        fbuf[fb][:], "sSt")
                b2_user[sp] = pM_user
                ffn_st = fb_user[1]

            ph3d_V = p.val("sV")
            if "d_g" in dbg_t:
                p.wait("gpsimd", "sV", ph3d_V)
                dgv = 0
                for tt in range(8):
                    dgv = p.dma("gpsimd", dbg_t["d_g"][128 * tt:128 * tt + 128, :],
                                g_sb[:, F_ * tt:F_ * tt + F_], "sG")
                p.wait("sync", "sG", dgv)
                p.wait("vector", "sG", dgv)
            ph3d.close()
            ph3g.close()
            if "d_ffn" in dbg_t:
                p.wait("gpsimd", "sSt", ffn_st)
                fscr = ph3p.enter_context(nc.sbuf_tensor("fscr", [128, D_], BF16))
                fscr2 = ph3p.enter_context(nc.sbuf_tensor("fscr2", [128, D_], F32))
                for tt in range(8):
                    fl2 = p.dma("gpsimd", fscr[:], ffn_d[128 * tt:128 * tt + 128, :], "sG")
                    p.wait("vector", "sG", fl2)
                    cv2 = p.emi("vector", lambda e: e.tensor_copy(fscr2[:], fscr[:]), "sV")
                    p.wait("gpsimd", "sV", cv2)
                    sv2 = p.dma("gpsimd", dbg_t["d_ffn"][128 * tt:128 * tt + 128, :],
                                fscr2[:], "sG")
                    p.wait("gpsimd", "sG", sv2)
                    p.wait("vector", "sG", sv2)
            if "d_thr" in dbg_t:
                p.wait("gpsimd", "sLam", ws_bc)
                p.dma("gpsimd", dbg_t["d_thr"][:, :], wstat[:, :], "sG")

            ph4 = ExitStack()
            sb4 = lambda n, sh, d: ph4.enter_context(nc.sbuf_tensor(n, sh, d))
            ps4 = lambda n, sh, d: ph4.enter_context(nc.psum_tensor(n, sh, d))
            wo_sb = sb4("wo_sb", [128, 16 * D_], BF16)
            ffn_t = [sb4(f"ffn_t{i}", [128, 512], BF16) for i in range(2)]
            afm = sb4("afm", [128, 16 * R_], BF16)
            xr4 = [sb4(f"xr4_{i}", [128, D_], F32) for i in range(2)]
            obuf = [sb4(f"obuf{i}", [128, 512], F32) for i in range(2)]
            osum = sb4("osum", [128, 512], F32)
            pWo = [ps4(f"pWo{i}", [128, 512], F32) for i in range(2)]

            p.wait("gpsimd", "sV", ph3d_V)
            wo_l = 0
            for cc in range(16):
                wo_l = p.dma("gpsimd", wo_sb[:, D_ * cc:D_ * cc + D_],
                             wo_in[128 * cc:128 * cc + 128, :], "sWo")
            p.wait("sync", "sV", ph3d_V)
            p.wait("sync", "sT", p.val("sT"))
            if collective:
                p.wait("sync", "sCC", cc_v)
            agv = p.tick("sAG", 16 * 16)
            def ag_read(e):
                rk = e.cc_rank([[0, 1, 2, 3], [4, 5, 6, 7]])
                last = None
                for cc in range(16):
                    last = e.dma_start(
                        out=afm[:, R_ * cc:R_ * cc + R_],
                        in_=ag_out_t[128 * cc:128 * cc + 128, bass.ts(rk, R_)])
                    last.then_inc(p.sems["sAG"], 16)
                return last
            p.ops["sync"].append(lambda e: ag_read(e))

            xr4_user = [0, 0]
            ob_user = [0, 0]
            ffn_u = [0, 0]
            st_v = 0
            for tt in range(8):
                db = tt % 2
                p.wait("sync", "sSt", xr4_user[db])
                xl = p.dma("sync", xr4[db][:], xr_in[128 * tt:128 * tt + 128, :],
                           ["sX0", "sX1"][db])
                for os in range(4):
                    ob = os % 2
                    p.wait("sync", "sV", ffn_u[ob] if True else 0)
                    fl_ = p.dma("sync", ffn_t[ob][:],
                                ffn_d[128 * tt:128 * tt + 128, 512 * os:512 * os + 512],
                                ["sW10", "sW11"][ob])
                    p.wait("tensor", "sWo", wo_l)
                    p.wait("tensor", "sAG", agv)
                    p.wait("tensor", "sV", ob_user[ob])
                    t_wo = None
                    for cc in range(16):
                        t_wo = p.emi(
                            "tensor", lambda e, cc=cc, tt=tt, os=os, ob=ob:
                            e.matmul(pWo[ob][:],
                                     afm[:, R_ * cc + 128 * tt:R_ * cc + 128 * tt + 128],
                                     wo_sb[:, D_ * cc + 512 * os:D_ * cc + 512 * os + 512],
                                     start=(cc == 0), stop=(cc == 15)),
                            "sT" if cc == 15 else None)
                    p.wait("vector", "sT", t_wo)
                    p.wait("vector", ["sX0", "sX1"][db], xl)
                    p.wait("vector", ["sW10", "sW11"][ob], fl_)
                    def fin(e, tt=tt, os=os, ob=ob, db=db):
                        e.tensor_tensor(osum[:], pWo[ob][:], ffn_t[ob][:], OP.add)
                        return e.tensor_tensor(obuf[ob][:], osum[:],
                                               xr4[db][:, 512 * os:512 * os + 512], OP.add)
                    v_f = p.emi("vector", fin, "sV")
                    ob_user[ob] = v_f
                    ffn_u[ob] = v_f
                    p.wait("sync", "sV", v_f)
                    st_v = p.dma("sync", out_t[128 * tt:128 * tt + 128,
                                               512 * os:512 * os + 512],
                                 obuf[ob][:], "sSt")
                    p.wait("vector", "sSt", st_v)
                xr4_user[db] = st_v
            ph4.close()
            ph3p.close()
        st_total = p.val("sSt")
        if st_total:
            p.wait("gpsimd", "sSt", st_total)
        if cc_v:
            p.wait("gpsimd", "sCC", cc_v)

        with nc.Block() as block:
            @block.sync
            def _(e):
                p.run(e, "sync")

            @block.tensor
            def _(e):
                p.run(e, "tensor")

            @block.vector
            def _(e):
                p.run(e, "vector")

            @block.scalar
            def _(e):
                p.run(e, "scalar")

            @block.gpsimd
            def _(e):
                p.run(e, "gpsimd")

    return nc


def _shard_inputs(inputs):
    x = np.asarray(inputs["L0_attn_x"], dtype=np.float32)
    Wq = np.asarray(inputs["L0_attn_Wq"], dtype=np.float32)
    Wk = np.asarray(inputs["L0_attn_Wk"], dtype=np.float32)
    Wv = np.asarray(inputs["L0_attn_Wv"], dtype=np.float32)
    Wo = np.asarray(inputs["L0_attn_Wo"], dtype=np.float32)
    w1 = np.asarray(inputs["L0_ffn_w1"], dtype=np.float32)
    w2 = np.asarray(inputs["L0_ffn_w2"], dtype=np.float32)
    import ml_dtypes
    ident = np.eye(128, dtype=ml_dtypes.bfloat16)
    masks = np.zeros((4, 128, 512), dtype=np.float32)
    tri = np.tril(np.full((128, 128), NEG, dtype=np.float32), -1)
    for pvar in range(4):
        for j in range(4):
            if j < pvar:
                masks[pvar][:, 128 * j:128 * j + 128] = NEG
            elif j == pvar:
                masks[pvar][:, 128 * j:128 * j + 128] = tri
    masks = masks.reshape(4 * 128, 512)
    w1t = np.ascontiguousarray(w1.T)
    w2t = np.ascontiguousarray(w2.T)
    in_maps = []
    for c in range(8):
        s, g = c // 4, c % 4
        m = {
            "x": np.ascontiguousarray(x[s]),
            "xr": np.ascontiguousarray(x[s][1024 * g:1024 * g + 1024]),
            "wq": np.ascontiguousarray(Wq[:, 512 * g:512 * g + 512]),
            "wk": np.ascontiguousarray(Wk[:, 512 * g:512 * g + 512]),
            "wv": np.ascontiguousarray(Wv[:, 512 * g:512 * g + 512]),
            "wo": Wo,
            "w1t": w1t,
            "w2t": w2t,
            "b1": np.asarray(inputs["L0_ffn_b1"], dtype=np.float32).reshape(1, F_),
            "gam": np.asarray(inputs["L0_ffn_gamma"], dtype=np.float32).reshape(1, F_),
            "bet": np.asarray(inputs["L0_ffn_beta"], dtype=np.float32).reshape(1, F_),
            "b2": np.asarray(inputs["L0_ffn_b2"], dtype=np.float32).reshape(1, D_),
            "lq1": np.asarray(inputs["L0_attn_lq1"], dtype=np.float32).reshape(1, 64),
            "lk1": np.asarray(inputs["L0_attn_lk1"], dtype=np.float32).reshape(1, 64),
            "lq2": np.asarray(inputs["L0_attn_lq2"], dtype=np.float32).reshape(1, 64),
            "lk2": np.asarray(inputs["L0_attn_lk2"], dtype=np.float32).reshape(1, 64),
            "ident": ident,
            "identf": np.eye(128, dtype=np.float32),
            "masks": masks,
        }
        in_maps.append(m)
    return in_maps


def kernel(**inputs):
    in_maps = _shard_inputs(inputs)
    nc = build_nc(dbg=DBG, phases=PHASES)
    res = run_bass_kernel_spmd(nc, in_maps, core_ids=list(range(8)))
    out = np.empty((2, S_, D_), dtype=np.float32)
    for c in range(8):
        s, g = c // 4, c % 4
        out[s, R_ * g:R_ * g + R_] = res.results[c]["out"]
    return out


# version 11
# speedup vs baseline: 1.4176x; 1.2564x over previous
import math
import numpy as np
from contextlib import ExitStack

import concourse.bass as bass
import concourse.mybir as mybir
from concourse.bass_utils import run_bass_kernel_spmd
from concourse.alu_op_type import AluOpType

F32 = mybir.dt.float32
BF16 = mybir.dt.bfloat16
I32 = mybir.dt.int32
AF = mybir.ActivationFunctionType
AX = mybir.AxisListType.X
OP = AluOpType

S_ = 4096
D_ = 2048
F_ = 8192
R_ = 1024
NU = 4
LAMBDA_INIT = 0.8 - 0.6 * math.exp(-0.3 * 12)
CSH = 12.0
EPS = 1e-5
NEG = -1.0e9

PHASES = (1, 2, 3, 4)
DBG = ()
NKCAP = None
SERIAL1 = False
SERIAL_ACT = True


class Prog:
    ENGINES = ("sync", "tensor", "vector", "scalar", "gpsimd")

    def __init__(self):
        self.ops = {e: [] for e in self.ENGINES}
        self.cnt = {}
        self.sems = {}
        self.last_wait = {e: {} for e in self.ENGINES}

    def tick(self, sem, n=1):
        self.cnt[sem] = self.cnt.get(sem, 0) + n
        return self.cnt[sem]

    def val(self, sem):
        return self.cnt.get(sem, 0)

    def emi(self, eng, fn, sem=None, n=1):
        if sem is None:
            self.ops[eng].append(lambda e, fn=fn: fn(e))
            return None
        v = self.tick(sem, n)
        def run(e, fn=fn, sem=sem, n=n):
            fn(e).then_inc(self.sems[sem], n)
        self.ops[eng].append(run)
        return v

    def wait(self, eng, sem, v):
        if not v or v <= 0:
            return
        if self.last_wait[eng].get(sem, 0) >= v:
            return
        self.last_wait[eng][sem] = v
        self.ops[eng].append(lambda e, sem=sem, v=v: e.wait_ge(self.sems[sem], v))

    def dma(self, qeng, dst, src, sem, **kw):
        return self.emi(qeng, lambda e, d=dst, s=src, kw=kw:
                        e.dma_start(out=d, in_=s, **kw), sem, 16)

    def run(self, eng_obj, name):
        for fn in self.ops[name]:
            fn(eng_obj)


def build_nc(dbg=(), phases=PHASES, collective=True):
    nc = bass.Bass("TRN2", target_bir_lowering=False, detect_race_conditions=False)
    ap = lambda t: t.ap()

    x_in = ap(nc.dram_tensor("x", [S_, D_], F32, kind="ExternalInput"))
    xr_in = ap(nc.dram_tensor("xr", [R_, D_], F32, kind="ExternalInput"))
    wstat_d = None
    wq_in = ap(nc.dram_tensor("wq", [D_, 512], F32, kind="ExternalInput"))
    wk_in = ap(nc.dram_tensor("wk", [D_, 512], F32, kind="ExternalInput"))
    wv_in = ap(nc.dram_tensor("wv", [D_, 512], F32, kind="ExternalInput"))
    wo_in = ap(nc.dram_tensor("wo", [D_, D_], F32, kind="ExternalInput"))
    w1t_in = ap(nc.dram_tensor("w1t", [D_, F_], F32, kind="ExternalInput"))
    w2t_in = ap(nc.dram_tensor("w2t", [F_, D_], F32, kind="ExternalInput"))
    b1_in = ap(nc.dram_tensor("b1", [1, F_], F32, kind="ExternalInput"))
    gam_in = ap(nc.dram_tensor("gam", [1, F_], F32, kind="ExternalInput"))
    bet_in = ap(nc.dram_tensor("bet", [1, F_], F32, kind="ExternalInput"))
    b2_in = ap(nc.dram_tensor("b2", [1, D_], F32, kind="ExternalInput"))
    lq1_in = ap(nc.dram_tensor("lq1", [1, 64], F32, kind="ExternalInput"))
    lk1_in = ap(nc.dram_tensor("lk1", [1, 64], F32, kind="ExternalInput"))
    lq2_in = ap(nc.dram_tensor("lq2", [1, 64], F32, kind="ExternalInput"))
    lk2_in = ap(nc.dram_tensor("lk2", [1, 64], F32, kind="ExternalInput"))
    ident_in = ap(nc.dram_tensor("ident", [128, 128], BF16, kind="ExternalInput"))
    identf_in = ap(nc.dram_tensor("identf", [128, 128], F32, kind="ExternalInput"))
    masks_in = ap(nc.dram_tensor("masks", [4 * 128, 512], F32, kind="ExternalInput"))

    out_t = ap(nc.dram_tensor("out", [R_, D_], F32, kind="ExternalOutput"))

    lam_d = ap(nc.dram_tensor("lam_d", [1, 1], F32))
    wstat_d = ap(nc.dram_tensor("wstat_d", [1, 8], F32))
    ffn_d = ap(nc.dram_tensor("ffn_d", [R_, D_], BF16))
    ag_in_t = ap(nc.dram_tensor("ag_in", [512, S_], BF16))
    ag_out_t = ap(nc.dram_tensor("ag_out", [4 * 512, S_], BF16))

    dbg_t = {}
    for name, shape in [
        ("d_qfm", [512, S_]), ("d_kfm", [512, S_]), ("d_v", [S_, 512]),
        ("d_lam", [128, 1]), ("d_attn", [512, S_]), ("d_ag", [2048, S_]),
        ("d_xq", [R_, D_]), ("d_alpha", [128, 8]), ("d_g", [R_, F_]),
        ("d_hq", [R_, F_]), ("d_ffn", [R_, D_]), ("d_thr", [128, 8]),
        ("d_p1", [128, 512]), ("d_p2", [128, 512]), ("d_l", [1, 1024]),
        ("d_xn", [S_, D_]), ("d_st", [128, 16]),
        ("d_s1", [128, 512]), ("d_p1k0", [128, 512]),
        ("d_a1", [128, 512]), ("d_a2", [128, 512]), ("d_a1t", [128, 512]),
        ("d_comb", [128, 512]), ("d_rc", [128, 12]), ("d_linv", [128, 8]),
        ("d_3c", [384, 8192]),
    ]:
        if name in dbg:
            dbg_t[name] = ap(nc.dram_tensor(name, shape, F32, kind="ExternalOutput"))

    p = Prog()

    ctx = ExitStack()
    with ctx:
        sbg = lambda n, s, d, **kw: ctx.enter_context(nc.sbuf_tensor(n, s, d, **kw))
        SEMS = ("sIdent", "sLam", "sX0", "sX1", "sWp", "sMsk", "sW10", "sW11",
                "sW12", "sW13", "sW20", "sW21", "sW22", "sW23",
                "sB0", "sB1", "sWo", "sAG", "sAT", "sSt",
                "sM0", "sM1", "sT", "sV", "sA", "sG", "sCC")
        for s in SEMS:
            p.sems[s] = ctx.enter_context(nc.semaphore(s))

        ident = sbg("ident_sb", [128, 128], BF16)
        identf = sbg("identf_sb", [128, 128], F32)
        ones_f = sbg("ones_f", [128, 1], F32)
        ones_b = sbg("ones_b", [128, 128], BF16)
        lam_col = sbg("lam_col", [128, 1], F32)
        lam_sc = sbg("lam_sc", [1, 8], F32)
        consts = sbg("consts", [128, 8], F32)
        wmacc = sbg("wmacc", [128, 4], F32)
        wstat = sbg("wstat", [128, 8], F32)
        alph = sbg("alph", [128, 8], F32)
        lamv = sbg("lamv", [1, 64], F32)
        lamv2 = sbg("lamv2", [1, 64], F32)
        lamk = sbg("lamk", [1, 64], F32)
        lamk2 = sbg("lamk2", [1, 64], F32)
        ph12 = ExitStack()
        sb12 = lambda n, sh, d: ph12.enter_context(nc.sbuf_tensor(n, sh, d))
        kfm = [sb12(f"kfm{u}", [128, S_], BF16) for u in range(NU)]
        qfm = [sb12(f"qfm{u}", [128, S_], BF16) for u in range(NU)]
        v_sb = sb12("v_sb", [128, 32 * 512], BF16)

        def setup_consts(e):
            e.memset(consts[:, 0:1], 0.125)
            e.memset(consts[:, 1:2], -CSH)
            e.memset(consts[:, 2:3], 256.5)
            e.memset(consts[:, 3:4], -0.5)
            e.memset(ones_f[:], 1.0)
            e.memset(ones_b[:], 1.0)
            e.memset(wmacc[:], 0.0)
            return e.engine_nop()
        vconst = p.emi("vector", setup_consts, "sV")

        p.dma("sync", ident[:], ident_in[:, :], "sIdent")
        ld_ident = p.dma("sync", identf[:], identf_in[:, :], "sIdent")
        p.dma("sync", lamv[0:1, :], lq1_in[:, :], "sLam")
        p.dma("sync", lamk[0:1, :], lk1_in[:, :], "sLam")
        p.dma("sync", lamv2[0:1, :], lq2_in[:, :], "sLam")
        lam_ld = p.dma("sync", lamk2[0:1, :], lk2_in[:, :], "sLam")

        p.wait("vector", "sLam", lam_ld)
        def lam_dots(e):
            e.tensor_tensor(lamv[0:1, :], lamv[0:1, :], lamk[0:1, :], OP.mult)
            e.engine_nop()
            e.tensor_reduce(lam_sc[0:1, 0:1], lamv[0:1, :], AX, OP.add)
            e.tensor_tensor(lamv2[0:1, :], lamv2[0:1, :], lamk2[0:1, :], OP.mult)
            e.engine_nop()
            return e.tensor_reduce(lam_sc[0:1, 1:2], lamv2[0:1, :], AX, OP.add)
        v_dots = p.emi("vector", lam_dots, "sV")
        p.wait("scalar", "sV", v_dots)
        a_exp = p.emi("scalar", lambda e:
                      e.activation(lam_sc[0:1, 2:4], lam_sc[0:1, 0:2], AF.Exp), "sA")
        p.wait("vector", "sA", a_exp)
        def lam_fin(e):
            e.tensor_tensor(lam_sc[0:1, 4:5], lam_sc[0:1, 2:3], lam_sc[0:1, 3:4], OP.subtract)
            e.engine_nop()
            return e.tensor_single_scalar(lam_sc[0:1, 5:6], lam_sc[0:1, 4:5],
                                          LAMBDA_INIT, OP.add)
        v_lam = p.emi("vector", lam_fin, "sV")
        p.wait("sync", "sV", v_lam)
        lam_st = p.dma("sync", lam_d[:, :], lam_sc[0:1, 5:6], "sLam")
        p.wait("sync", "sLam", lam_st)
        lam_bc = p.dma("sync", lam_col[:], lam_d[:, :].partition_broadcast(128), "sLam")
        if "d_lam" in dbg_t:
            p.wait("gpsimd", "sLam", lam_bc)
            p.dma("gpsimd", dbg_t["d_lam"][:, :], lam_col[:], "sG")

        ph1 = ExitStack()
        sb1 = lambda n, s, d: ph1.enter_context(nc.sbuf_tensor(n, s, d))
        ps1 = lambda n, s, d: ph1.enter_context(nc.psum_tensor(n, s, d))
        wq_sb = sb1("wq_sb", [128, 16 * 512], BF16)
        wk_sb = sb1("wk_sb", [128, 16 * 512], BF16)
        wv_sb = sb1("wv_sb", [128, 16 * 512], BF16)
        x_db = [sb1(f"x_db{i}", [128, D_], F32) for i in range(2)]
        sq_scr = sb1("sq_scr", [128, D_], F32)
        xnb_db = [sb1(f"xnb{i}", [128, D_], BF16) for i in range(2)]
        xnfm_blk = sb1("xnfm_blk", [128, 16 * 512], BF16)
        st_sb = sb1("st_sb", [128, 16], F32)
        pQ = [ps1(f"pQ{i}", [128, 512], F32) for i in range(2)]
        pTr = [ps1(f"pTr{i}", [128, 512], BF16) for i in range(2)]

        wld = 0
        for (wsb, win) in ((wq_sb, wq_in), (wk_sb, wk_in), (wv_sb, wv_in)):
            for dc in range(16):
                wld = p.dma("gpsimd", wsb[:, 512 * dc:512 * dc + 512],
                            win[128 * dc:128 * dc + 128, :], "sWp")

        NT = S_ // 128
        x_user = [0, 0]
        x_userV = [0, 0]
        xnb_user = [0, 0]
        tr_copy = [0, 0]
        st_user = [0, 0]
        xsem = ["sX0", "sX1"]

        for t in range(NT):
            db = t % 2
            p.wait("sync", "sA", x_user[db])
            p.wait("sync", "sV", x_userV[db])
            xl = p.dma("sync", x_db[db][:], x_in[128 * t:128 * t + 128, :], xsem[db])
            st = st_sb[:, 8 * db:8 * db + 8]
            p.wait("scalar", xsem[db], xl)
            p.wait("scalar", "sV", st_user[db] if not SERIAL_ACT else p.val("sV"))
            a_sq = p.emi("scalar", lambda e, xt=x_db[db], st=st:
                         e.activation(sq_scr[:], xt[:], AF.Square, accum_out=st[:, 1:2]), "sA")
            p.wait("vector", xsem[db], xl)
            def red1(e, xt=x_db[db], st=st):
                e.tensor_reduce(st[:, 0:1], xt[:], AX, OP.add)
                for _ in range(4):
                    e.engine_nop()
                return e.engine_nop()
            p.emi("vector", red1)
            p.wait("vector", "sA", a_sq)
            def ln_stats(e, st=st):
                e.engine_nop()
                e.tensor_single_scalar(st[:, 2:3], st[:, 0:1], 1.0 / D_, OP.mult)
                e.tensor_single_scalar(st[:, 3:4], st[:, 1:2], 1.0 / D_, OP.mult)
                e.tensor_tensor(st[:, 4:5], st[:, 2:3], st[:, 2:3], OP.mult)
                e.engine_nop()
                e.tensor_tensor(st[:, 5:6], st[:, 3:4], st[:, 4:5], OP.subtract)
                e.engine_nop()
                return e.tensor_single_scalar(st[:, 5:6], st[:, 5:6], EPS, OP.add)
            v_st = p.emi("vector", ln_stats, "sV")
            st_user[db] = v_st
            x_userV[db] = v_st
            p.wait("scalar", "sV", max(v_st, vconst))
            p.emi("scalar", lambda e, st=st: e.activation(st[:, 6:7], st[:, 5:6], AF.Ln))
            a_r = p.emi("scalar", lambda e, st=st:
                        e.activation(st[:, 7:8], st[:, 6:7], AF.Exp, scale=consts[:, 3:4]), "sA")
            p.wait("vector", "sA", a_r)
            def mk_mr(e, st=st):
                e.scalar_tensor_tensor(st[:, 4:5], st[:, 2:3], -1.0, st[:, 7:8],
                                       OP.mult, OP.mult)
                return e.tensor_copy(st[:, 6:7], st[:, 7:8])
            v_mr = p.emi("vector", mk_mr, "sV")
            p.wait("scalar", "sV", v_mr)
            p.wait("scalar", "sT", xnb_user[db])
            a_xn = p.emi("scalar", lambda e, xt=x_db[db], st=st, ot=xnb_db[db]:
                         e.activation(ot[:], xt[:], AF.Identity,
                                      bias=st[:, 4:5], scale=st[:, 6:7]), "sA")
            x_user[db] = a_xn

            if "d_xn" in dbg_t:
                p.wait("gpsimd", "sA", a_xn)
                dxn = p.dma("gpsimd", dbg_t["d_xn"][128 * t:128 * t + 128, :],
                            xnb_db[db][:], "sG")
                p.wait("scalar", "sG", dxn)
                if "d_st" in dbg_t and t == 4:
                    p.dma("gpsimd", dbg_t["d_st"][:, :], st_sb[:, :], "sG")
            p.wait("tensor", "sA", a_xn)
            p.wait("tensor", "sIdent", ld_ident)
            tsub = t % 4
            for dc in range(16):
                tb = dc % 2
                p.wait("tensor", "sV", tr_copy[tb] if not SERIAL1 else p.val("sV"))
                t_tr = p.emi("tensor", lambda e, xt=xnb_db[db], dc=dc, pt=pTr[tb]:
                             e.transpose(pt[:, :128], xt[:, 128 * dc:128 * dc + 128],
                                         ident[:]), "sT")
                p.wait("vector", "sT", t_tr)
                tr_copy[tb] = p.emi(
                    "vector", lambda e, dc=dc, tsub=tsub, pt=pTr[tb]:
                    e.tensor_copy(xnfm_blk[:, 512 * dc + 128 * tsub:512 * dc + 128 * tsub + 128],
                                  pt[:, :128]), "sV")
            xnb_user[db] = p.val("sT")

            if t % 4 == 3:
                blk = t // 4
                if SERIAL1:
                    p.wait("tensor", "sV", p.val("sV"))
                    p.wait("tensor", "sA", p.val("sA"))
                if blk == 0 and "d_g" in dbg_t and 1 in phases and 3 not in phases:
                    p.wait("gpsimd", "sV", p.val("sV"))
                    dxf = p.dma("gpsimd", dbg_t["d_g"][0:128, :], xnfm_blk[:], "sG")
                    p.wait("tensor", "sG", dxf)
                p.wait("tensor", "sV", p.val("sV"))
                p.wait("tensor", "sWp", wld)
                pq_guard = [0, 0]
                pi = 0
                for (wsb, dst) in ((wq_sb, qfm), (wk_sb, kfm)):
                    for u in range(NU):
                        cc = 128 * u
                        pqb = pi % 2
                        pi += 1
                        p.wait("tensor", "sV", pq_guard[pqb])
                        if SERIAL1:
                            p.wait("tensor", "sV", p.val("sV"))
                        t_q = None
                        for dc in range(16):
                            t_q = p.emi(
                                "tensor", lambda e, w=wsb, dc=dc, cc=cc, pq=pQ[pqb]:
                                e.matmul(pq[:],
                                         w[:, 512 * dc + cc:512 * dc + cc + 128],
                                         xnfm_blk[:, 512 * dc:512 * dc + 512],
                                         start=(dc == 0), stop=(dc == 15)),
                                "sT" if dc == 15 else None)
                        p.wait("vector", "sT", t_q)
                        pq_guard[pqb] = p.emi(
                            "vector", lambda e, d=dst[u], blk=blk, pq=pQ[pqb]:
                            e.tensor_copy(d[:, 512 * blk:512 * blk + 512],
                                          pq[:]), "sV")
                for tsub2 in range(4):
                    kt = 4 * blk + tsub2
                    pqb = pi % 2
                    pi += 1
                    p.wait("tensor", "sV", pq_guard[pqb])
                    if SERIAL1:
                        p.wait("tensor", "sV", p.val("sV"))
                    t_v = None
                    for dc in range(16):
                        t_v = p.emi(
                            "tensor", lambda e, dc=dc, t2=tsub2, pv=pQ[pqb]:
                            e.matmul(pv[:],
                                     xnfm_blk[:, 512 * dc + 128 * t2:512 * dc + 128 * t2 + 128],
                                     wv_sb[:, 512 * dc:512 * dc + 512],
                                     start=(dc == 0), stop=(dc == 15)),
                            "sT" if dc == 15 else None)
                    p.wait("vector", "sT", t_v)
                    pq_guard[pqb] = p.emi(
                        "vector", lambda e, kt=kt, pv=pQ[pqb]:
                        e.tensor_copy(v_sb[:, 512 * kt:512 * kt + 512], pv[:]), "sV")

        ph1_V = p.val("sV")
        ph1_T = p.val("sT")

        if dbg_t:
            p.wait("gpsimd", "sV", ph1_V)
            p.wait("gpsimd", "sT", ph1_T)
            if "d_qfm" in dbg_t:
                for u in range(NU):
                    p.dma("gpsimd", dbg_t["d_qfm"][128 * u:128 * u + 128, :], qfm[u][:, :], "sG")
            if "d_kfm" in dbg_t:
                for u in range(NU):
                    p.dma("gpsimd", dbg_t["d_kfm"][128 * u:128 * u + 128, :], kfm[u][:, :], "sG")
            if "d_v" in dbg_t:
                for kt in range(32):
                    p.dma("gpsimd", dbg_t["d_v"][128 * kt:128 * kt + 128, :],
                          v_sb[:, 512 * kt:512 * kt + 512], "sG")

        ph1.close()

        do_wmean = (3 in phases and 4 in phases)
        wmean_state = {"col": 0, "m_user": [0, 0]}
        if do_wmean:
            ph3m = ExitStack()
            sb3m = lambda n, sh, d: ph3m.enter_context(
                nc.sbuf_tensor(n, sh, d, side="right"))
            wmp = sb3m("wmp", [128, 256], F32)
            wrow3 = sb3m("wrow3", [1, 8], F32)
            mscr = [sb3m(f"mscr{i}", [128, D_ // 2], BF16) for i in range(2)]
            _wm_src = []
            for wi, win, nr in ((0, w1t_in, 16), (1, w2t_in, 64)):
                for i in range(nr):
                    nchunk = 8 if wi == 0 else 2
                    for j in range(nchunk):
                        _wm_src.append(
                            win[128 * i:128 * i + 128,
                                (D_ // 2) * j:(D_ // 2) * j + D_ // 2])
            assert len(_wm_src) == 256
            _wm_sem = ["sM0", "sM1"]
            _wm_eng = ["gpsimd", "gpsimd"]

            def wmean_step(n):
                for _ in range(n):
                    col = wmean_state["col"]
                    if col >= 256:
                        return
                    db = col % 2
                    p.wait(_wm_eng[db], "sV", wmean_state["m_user"][db])
                    ml = p.dma(_wm_eng[db], mscr[db][:], _wm_src[col], _wm_sem[db])
                    p.wait("vector", _wm_sem[db], ml)
                    wmean_state["m_user"][db] = p.emi(
                        "vector", lambda e, db=db, col=col:
                        e.tensor_reduce(wmp[:, col:col + 1], mscr[db][:], AX,
                                        OP.add, apply_absolute_value=True), "sV")
                    wmean_state["col"] = col + 1
        else:
            def wmean_step(n):
                pass

        cc_v = 0
        at_store = 0
        if 2 in phases:
            ph2 = ExitStack()
            sb2 = lambda n, s, d: ph2.enter_context(nc.sbuf_tensor(n, s, d))
            ps2 = lambda n, s, d: ph2.enter_context(nc.psum_tensor(n, s, d))
            masks_sb = sb2("masks_sb", [128, 4 * 512], F32)
            P1 = [sb2(f"P1_{i}", [128, 512], BF16) for i in range(2)]
            P2 = [sb2(f"P2_{i}", [128, 512], BF16) for i in range(2)]
            a1_sb = sb2("a1_sb", [128, 512], F32)
            a2_sb = sb2("a2_sb", [128, 512], F32)
            lrow = sb2("lrow", [1, 1024], F32)
            linv = sb2("linv", [128, 8], F32)
            comb = sb2("comb", [128, 512], F32)
            rms_scr = sb2("rms_scr", [128, 512], F32)
            rcol = sb2("rcol", [128, 12], F32)
            atn_bf = sb2("atn_bf", [128, 512], BF16)
            atT_bf = sb2("atT_bf", [128, 512], BF16)
            pS1 = [ps2(f"pS1{i}", [128, 512], F32) for i in range(2)]
            pS2 = [ps2(f"pS2{i}", [128, 512], F32) for i in range(2)]
            pA1 = ps2("pA1", [128, 512], F32)
            pA2 = ps2("pA2", [128, 512], F32)
            pL1 = ps2("pL1", [128, 512], F32)
            pL2 = ps2("pL2", [128, 512], F32)
            pTb = pL2[:].bitcast(BF16)

            p.wait("sync", "sT", ph1_T)
            p.wait("sync", "sV", ph1_V)
            mld = p.dma("sync", masks_sb[:].rearrange("p (m q) -> p m q", m=4),
                        masks_in.rearrange("(m p) q -> p m q", p=128), "sMsk")

            expA = [0, 0]
            avT = [0, 0]
            prev_store = 0
            prev_cmb = 0
            prev_li = 0
            prev_att = 0
            for u in range(NU):
                for B in range(8):
                    nk = 4 * B + 4 if NKCAP is None else min(NKCAP, 4 * B + 4)
                    def emit_S(kt, u=u, B=B, nk=nk):
                        bi = kt % 2
                        p.wait("tensor", "sA", expA[bi])
                        p.wait("tensor", "sV", ph1_V)
                        p.emi("tensor", lambda e, u=u, kt=kt, B=B, pb=pS1[bi]:
                              e.matmul(pb[:], kfm[u][0:64, 128 * kt:128 * kt + 128],
                                       qfm[u][0:64, 512 * B:512 * B + 512],
                                       start=True, stop=True))
                        t_s = p.emi("tensor", lambda e, u=u, kt=kt, B=B, pb=pS2[bi]:
                                    e.matmul(pb[:], kfm[u][64:128, 128 * kt:128 * kt + 128],
                                             qfm[u][64:128, 512 * B:512 * B + 512],
                                             start=True, stop=True), "sT")
                        need_mask = kt >= 4 * B
                        s_mask = 0
                        if need_mask:
                            pv = kt - 4 * B
                            p.wait("vector", "sT", t_s)
                            p.wait("vector", "sMsk", mld)
                            p.emi("vector", lambda e, pb=pS1[bi], pv=pv:
                                  e.tensor_tensor(pb[:], pb[:],
                                                  masks_sb[:, 512 * pv:512 * pv + 512], OP.add))
                            s_mask = p.emi(
                                "vector", lambda e, pb=pS2[bi], pv=pv:
                                e.tensor_tensor(pb[:], pb[:],
                                                masks_sb[:, 512 * pv:512 * pv + 512], OP.add),
                                "sV")
                        p.wait("scalar", "sT", t_s)
                        if need_mask:
                            p.wait("scalar", "sV", s_mask)
                        p.wait("scalar", "sT", avT[bi])
                        p.emi("scalar", lambda e, pb=pS1[bi], o=P1[bi]:
                              e.activation(o[:], pb[:], AF.Exp,
                                           bias=consts[:, 1:2], scale=consts[:, 0:1]))
                        expA[bi] = p.emi("scalar", lambda e, pb=pS2[bi], o=P2[bi]:
                                         e.activation(o[:], pb[:], AF.Exp,
                                                      bias=consts[:, 1:2],
                                                      scale=consts[:, 0:1]), "sA")

                    def emit_AV(kt, u=u, B=B, nk=nk):
                        bi = kt % 2
                        p.wait("tensor", "sA", expA[bi])
                        fl = dict(start=(kt == 0), stop=(kt == nk - 1))
                        if kt == 0:
                            p.wait("tensor", "sV", prev_cmb)
                        p.emi("tensor", lambda e, u=u, kt=kt, P=P1[bi], f=fl:
                              e.matmul(pA1[:],
                                       v_sb[:, 512 * kt + 128 * u:512 * kt + 128 * u + 128],
                                       P[:], **f))
                        p.emi("tensor", lambda e, u=u, kt=kt, P=P2[bi], f=fl:
                              e.matmul(pA2[:],
                                       v_sb[:, 512 * kt + 128 * u:512 * kt + 128 * u + 128],
                                       P[:], **f))
                        if kt == 0:
                            p.wait("tensor", "sV", prev_li)
                        p.emi("tensor", lambda e, P=P1[bi], f=fl:
                              e.matmul(pL1[:, 0:512], ones_b[:], P[:], **f))
                        if kt == 0:
                            p.wait("tensor", "sV", prev_att)
                        avT[bi] = p.emi("tensor", lambda e, P=P2[bi], f=fl:
                                        e.matmul(pL2[:, 0:512], ones_b[:], P[:], **f), "sT")

                    for kt in range(nk):
                        emit_S(kt)
                        if kt >= 1:
                            emit_AV(kt - 1)
                    emit_AV(nk - 1)
                    t_all = avT[(nk - 1) % 2]
                    p.wait("vector", "sT", t_all)
                    p.emi("vector", lambda e: e.tensor_copy(lrow[0:1, 0:512], pL1[0:1, 0:512]))
                    p.emi("vector", lambda e: e.tensor_copy(lrow[0:1, 512:1024], pL2[0:1, 0:512]))
                    p.emi("vector", lambda e: e.tensor_copy(a1_sb[:], pA1[:]))
                    evac_V = p.emi("vector", lambda e: e.tensor_copy(a2_sb[:], pA2[:]), "sV")
                    p.wait("tensor", "sV", evac_V)
                    for j in range(4):
                        p.emi("tensor", lambda e, j=j:
                              e.matmul(pL1[:, 8 + j:9 + j], lrow[0:1, 128 * j:128 * j + 128],
                                       ones_f[0:1, 0:1], start=True, stop=True))
                    t_lt = None
                    for j in range(4):
                        t_lt = p.emi("tensor", lambda e, j=j:
                                     e.matmul(pL2[:, 8 + j:9 + j],
                                              lrow[0:1, 512 + 128 * j:640 + 128 * j],
                                              ones_f[0:1, 0:1], start=True, stop=True),
                                     "sT" if j == 3 else None)
                    for j in range(4):
                        p.emi("tensor", lambda e, j=j:
                              e.transpose(pA1[:, 128 * j:128 * j + 128],
                                          a1_sb[:, 128 * j:128 * j + 128], identf[:]))
                    t_tr = None
                    for j in range(4):
                        t_tr = p.emi("tensor", lambda e, j=j:
                                     e.transpose(pA2[:, 128 * j:128 * j + 128],
                                                 a2_sb[:, 128 * j:128 * j + 128], identf[:]),
                                     "sT" if j == 3 else None)
                    p.wait("vector", "sT", t_lt)
                    p.wait("vector", "sLam", lam_bc)
                    p.emi("vector", lambda e: e.reciprocal(linv[:, 0:4], pL1[:, 8:12]))
                    p.emi("vector", lambda e: e.reciprocal(linv[:, 4:8], pL2[:, 8:12]))
                    p.emi("vector", lambda e: e.engine_nop())
                    v_li = p.emi("vector", lambda e:
                                 e.tensor_scalar(linv[:, 4:8], linv[:, 4:8],
                                                 lam_col[:, 0:1], None, OP.mult), "sV")
                    p.emi("vector", lambda e: e.engine_nop())
                    if u == 0 and B == 0 and "d_l" in dbg_t:
                        p.wait("gpsimd", "sV", evac_V)
                        dmp2 = p.dma("gpsimd", dbg_t["d_l"][:, :], lrow[0:1, :], "sG")
                        if "d_a1" in dbg_t and "d_a2" in dbg_t:
                            p.dma("gpsimd", dbg_t["d_a1"][:, :], a1_sb[:], "sG")
                            dmp2 = p.dma("gpsimd", dbg_t["d_a2"][:, :], a2_sb[:], "sG")
                        p.wait("vector", "sG", dmp2)
                    p.wait("vector", "sT", t_tr)
                    a_sq2 = 0
                    for j in range(4):
                        p.wait("vector", "sA", a_sq2)
                        p.emi("vector", lambda e, j=j:
                              e.tensor_scalar(comb[:, 128 * j:128 * j + 128],
                                              pA1[:, 128 * j:128 * j + 128],
                                              linv[:, j:j + 1], None, OP.mult))
                        v_cb = p.emi("vector", lambda e, j=j:
                                     e.scalar_tensor_tensor(comb[:, 128 * j:128 * j + 128],
                                                            pA2[:, 128 * j:128 * j + 128],
                                                            linv[:, 4 + j:5 + j],
                                                            comb[:, 128 * j:128 * j + 128],
                                                            OP.mult, OP.subtract), "sV")
                        p.wait("scalar", "sV", v_cb)
                        a_sq2 = p.emi("scalar", lambda e, j=j:
                                      e.activation(rms_scr[:, 128 * j:128 * j + 128],
                                                   comb[:, 128 * j:128 * j + 128],
                                                   AF.Square, accum_out=rcol[:, j:j + 1]), "sA")
                    prev_cmb = v_cb
                    if u == 0 and B == 0 and "d_comb" in dbg_t and "d_linv" in dbg_t:
                        p.wait("gpsimd", "sA", a_sq2)
                        p.dma("gpsimd", dbg_t["d_comb"][:, :], comb[:], "sG")
                        dmp3 = p.dma("gpsimd", dbg_t["d_linv"][:, :], linv[:], "sG")
                        p.wait("vector", "sG", dmp3)
                    p.wait("vector", "sA", a_sq2)
                    v_rs = p.emi("vector", lambda e:
                                 e.tensor_scalar(rcol[:, 0:4], rcol[:, 0:4], 1.0 / 128.0,
                                                 EPS, OP.mult, OP.add), "sV")
                    p.wait("scalar", "sV", v_rs)
                    p.emi("scalar", lambda e: e.activation(rcol[:, 4:8], rcol[:, 0:4], AF.Ln))
                    a_rr = p.emi("scalar", lambda e:
                                 e.activation(rcol[:, 8:12], rcol[:, 4:8], AF.Exp,
                                              scale=consts[:, 3:4]), "sA")
                    p.wait("vector", "sA", a_rr)
                    p.emi("vector", lambda e:
                          e.tensor_single_scalar(rcol[:, 8:12], rcol[:, 8:12],
                                                 LAMBDA_INIT - 1.0, OP.mult))
                    p.emi("vector", lambda e: e.engine_nop())
                    if u == 0 and B == 0 and "d_rc" in dbg_t:
                        p.wait("gpsimd", "sV", p.val("sV"))
                        dmp4 = p.dma("gpsimd", dbg_t["d_rc"][:, :], rcol[:], "sG")
                        p.wait("vector", "sG", dmp4)
                    p.wait("vector", "sAT", prev_store)
                    v_at = 0
                    for j in range(4):
                        v_at = p.emi("vector", lambda e, j=j:
                                     e.tensor_scalar(atn_bf[:, 128 * j:128 * j + 128],
                                                     comb[:, 128 * j:128 * j + 128],
                                                     rcol[:, 8 + j:9 + j], None, OP.mult), "sV")
                    p.wait("tensor", "sV", v_at)
                    t_bt = None
                    for j in range(4):
                        t_bt = p.emi("tensor", lambda e, j=j:
                                     e.transpose(pTb[:, 128 * j:128 * j + 128],
                                                 atn_bf[:, 128 * j:128 * j + 128], ident[:]),
                                     "sT" if j == 3 else None)
                    p.wait("vector", "sT", t_bt)
                    v_att = p.emi("vector", lambda e: e.tensor_copy(atT_bf[:], pTb[:, 0:512]), "sV")
                    p.wait("sync", "sV", v_att)
                    prev_store = p.dma("sync",
                                       ag_in_t[128 * u:128 * u + 128, 512 * B:512 * B + 512],
                                       atT_bf[:], "sAT")
                    prev_li = v_li
                    prev_att = v_att
                    wmean_step(8)
            at_store = prev_store

            if "d_attn" in dbg_t:
                p.wait("gpsimd", "sAT", at_store)
                abscr = sb2("abscr", [128, S_], BF16)
                for u in range(NU):
                    lb = p.dma("gpsimd", abscr[:], ag_in_t[128 * u:128 * u + 128, :], "sG")
                    p.wait("gpsimd", "sG", lb)
                    sv = p.dma("gpsimd", dbg_t["d_attn"][128 * u:128 * u + 128, :],
                               abscr[:], "sG")
                    p.wait("gpsimd", "sG", sv)

            if collective:
                p.wait("gpsimd", "sAT", at_store)
                cc_v = p.emi("gpsimd", lambda e:
                             e.collective_compute("AllGather", OP.bypass,
                                                  replica_groups=[[0, 1, 2, 3], [4, 5, 6, 7]],
                                                  ins=[ag_in_t.opt()],
                                                  outs=[ag_out_t.opt()]), "sCC")
            if "d_ag" in dbg_t:
                p.wait("gpsimd", "sCC", cc_v)
                agscr = sb2("agscr", [128, S_], BF16)
                for r in range(16):
                    lb = p.dma("gpsimd", agscr[:], ag_out_t[128 * r:128 * r + 128, :], "sG")
                    p.wait("gpsimd", "sG", lb)
                    sv = p.dma("gpsimd", dbg_t["d_ag"][128 * r:128 * r + 128, :],
                               agscr[:], "sG")
                    p.wait("gpsimd", "sG", sv)
            ph2.close()

        ph2_T = p.val("sT")
        ph2_V = p.val("sV")
        ph2_A = p.val("sA")

        ph12.close()

        if 3 in phases and 4 in phases:
            ph3p = ExitStack()
            sb3p = lambda n, sh, d: ph3p.enter_context(nc.sbuf_tensor(n, sh, d))
            alsc = sb3p("alsc", [128, 48], F32)
            gsums = sb3p("gsums", [128, 8 * 16], F32)
            gssqs = sb3p("gssqs", [128, 8 * 16], F32)

            ph3g = ExitStack()
            sb3g = lambda n, sh, d: ph3g.enter_context(nc.sbuf_tensor(n, sh, d))
            g_sb = sb3g("g_sb", [128, 8 * F_], BF16)

            for qe in ("sync", "gpsimd"):
                p.wait(qe, "sT", ph2_T)
                p.wait(qe, "sV", ph2_V)
                p.wait(qe, "sA", ph2_A)

            wmean_step(256 - wmean_state["col"])
            m_user = wmean_state["m_user"]
            mean_V = max(m_user)
            ph3ab = ExitStack()
            sb3a = lambda n, sh, d: ph3ab.enter_context(nc.sbuf_tensor(n, sh, d))
            ps3a = lambda n, sh, d: ph3ab.enter_context(nc.psum_tensor(n, sh, d))
            pH = [ps3a(f"pH{i}", [128, 512], F32) for i in range(8)]
            xqfm = sb3a("xqfm", [128, 16 * R_], BF16)
            ph3a = ExitStack()
            sb3aa = lambda n, sh, d: ph3a.enter_context(nc.sbuf_tensor(n, sh, d))
            pXT = pH[0].bitcast(BF16)
            xr_db = sb3aa("xr_db", [128, D_], F32)
            st3 = sb3aa("st3", [128, 16], F32)
            xnf = sb3aa("xnf", [128, D_], F32)
            isc = sb3aa("isc", [128, D_], I32)
            xqt = sb3aa("xqt", [128, D_], BF16)
            vsc = sb3aa("vsc", [128, D_], F32)
            xqt_v = xqt[:]


            def wm_fin(e):
                e.tensor_reduce(alsc[:, 40:41], wmp[:, 0:128], AX, OP.add)
                return e.tensor_reduce(alsc[:, 41:42], wmp[:, 128:256], AX, OP.add)
            v_wm = p.emi("vector", wm_fin, "sV")
            p.wait("tensor", "sV", v_wm)
            p.wait("tensor", "sIdent", ld_ident)
            t_wm = p.emi("tensor", lambda e:
                         e.matmul(pH[0][0:1, 0:2], ones_f[:], alsc[:, 40:42],
                                  start=True, stop=True), "sT")
            p.wait("vector", "sT", t_wm)
            def wrow_math(e):
                e.tensor_copy(wrow3[0:1, 0:2], pH[0][0:1, 0:2])
                e.engine_nop()
                e.tensor_single_scalar(wrow3[0:1, 0:2], wrow3[0:1, 0:2],
                                       1.0 / (D_ * F_), OP.mult)
                e.engine_nop()
                e.tensor_single_scalar(wrow3[0:1, 0:2], wrow3[0:1, 0:2], EPS, OP.max)
                e.engine_nop()
                e.tensor_single_scalar(wrow3[0:1, 2:4], wrow3[0:1, 0:2], 0.5, OP.mult)
                e.engine_nop()
                e.tensor_single_scalar(wrow3[0:1, 4:6], wrow3[0:1, 2:4], -1.0, OP.mult)
                e.engine_nop()
                return e.tensor_single_scalar(wrow3[0:1, 6:8], wrow3[0:1, 0:2],
                                              1.0 / 127.0, OP.mult)
            v_wr = p.emi("vector", wrow_math, "sV")
            p.wait("sync", "sV", v_wr)
            ws_st = p.dma("sync", wstat_d[:, :], wrow3[0:1, :], "sLam")
            p.wait("sync", "sLam", ws_st)
            ws_bc = p.dma("sync", wstat[:, :], wstat_d[:, :].partition_broadcast(128), "sLam")

            p.wait("sync", "sV", mean_V)
            x3_user = 0
            st3_user = 0
            xq_tr_ev = 0
            for tt in range(8):
                p.wait("sync", "sV", x3_user)
                xl = p.dma("sync", xr_db[:], xr_in[128 * tt:128 * tt + 128, :], "sX0")
                st = st3[:, 0:8]
                p.wait("scalar", "sX0", xl)
                p.wait("scalar", "sV", st3_user if not SERIAL_ACT else p.val("sV"))
                a_sq = p.emi("scalar", lambda e:
                             e.activation(xnf[:], xr_db[:], AF.Square,
                                          accum_out=st3[:, 1:2]), "sA")
                p.wait("vector", "sX0", xl)
                p.emi("vector", lambda e: e.tensor_reduce(st3[:, 0:1], xr_db[:], AX, OP.add))
                p.wait("vector", "sA", a_sq)
                def ln3(e):
                    e.engine_nop()
                    e.tensor_single_scalar(st3[:, 2:3], st3[:, 0:1], 1.0 / D_, OP.mult)
                    e.tensor_single_scalar(st3[:, 3:4], st3[:, 1:2], 1.0 / D_, OP.mult)
                    e.tensor_tensor(st3[:, 4:5], st3[:, 2:3], st3[:, 2:3], OP.mult)
                    e.engine_nop()
                    e.tensor_tensor(st3[:, 5:6], st3[:, 3:4], st3[:, 4:5], OP.subtract)
                    e.engine_nop()
                    return e.tensor_single_scalar(st3[:, 5:6], st3[:, 5:6], EPS, OP.add)
                v_st = p.emi("vector", ln3, "sV")
                st3_user = v_st
                p.wait("scalar", "sV", v_st)
                p.emi("scalar", lambda e: e.activation(st3[:, 6:7], st3[:, 5:6], AF.Ln))
                a_r = p.emi("scalar", lambda e:
                            e.activation(st3[:, 7:8], st3[:, 6:7], AF.Exp,
                                         scale=consts[:, 3:4]), "sA")
                p.wait("vector", "sA", a_r)
                def mr3(e):
                    e.scalar_tensor_tensor(st3[:, 4:5], st3[:, 2:3], -1.0, st3[:, 7:8],
                                           OP.mult, OP.mult)
                    return e.tensor_copy(st3[:, 6:7], st3[:, 7:8])
                v_mr = p.emi("vector", mr3, "sV")
                p.wait("scalar", "sV", v_mr)
                a_xn = p.emi("scalar", lambda e:
                             e.activation(xnf[:], xr_db[:], AF.Identity,
                                          bias=st3[:, 4:5], scale=st3[:, 6:7]), "sA")
                p.wait("vector", "sA", a_xn)
                def qsc(e, tt=tt):
                    e.tensor_reduce(st3[:, 8:9], xnf[:], AX, OP.max,
                                    apply_absolute_value=True)
                    for _ in range(4):
                        e.engine_nop()
                    e.tensor_single_scalar(alph[:, tt:tt + 1], st3[:, 8:9], EPS, OP.max)
                    for _ in range(4):
                        e.engine_nop()
                    e.reciprocal(st3[:, 9:10], alph[:, tt:tt + 1])
                    for _ in range(4):
                        e.engine_nop()
                    return e.tensor_single_scalar(alsc[:, tt:tt + 1], st3[:, 9:10],
                                                  127.0, OP.mult)
                v_qs = p.emi("vector", qsc, "sV")
                def qint(e, tt=tt):
                    for _ in range(4):
                        e.engine_nop()
                    e.tensor_scalar(vsc[:], xnf[:], alsc[:, tt:tt + 1], None,
                                    OP.mult)
                    e.tensor_copy(isc[:], vsc[:])
                    return e.tensor_copy(xqt_v[:, 0:D_], isc[:])
                v_qi = p.emi("vector", qint, "sV")
                x3_user = v_qi
                if tt == 0 and "d_ffn" in dbg_t and "d_xq" in dbg_t:
                    p.wait("gpsimd", "sV", v_qi)
                    p.dma("gpsimd", dbg_t["d_ffn"][0:128, :], vsc[:], "sG")
                    dqq = p.dma("gpsimd", dbg_t["d_ffn"][128:256, :], xqt[:], "sG")
                    p.wait("tensor", "sG", dqq)
                    p.wait("vector", "sG", dqq)
                p.wait("tensor", "sV", v_qi)
                for dc in range(16):
                    t_tr = p.emi("tensor", lambda e, dc=dc:
                                 e.transpose(pXT[:, 0:128],
                                             xqt_v[:, 128 * dc:128 * dc + 128],
                                             ident[:]), "sT")
                    p.wait("vector", "sT", t_tr)
                    xq_tr_ev = p.emi(
                        "vector", lambda e, dc=dc, tt=tt:
                        e.tensor_copy(xqfm[:, R_ * dc + 128 * tt:R_ * dc + 128 * tt + 128],
                                      pXT[:, 0:128]), "sV")
                    p.wait("tensor", "sV", xq_tr_ev)
            if "d_xq" in dbg_t:
                p.wait("gpsimd", "sV", p.val("sV"))
                p.dma("gpsimd", dbg_t["d_xq"][0:128, :], xqfm[:, 0:D_], "sG")
                dxq_v = p.dma("gpsimd", dbg_t["d_xq"][128:256, :], xqfm[:, D_:2 * D_], "sG")
                p.wait("vector", "sG", dxq_v)
            p.wait("vector", "sLam", ws_bc)
            v_a1c = p.emi("vector", lambda e:
                          e.tensor_scalar(alsc[:, 8:16], alph[:, 0:8],
                                          wstat[:, 6:7], None, OP.mult), "sV")
            ph3a_A = p.val("sA")
            ph3a_V = p.val("sV")
            ph3a.close()
            ph3b = ExitStack()
            sb3b = lambda n, sh, d: ph3b.enter_context(nc.sbuf_tensor(n, sh, d))
            w1tile = [sb3b(f"w1tile{i}", [128, 512], F32) for i in range(4)]
            tqs = sb3b("tqs", [128, 512], BF16)
            sqb = sb3b("sqb", [128, 512], BF16)
            w1q = [sb3b(f"w1q{i}", [128, 512], BF16) for i in range(4)]
            b1b = [sb3b(f"b1b{i}", [128, 512], F32) for i in range(2)]
            p.wait("sync", "sA", ph3a_A)
            p.wait("sync", "sV", ph3a_V)
            p.wait("gpsimd", "sA", ph3a_A)
            p.wait("gpsimd", "sV", ph3a_V)

            wsem = ["sW10", "sW11", "sW12", "sW13"]
            wqeng = ["sync", "sync", "gpsimd", "gpsimd"]
            w1t_user = [0, 0, 0, 0]
            w1q_user = [0, 0, 0, 0]
            b1_user = [0, 0]
            pH_gelu = [0] * 8
            for fs in range(16):
                sp = fs % 2
                p.wait("sync", "sV", b1_user[sp])
                bl = p.dma("sync", b1b[sp][:],
                           b1_in[0:1, 512 * fs:512 * fs + 512].partition_broadcast(128),
                           "sB0")
                t_mm = None
                for dc in range(16):
                    db = dc % 4
                    p.wait(wqeng[db], "sV", w1t_user[db])
                    wl = p.dma(wqeng[db], w1tile[db][:],
                               w1t_in[128 * dc:128 * dc + 128, 512 * fs:512 * fs + 512],
                               wsem[db])
                    p.wait("vector", wsem[db], wl)
                    p.wait("vector", "sT", w1q_user[db])
                    p.wait("vector", "sLam", ws_bc)
                    def wq1(e, db=db):
                        e.tensor_scalar(tqs[:], w1tile[db][:], wstat[:, 4:5], None,
                                        OP.is_lt)
                        return e.scalar_tensor_tensor(w1q[db][:], w1tile[db][:],
                                                      wstat[:, 2:3], tqs[:],
                                                      OP.is_gt, OP.subtract)
                    q_v = p.emi("vector", wq1, "sV")
                    w1t_user[db] = q_v
                    p.wait("tensor", "sV", q_v)
                    if fs == 0 and dc == 0:
                        p.wait("tensor", "sV", xq_tr_ev)
                    for tt in range(8):
                        if dc == 0:
                            p.wait("tensor", "sA", pH_gelu[tt])
                        t_mm = p.emi(
                            "tensor", lambda e, dc=dc, tt=tt, db=db:
                            e.matmul(pH[tt][:],
                                     xqfm[:, R_ * dc + 128 * tt:R_ * dc + 128 * tt + 128],
                                     w1q[db][:], start=(dc == 0), stop=(dc == 15)),
                            "sT" if tt == 7 else None)
                    w1q_user[db] = t_mm
                if fs == 0 and "d_alpha" in dbg_t:
                    p.wait("gpsimd", "sV", w1t_user[0])
                    p.dma("gpsimd", dbg_t["d_alpha"][:, :], alsc[:, 0:8], "sG")
                p.wait("vector", "sT", t_mm)
                p.wait("vector", "sV", v_a1c)
                p.wait("vector", "sB0", bl)
                for tt in range(8):
                    v_ev = p.emi("vector", lambda e, tt=tt, sp=sp:
                                 e.scalar_tensor_tensor(pH[tt][:], pH[tt][:],
                                                        alsc[:, 8 + tt:9 + tt],
                                                        b1b[sp][:],
                                                        OP.mult, OP.add), "sV")
                    b1_user[sp] = v_ev
                    p.wait("scalar", "sV", v_ev)
                    pH_gelu[tt] = p.emi(
                        "scalar", lambda e, tt=tt, fs=fs:
                        e.activation(g_sb[:, F_ * tt + 512 * fs:F_ * tt + 512 * fs + 512],
                                     pH[tt][:], AF.Gelu,
                                     accum_out=gsums[:, 16 * tt + fs:16 * tt + fs + 1]),
                        "sA")
                    p.emi("scalar", lambda e, tt=tt, fs=fs:
                          e.activation(sqb[:],
                                       g_sb[:, F_ * tt + 512 * fs:F_ * tt + 512 * fs + 512],
                                       AF.Square,
                                       accum_out=gssqs[:, 16 * tt + fs:16 * tt + fs + 1]),
                          "sA")

            ph3b_A = p.val("sA")
            ph3b_V = p.val("sV")
            ph3b_T = p.val("sT")
            if "d_hq" in dbg_t:
                gdump = sb3b("gdump", [128, 512], F32)
                dg_v = 0
                for tt in range(8):
                    for hh in range(F_ // 512):
                        p.wait("vector", "sA", ph3b_A)
                        cv = p.emi("vector", lambda e, tt=tt, hh=hh: e.tensor_copy(
                            gdump[:], g_sb[:, F_ * tt + 512 * hh:F_ * tt + 512 * hh + 512]),
                            "sV")
                        p.wait("gpsimd", "sV", cv)
                        dg_v = p.dma("gpsimd",
                                     dbg_t["d_hq"][128 * tt:128 * tt + 128,
                                                   512 * hh:512 * hh + 512],
                                     gdump[:], "sG")
                        p.wait("vector", "sG", dg_v)
            ph3b.close()
            ph3ab.close()

            ph3c = ExitStack()
            sb3c = lambda n, sh, d: ph3c.enter_context(nc.sbuf_tensor(n, sh, d))
            ps3c = lambda n, sh, d: ph3c.enter_context(nc.psum_tensor(n, sh, d))
            gamb = [sb3c(f"gamb{i}", [128, 512], F32) for i in range(2)]
            betb = [sb3c(f"betb{i}", [128, 512], F32) for i in range(2)]
            vsc3 = sb3c("vsc3", [128, F_ // 2], F32)
            isc3 = sb3c("isc3", [128, F_ // 2], I32)
            hqt_scr = sb3c("hqt_scr", [128, F_ // 2], F32)
            st3c = sb3c("st3c", [128, 16], F32)
            pXT2 = [ps3c(f"pXT2_{i}", [128, 512], F32) for i in range(2)]

            p.wait("sync", "sT", ph3b_T)
            p.wait("sync", "sV", ph3b_V)
            p.wait("sync", "sA", ph3b_A)
            p.wait("vector", "sA", ph3b_A)
            t1_v = 0
            for tt in range(8):
                def lnstats(e, tt=tt):
                    e.tensor_reduce(st3c[:, 0:1], gsums[:, 16 * tt:16 * tt + 16], AX, OP.add)
                    e.tensor_reduce(st3c[:, 1:2], gssqs[:, 16 * tt:16 * tt + 16], AX, OP.add)
                    for _ in range(4):
                        e.engine_nop()
                    e.tensor_single_scalar(st3c[:, 2:3], st3c[:, 0:1], 1.0 / F_, OP.mult)
                    e.tensor_single_scalar(st3c[:, 3:4], st3c[:, 1:2], 1.0 / F_, OP.mult)
                    e.tensor_tensor(st3c[:, 4:5], st3c[:, 2:3], st3c[:, 2:3], OP.mult)
                    e.engine_nop()
                    e.tensor_tensor(st3c[:, 5:6], st3c[:, 3:4], st3c[:, 4:5], OP.subtract)
                    e.engine_nop()
                    return e.tensor_single_scalar(st3c[:, 5:6], st3c[:, 5:6], EPS, OP.add)
                v_s = p.emi("vector", lnstats, "sV")
                p.wait("scalar", "sV", v_s)
                p.emi("scalar", lambda e: e.activation(st3c[:, 6:7], st3c[:, 5:6], AF.Ln))
                a_r = p.emi("scalar", lambda e:
                            e.activation(st3c[:, 7:8], st3c[:, 6:7], AF.Exp,
                                         scale=consts[:, 3:4]), "sA")
                p.wait("vector", "sA", a_r)
                t1_v = p.emi("vector", lambda e, tt=tt:
                             e.tensor_scalar(g_sb[:, F_ * tt:F_ * tt + F_],
                                             g_sb[:, F_ * tt:F_ * tt + F_],
                                             st3c[:, 2:3], st3c[:, 7:8],
                                             OP.subtract, OP.mult), "sV")
            gb_user = [0, 0]
            gb_v = 0
            for fs in range(16):
                gp = fs % 2
                p.wait("sync", "sV", gb_user[gp])
                gl = p.dma("sync", gamb[gp][:],
                           gam_in[0:1, 512 * fs:512 * fs + 512].partition_broadcast(128),
                           "sB0")
                gl2 = p.dma("sync", betb[gp][:],
                            bet_in[0:1, 512 * fs:512 * fs + 512].partition_broadcast(128),
                            "sB1")
                p.wait("vector", "sB0", gl)
                p.wait("vector", "sB1", gl2)
                p.wait("vector", "sV", t1_v)
                for tt in range(8):
                    sl = slice(F_ * tt + 512 * fs, F_ * tt + 512 * fs + 512)
                    p.emi("vector", lambda e, sl=sl, gp=gp:
                          e.tensor_tensor(g_sb[:, sl], g_sb[:, sl], gamb[gp][:], OP.mult))
                    gb_v = p.emi("vector", lambda e, sl=sl, gp=gp:
                                 e.tensor_tensor(g_sb[:, sl], g_sb[:, sl], betb[gp][:],
                                                 OP.add), "sV")
                gb_user[gp] = gb_v
            if "d_3c" in dbg_t:
                gd3 = sb3c("gd3", [128, 512], F32)
            cp_ev = [0] * 2
            tr_last = 0
            v_q2b = 0
            for tt in range(8):
                p.wait("vector", "sV", gb_v)
                if tt == 0 and "d_3c" in dbg_t:
                    dga = 0
                    for cc in range(16):
                        p.wait("vector", "sG", dga)
                        cva = p.emi("vector", lambda e, cc=cc: e.tensor_copy(
                            gd3[:], g_sb[:, 512 * cc:512 * cc + 512]), "sV")
                        p.wait("gpsimd", "sV", cva)
                        dga = p.dma("gpsimd",
                                    dbg_t["d_3c"][0:128, 512 * cc:512 * cc + 512],
                                    gd3[:], "sG")
                    p.wait("vector", "sG", dga)
                def qsc2(e, tt=tt):
                    e.tensor_reduce(st3c[:, 8:9], g_sb[:, F_ * tt:F_ * tt + F_], AX,
                                    OP.max, apply_absolute_value=True)
                    for _ in range(4):
                        e.engine_nop()
                    e.tensor_single_scalar(alsc[:, 16 + tt:17 + tt], st3c[:, 8:9],
                                           EPS, OP.max)
                    for _ in range(4):
                        e.engine_nop()
                    e.reciprocal(st3c[:, 9:10], alsc[:, 16 + tt:17 + tt])
                    for _ in range(4):
                        e.engine_nop()
                    e.tensor_single_scalar(alsc[:, 24 + tt:25 + tt], st3c[:, 9:10],
                                           127.0, OP.mult)
                    for _ in range(4):
                        e.engine_nop()
                    return e.tensor_scalar(alsc[:, 32 + tt:33 + tt],
                                           alsc[:, 16 + tt:17 + tt],
                                           wstat[:, 7:8], None, OP.mult)
                v_q2 = p.emi("vector", qsc2, "sV")
                for hh in range(2):
                    hof = (F_ // 2) * hh
                    p.wait("scalar", "sV", max(v_q2, v_q2b))
                    a_v2 = p.emi("scalar", lambda e, tt=tt, hof=hof:
                                 e.activation(vsc3[:],
                                              g_sb[:, F_ * tt + hof:F_ * tt + hof + F_ // 2],
                                              AF.Identity,
                                              scale=alsc[:, 24 + tt:25 + tt]), "sA")
                    p.wait("vector", "sA", a_v2)
                    def qint2(e):
                        e.tensor_copy(isc3[:], vsc3[:])
                        return e.tensor_copy(hqt_scr[:], isc3[:])
                    p.wait("vector", "sT", tr_last)
                    v_q2b = p.emi("vector", qint2, "sV")
                    p.wait("tensor", "sV", v_q2b)
                    for fc2 in range(32):
                        fc = 32 * hh + fc2
                        slt = fc2 % 2
                        bk, off = slt, 0
                        p.wait("tensor", "sA", cp_ev[slt])
                        t_tr = p.emi("tensor", lambda e, fc2=fc2, bk=bk, off=off:
                                     e.transpose(pXT2[bk][:, off:off + 128],
                                                 hqt_scr[:, 128 * fc2:128 * fc2 + 128],
                                                 identf[:]), "sT")
                        p.wait("scalar", "sT", t_tr)
                        cp_ev[slt] = p.emi(
                            "scalar", lambda e, fc=fc, tt=tt, bk=bk, off=off:
                            e.activation(
                                g_sb[:, F_ * tt + 128 * fc:F_ * tt + 128 * fc + 128],
                                pXT2[bk][:, off:off + 128], AF.Identity), "sA")
                    tr_last = t_tr
                    if tt == 0 and hh == 0 and "d_3c" in dbg_t:
                        p.wait("gpsimd", "sV", v_q2b)
                        dqh = p.dma("gpsimd", dbg_t["d_3c"][128:256, 0:F_ // 2],
                                    hqt_scr[:], "sG")
                        p.wait("vector", "sG", dqh)
                        p.wait("tensor", "sG", dqh)

            hq_done = max(cp_ev)
            ph3c_V = p.val("sV")
            ph3c_T = p.val("sT")
            ph3c_A = p.val("sA")
            ph3c.close()

            ph3d = ExitStack()
            sb3d = lambda n, sh, d: ph3d.enter_context(nc.sbuf_tensor(n, sh, d))
            ps3d = lambda n, sh, d: ph3d.enter_context(nc.psum_tensor(n, sh, d))
            w2tile = [sb3d(f"w2tile{i}", [128, 512], F32) for i in range(4)]
            fbuf = [sb3d(f"fbuf{i}", [128, 512], BF16) for i in range(2)]
            tqs2 = sb3d("tqs2", [128, 512], BF16)
            w2q = [sb3d(f"w2q{i}", [128, 512], BF16) for i in range(4)]
            b2b = [sb3d(f"b2b{i}", [128, 512], F32) for i in range(2)]
            pM2 = [ps3d(f"pM2_{i}", [128, 512], F32) for i in range(8)]

            p.wait("sync", "sT", ph3c_T)
            p.wait("sync", "sV", ph3c_V)
            p.wait("sync", "sA", ph3c_A)
            p.wait("gpsimd", "sT", ph3c_T)
            p.wait("gpsimd", "sV", ph3c_V)
            p.wait("gpsimd", "sA", ph3c_A)
            w2sem = ["sW20", "sW21", "sW22", "sW23"]
            w2eng = ["sync", "sync", "gpsimd", "gpsimd"]
            w2t_user = [0, 0, 0, 0]
            w2q_user = [0, 0, 0, 0]
            b2_user = [0, 0]
            pM_user = 0
            for ds in range(4):
                sp = ds % 2
                p.wait("sync", "sV", b2_user[sp])
                b2l = p.dma("sync", b2b[sp][:],
                            b2_in[0:1, 512 * ds:512 * ds + 512].partition_broadcast(128),
                            "sB0")
                t_mm2 = None
                for fc in range(64):
                    db = fc % 4
                    p.wait(w2eng[db], "sV", w2t_user[db])
                    wl = p.dma(w2eng[db], w2tile[db][:],
                               w2t_in[128 * fc:128 * fc + 128, 512 * ds:512 * ds + 512],
                               w2sem[db])
                    p.wait("vector", w2sem[db], wl)
                    p.wait("vector", "sT", w2q_user[db])
                    def wq2(e, db=db):
                        e.tensor_scalar(tqs2[:], w2tile[db][:], wstat[:, 5:6], None,
                                        OP.is_lt)
                        return e.scalar_tensor_tensor(w2q[db][:], w2tile[db][:],
                                                      wstat[:, 3:4], tqs2[:],
                                                      OP.is_gt, OP.subtract)
                    q_v = p.emi("vector", wq2, "sV")
                    w2t_user[db] = q_v
                    p.wait("tensor", "sV", q_v)
                    if fc == 0:
                        p.wait("tensor", "sV", pM_user)
                        p.wait("tensor", "sA", hq_done)
                    for tt in range(8):
                        t_mm2 = p.emi(
                            "tensor", lambda e, tt=tt, fc=fc, db=db:
                            e.matmul(pM2[tt][:],
                                     g_sb[:, F_ * tt + 128 * fc:F_ * tt + 128 * fc + 128],
                                     w2q[db][:], start=(fc == 0), stop=(fc == 63)),
                            "sT" if tt == 7 else None)
                    w2q_user[db] = t_mm2
                p.wait("vector", "sT", t_mm2)
                p.wait("vector", "sB0", b2l)
                fb_user = [0, 0]
                for tt in range(8):
                    fb = tt % 2
                    p.wait("vector", "sSt", fb_user[fb])
                    pM_user = p.emi(
                        "vector", lambda e, tt=tt, fb=fb, sp=sp:
                        e.scalar_tensor_tensor(
                            fbuf[fb][:],
                            pM2[tt][:], alsc[:, 32 + tt:33 + tt], b2b[sp][:],
                            OP.mult, OP.add), "sV")
                    p.wait("sync", "sV", pM_user)
                    fb_user[fb] = p.dma(
                        "sync", ffn_d[128 * tt:128 * tt + 128, 512 * ds:512 * ds + 512],
                        fbuf[fb][:], "sSt")
                b2_user[sp] = pM_user
                ffn_st = fb_user[1]

            ph3d_V = p.val("sV")
            if "d_g" in dbg_t:
                p.wait("gpsimd", "sV", ph3d_V)
                dgv = 0
                for tt in range(8):
                    dgv = p.dma("gpsimd", dbg_t["d_g"][128 * tt:128 * tt + 128, :],
                                g_sb[:, F_ * tt:F_ * tt + F_], "sG")
                p.wait("sync", "sG", dgv)
                p.wait("vector", "sG", dgv)
            ph3d.close()
            ph3g.close()
            if "d_ffn" in dbg_t:
                p.wait("gpsimd", "sSt", ffn_st)
                fscr = ph3p.enter_context(nc.sbuf_tensor("fscr", [128, D_], BF16))
                fscr2 = ph3p.enter_context(nc.sbuf_tensor("fscr2", [128, D_], F32))
                for tt in range(8):
                    fl2 = p.dma("gpsimd", fscr[:], ffn_d[128 * tt:128 * tt + 128, :], "sG")
                    p.wait("vector", "sG", fl2)
                    cv2 = p.emi("vector", lambda e: e.tensor_copy(fscr2[:], fscr[:]), "sV")
                    p.wait("gpsimd", "sV", cv2)
                    sv2 = p.dma("gpsimd", dbg_t["d_ffn"][128 * tt:128 * tt + 128, :],
                                fscr2[:], "sG")
                    p.wait("gpsimd", "sG", sv2)
                    p.wait("vector", "sG", sv2)
            if "d_thr" in dbg_t:
                p.wait("gpsimd", "sLam", ws_bc)
                p.dma("gpsimd", dbg_t["d_thr"][:, :], wstat[:, :], "sG")

            ph4 = ExitStack()
            sb4 = lambda n, sh, d: ph4.enter_context(nc.sbuf_tensor(n, sh, d))
            ps4 = lambda n, sh, d: ph4.enter_context(nc.psum_tensor(n, sh, d))
            wo_sb = sb4("wo_sb", [128, 16 * D_], BF16)
            ffn_t = [sb4(f"ffn_t{i}", [128, 512], BF16) for i in range(2)]
            afm = sb4("afm", [128, 16 * R_], BF16)
            xr4 = [sb4(f"xr4_{i}", [128, D_], F32) for i in range(2)]
            obuf = [sb4(f"obuf{i}", [128, 512], F32) for i in range(2)]
            osum = sb4("osum", [128, 512], F32)
            pWo = [ps4(f"pWo{i}", [128, 512], F32) for i in range(2)]

            p.wait("gpsimd", "sV", ph3d_V)
            wo_l = 0
            for cc in range(16):
                wo_l = p.dma("gpsimd", wo_sb[:, D_ * cc:D_ * cc + D_],
                             wo_in[128 * cc:128 * cc + 128, :], "sWo")
            p.wait("sync", "sV", ph3d_V)
            p.wait("sync", "sT", p.val("sT"))
            if collective:
                p.wait("sync", "sCC", cc_v)
            agv = p.tick("sAG", 16 * 16)
            def ag_read(e):
                rk = e.cc_rank([[0, 1, 2, 3], [4, 5, 6, 7]])
                last = None
                for cc in range(16):
                    last = e.dma_start(
                        out=afm[:, R_ * cc:R_ * cc + R_],
                        in_=ag_out_t[128 * cc:128 * cc + 128, bass.ts(rk, R_)])
                    last.then_inc(p.sems["sAG"], 16)
                return last
            p.ops["sync"].append(lambda e: ag_read(e))

            xr4_user = [0, 0]
            ob_user = [0, 0]
            ffn_u = [0, 0]
            st_v = 0
            for tt in range(8):
                db = tt % 2
                p.wait("sync", "sSt", xr4_user[db])
                xl = p.dma("sync", xr4[db][:], xr_in[128 * tt:128 * tt + 128, :],
                           ["sX0", "sX1"][db])
                for os in range(4):
                    ob = os % 2
                    p.wait("sync", "sV", ffn_u[ob] if True else 0)
                    fl_ = p.dma("sync", ffn_t[ob][:],
                                ffn_d[128 * tt:128 * tt + 128, 512 * os:512 * os + 512],
                                ["sW10", "sW11"][ob])
                    p.wait("tensor", "sWo", wo_l)
                    p.wait("tensor", "sAG", agv)
                    p.wait("tensor", "sV", ob_user[ob])
                    t_wo = None
                    for cc in range(16):
                        t_wo = p.emi(
                            "tensor", lambda e, cc=cc, tt=tt, os=os, ob=ob:
                            e.matmul(pWo[ob][:],
                                     afm[:, R_ * cc + 128 * tt:R_ * cc + 128 * tt + 128],
                                     wo_sb[:, D_ * cc + 512 * os:D_ * cc + 512 * os + 512],
                                     start=(cc == 0), stop=(cc == 15)),
                            "sT" if cc == 15 else None)
                    p.wait("vector", "sT", t_wo)
                    p.wait("vector", ["sX0", "sX1"][db], xl)
                    p.wait("vector", ["sW10", "sW11"][ob], fl_)
                    def fin(e, tt=tt, os=os, ob=ob, db=db):
                        e.tensor_tensor(osum[:], pWo[ob][:], ffn_t[ob][:], OP.add)
                        return e.tensor_tensor(obuf[ob][:], osum[:],
                                               xr4[db][:, 512 * os:512 * os + 512], OP.add)
                    v_f = p.emi("vector", fin, "sV")
                    ob_user[ob] = v_f
                    ffn_u[ob] = v_f
                    p.wait("sync", "sV", v_f)
                    st_v = p.dma("sync", out_t[128 * tt:128 * tt + 128,
                                               512 * os:512 * os + 512],
                                 obuf[ob][:], "sSt")
                    p.wait("vector", "sSt", st_v)
                xr4_user[db] = st_v
            ph4.close()
            ph3p.close()
        st_total = p.val("sSt")
        if st_total:
            p.wait("gpsimd", "sSt", st_total)
        if cc_v:
            p.wait("gpsimd", "sCC", cc_v)

        with nc.Block() as block:
            @block.sync
            def _(e):
                p.run(e, "sync")

            @block.tensor
            def _(e):
                p.run(e, "tensor")

            @block.vector
            def _(e):
                p.run(e, "vector")

            @block.scalar
            def _(e):
                p.run(e, "scalar")

            @block.gpsimd
            def _(e):
                p.run(e, "gpsimd")

    return nc


def _shard_inputs(inputs):
    x = np.asarray(inputs["L0_attn_x"], dtype=np.float32)
    Wq = np.asarray(inputs["L0_attn_Wq"], dtype=np.float32)
    Wk = np.asarray(inputs["L0_attn_Wk"], dtype=np.float32)
    Wv = np.asarray(inputs["L0_attn_Wv"], dtype=np.float32)
    Wo = np.asarray(inputs["L0_attn_Wo"], dtype=np.float32)
    w1 = np.asarray(inputs["L0_ffn_w1"], dtype=np.float32)
    w2 = np.asarray(inputs["L0_ffn_w2"], dtype=np.float32)
    import ml_dtypes
    ident = np.eye(128, dtype=ml_dtypes.bfloat16)
    masks = np.zeros((4, 128, 512), dtype=np.float32)
    tri = np.tril(np.full((128, 128), NEG, dtype=np.float32), -1)
    for pvar in range(4):
        for j in range(4):
            if j < pvar:
                masks[pvar][:, 128 * j:128 * j + 128] = NEG
            elif j == pvar:
                masks[pvar][:, 128 * j:128 * j + 128] = tri
    masks = masks.reshape(4 * 128, 512)
    w1t = np.ascontiguousarray(w1.T)
    w2t = np.ascontiguousarray(w2.T)
    in_maps = []
    for c in range(8):
        s, g = c // 4, c % 4
        m = {
            "x": np.ascontiguousarray(x[s]),
            "xr": np.ascontiguousarray(x[s][1024 * g:1024 * g + 1024]),
            "wq": np.ascontiguousarray(Wq[:, 512 * g:512 * g + 512]),
            "wk": np.ascontiguousarray(Wk[:, 512 * g:512 * g + 512]),
            "wv": np.ascontiguousarray(Wv[:, 512 * g:512 * g + 512]),
            "wo": Wo,
            "w1t": w1t,
            "w2t": w2t,
            "b1": np.asarray(inputs["L0_ffn_b1"], dtype=np.float32).reshape(1, F_),
            "gam": np.asarray(inputs["L0_ffn_gamma"], dtype=np.float32).reshape(1, F_),
            "bet": np.asarray(inputs["L0_ffn_beta"], dtype=np.float32).reshape(1, F_),
            "b2": np.asarray(inputs["L0_ffn_b2"], dtype=np.float32).reshape(1, D_),
            "lq1": np.asarray(inputs["L0_attn_lq1"], dtype=np.float32).reshape(1, 64),
            "lk1": np.asarray(inputs["L0_attn_lk1"], dtype=np.float32).reshape(1, 64),
            "lq2": np.asarray(inputs["L0_attn_lq2"], dtype=np.float32).reshape(1, 64),
            "lk2": np.asarray(inputs["L0_attn_lk2"], dtype=np.float32).reshape(1, 64),
            "ident": ident,
            "identf": np.eye(128, dtype=np.float32),
            "masks": masks,
        }
        in_maps.append(m)
    return in_maps


def kernel(**inputs):
    in_maps = _shard_inputs(inputs)
    nc = build_nc(dbg=DBG, phases=PHASES)
    res = run_bass_kernel_spmd(nc, in_maps, core_ids=list(range(8)))
    out = np.empty((2, S_, D_), dtype=np.float32)
    for c in range(8):
        s, g = c // 4, c % 4
        out[s, R_ * g:R_ * g + R_] = res.results[c]["out"]
    return out
